# version 01
# baseline (speedup 1.0000x reference)
import math
from contextlib import ExitStack

import numpy as np
import ml_dtypes
import concourse.bass as bass
import concourse.mybir as mybir
from concourse.bass_utils import run_bass_kernel_spmd

F32 = mybir.dt.float32
BF16 = mybir.dt.bfloat16
U8 = mybir.dt.uint8
AF = mybir.ActivationFunctionType
ALU = mybir.AluOpType
AX = mybir.AxisListType
BF = ml_dtypes.bfloat16

B = 2
T = 8192
D = 2048
NH = 16
HD = 128
NCORE = 8
TPC = T * B // NCORE
NT = TPC // 128
HEAVY = 819
ISCALE = 1.0 / math.sqrt(128.0)
NEG = -1.0e38

L1_RD = 64
L1_SW = 512
L1_M = 32
L1_TOPK = 32
L1_G = 4
L1_DG = 512
L1_NIH = 4
L1_CI = 64
NB = T // L1_M


def _rope_l0():
    inv = 1.0 / (10000.0 ** (np.arange(0, HD, 2, dtype=np.float64) / HD))
    pos = np.arange(T, dtype=np.float64)
    fr = pos[:, None] * inv[None, :]
    return np.cos(fr).astype(np.float32), np.sin(fr).astype(np.float32)


def _rope_l1():
    inv = 1.0 / (10000.0 ** (np.arange(0, L1_RD, 2, dtype=np.float64) / L1_RD))
    pos = np.arange(T, dtype=np.float64)
    fr = pos[:, None] * inv[None, :]
    return np.cos(fr).astype(np.float32), np.sin(fr).astype(np.float32)


_BAND_DELTAS = [768, 896, 1024, 1152, 1280]


def _band_tiles():
    out = np.zeros((len(_BAND_DELTAS), 128, 512), dtype=np.float32)
    pi = np.arange(128)[:, None]
    j = np.arange(512)[None, :]
    for k, d in enumerate(_BAND_DELTAS):
        out[k] = ((d + pi - j) <= HEAVY).astype(np.float32)
    return out.astype(BF)


def _l1_valid_maskadd(pos0):
    t = (pos0 + np.arange(TPC))[:, None]
    n = np.arange(NB)[None, :]
    valid = (n * L1_M + (L1_M - 1)) < t
    return np.where(valid, 0.0, NEG).astype(np.float32)


def _l1_guard(pos0):
    t = pos0 + np.arange(TPC)
    return (t < L1_M + 0.5).astype(np.float32)[:, None]


def _l1_haspick(pos0):
    t = pos0 + np.arange(TPC)
    return (t >= L1_M).astype(np.float32)[:, None]


ENGINES = ("sync", "scalar", "vector", "tensor", "gpsimd")


class Sched:

    def __init__(self):
        self.steps = {e: [] for e in ENGINES}
        self.total = 0

    def add(self, engine, fn, dma=False):
        inc = 16 if dma else 1
        self.steps[engine].append((self.total, fn, inc))
        self.total += inc

    def emit(self, nc, sem):
        with nc.Block() as block:
            for ename in ENGINES:
                steps = self.steps[ename]
                if not steps and ename != "gpsimd":
                    continue

                def body(eng, steps=steps, ename=ename):
                    last_wait = -1
                    for (w, fn, inc) in steps:
                        if w > last_wait:
                            eng.wait_ge(sem, w)
                        fn(eng).then_inc(sem, inc)
                        last_wait = w
                    if ename == "gpsimd":
                        eng.wait_ge(sem, self.total)

                getattr(block, ename)(body)


def _mk_consts(nc, sc, ident_bf, ident_f32=None, causal01=None, upper01=None):
    sc.add("gpsimd", lambda g: nc.gpsimd.memset(ident_bf[:], 0.0))
    sc.add("gpsimd", lambda g: nc.gpsimd.affine_select(
        out=ident_bf[:], in_=ident_bf[:], compare_op=ALU.not_equal,
        fill=1.0, base=0, pattern=[[-1, 128]], channel_multiplier=1))
    if ident_f32 is not None:
        sc.add("gpsimd", lambda g: nc.gpsimd.memset(ident_f32[:], 0.0))
        sc.add("gpsimd", lambda g: nc.gpsimd.affine_select(
            out=ident_f32[:], in_=ident_f32[:], compare_op=ALU.not_equal,
            fill=1.0, base=0, pattern=[[-1, 128]], channel_multiplier=1))
    if causal01 is not None:
        sc.add("gpsimd", lambda g: nc.gpsimd.memset(causal01[:], 1.0))
        sc.add("gpsimd", lambda g: nc.gpsimd.affine_select(
            out=causal01[:], in_=causal01[:], compare_op=ALU.is_ge,
            fill=0.0, base=0, pattern=[[-1, 128]], channel_multiplier=1))
    if upper01 is not None:
        sc.add("gpsimd", lambda g: nc.gpsimd.memset(upper01[:], 1.0))
        sc.add("gpsimd", lambda g: nc.gpsimd.affine_select(
            out=upper01[:], in_=upper01[:], compare_op=ALU.is_lt,
            fill=0.0, base=0, pattern=[[-1, 128]], channel_multiplier=1))


_IDENT = [None]


def build_l0pre():
    nc = bass.Bass("TRN2", target_bir_lowering=False)
    x = nc.dram_tensor("x", [TPC, D], F32, kind="ExternalInput").ap()
    wq = nc.dram_tensor("wq", [D, D], F32, kind="ExternalInput").ap()
    wk = nc.dram_tensor("wk", [D, D], F32, kind="ExternalInput").ap()
    wv = nc.dram_tensor("wv", [D, D], F32, kind="ExternalInput").ap()
    cos = nc.dram_tensor("cos", [TPC, 64], F32, kind="ExternalInput").ap()
    sin = nc.dram_tensor("sin", [TPC, 64], F32, kind="ExternalInput").ap()
    o_xn = nc.dram_tensor("xn", [TPC, D], BF16, kind="ExternalOutput").ap()
    o_q = nc.dram_tensor("q", [TPC, D], BF16, kind="ExternalOutput").ap()
    o_k = nc.dram_tensor("k", [TPC, D], BF16, kind="ExternalOutput").ap()
    o_v = nc.dram_tensor("v", [TPC, D], BF16, kind="ExternalOutput").ap()
    o_qn2 = nc.dram_tensor("qn2", [TPC, NH], F32, kind="ExternalOutput").ap()
    o_kn2 = nc.dram_tensor("kn2", [TPC, NH], F32, kind="ExternalOutput").ap()

    sc = Sched()
    with ExitStack() as es:
        sb = lambda n, s, d: es.enter_context(nc.sbuf_tensor(n, s, d))
        ps = lambda n, s, d=F32: es.enter_context(nc.psum_tensor(n, s, d))
        identb = sb("identb", [128, 128], BF16)
        sx = sb("sx", [128, D], F32)
        strash = sb("strash", [128, D], F32)
        st1 = sb("st1", [128, 1], F32)
        st2 = sb("st2", [128, 1], F32)
        st3 = sb("st3", [128, 1], F32)
        st4 = sb("st4", [128, 1], F32)
        xnb = sb("xnb", [128, D], BF16)
        xnT_ = sb("xnT", [128, 16 * D], BF16)
        wt_ = sb("wt", [128, 16 * D], BF16)
        wload = sb("wload", [128, D], BF16)
        qt = sb("qt", [128, D], F32)
        qrot = sb("qrot", [128, D], F32)
        tmp64 = sb("tmp64", [128, NH * 64], F32)
        scos = sb("scos", [128, 64], F32)
        ssin = sb("ssin", [128, 64], F32)
        nbuf = sb("nbuf", [128, NH], F32)
        ntrash = sb("ntrash", [128, 128], F32)
        outb = sb("outb", [128, D], BF16)
        epsb = sb("epsb", [128, 1], F32)
        pT = ps("pT", [128, 512], BF16)
        pmm = ps("pmm", [128, 512], F32)
        sem = es.enter_context(nc.semaphore("sem"))
        _IDENT[0] = identb
        sc.add("gpsimd", lambda e: nc.gpsimd.memset(epsb[:], 1e-5))
        xnT = xnT_[:].rearrange("p (c t) -> p c t", c=16)
        wt = wt_[:].rearrange("p (c t) -> p c t", c=16)
        _mk_consts(nc, sc, identb)

        for i in range(NT):
            sc.add("sync", lambda e, i=i: e.dma_start(out=sx[:], in_=x[i * 128:(i + 1) * 128, :]), dma=True)
            sc.add("vector", lambda e: nc.vector.reduce_sum(st1[:], sx[:], axis=AX.X))
            sc.add("scalar", lambda e: nc.scalar.activation(strash[:], sx[:], AF.Square, accum_out=st2[:]))
            sc.add("vector", lambda e: nc.vector.tensor_scalar_mul(st1[:], st1[:], 1.0 / D))
            sc.add("vector", lambda e: nc.vector.tensor_scalar_mul(st2[:], st2[:], 1.0 / D))
            sc.add("vector", lambda e: nc.vector.tensor_mul(st3[:], st1[:], st1[:]))
            sc.add("vector", lambda e: nc.vector.tensor_sub(st2[:], st2[:], st3[:]))
            sc.add("scalar", lambda e: nc.scalar.activation(st3[:], st2[:], AF.Sqrt, bias=epsb[:]))
            sc.add("vector", lambda e: nc.vector.reciprocal(st4[:], st3[:]))
            sc.add("vector", lambda e: nc.vector.scalar_tensor_tensor(
                st3[:], st1[:], -1.0, st4[:], op0=ALU.mult, op1=ALU.mult))
            sc.add("vector", lambda e: nc.vector.scalar_tensor_tensor(
                xnb[:], sx[:], st4[:], st3[:].broadcast_to([128, D]),
                op0=ALU.mult, op1=ALU.add))
            sc.add("gpsimd", lambda e, i=i: e.dma_start(out=o_xn[i * 128:(i + 1) * 128, :], in_=xnb[:]), dma=True)
            for c0 in range(0, 16, 4):
                for j in range(4):
                    c = c0 + j
                    sc.add("tensor", lambda e, c=c, j=j: nc.tensor.transpose(
                        pT[:, j * 128:(j + 1) * 128], xnb[:, c * 128:(c + 1) * 128], identb[:]))
                for j in range(4):
                    c = c0 + j
                    sc.add("scalar", lambda e, c=c, j=j, i=i: nc.scalar.copy(
                        xnT[:, c, i * 128:(i + 1) * 128], pT[:, j * 128:(j + 1) * 128]))

        def w_phase(w_ap, out_ap, is_q, is_k, n2_out):
            for oc in range(16):
                sc.add("gpsimd", lambda e, oc=oc: e.dma_start(
                    out=wload[:], in_=w_ap[oc * 128:(oc + 1) * 128, :]), dma=True)
                for c0 in range(0, 16, 4):
                    for j in range(4):
                        hc = c0 + j
                        sc.add("tensor", lambda e, hc=hc, j=j: nc.tensor.transpose(
                            pT[:, j * 128:(j + 1) * 128], wload[:, hc * 128:(hc + 1) * 128], identb[:]))
                    for j in range(4):
                        hc = c0 + j
                        sc.add("scalar", lambda e, hc=hc, j=j, oc=oc: nc.scalar.copy(
                            wt[:, hc, oc * 128:(oc + 1) * 128], pT[:, j * 128:(j + 1) * 128]))
            for i in range(NT):
                if is_q or is_k:
                    sc.add("sync", lambda e, i=i: e.dma_start(out=scos[:], in_=cos[i * 128:(i + 1) * 128, :]), dma=True)
                    sc.add("sync", lambda e, i=i: e.dma_start(out=ssin[:], in_=sin[i * 128:(i + 1) * 128, :]), dma=True)
                for b4 in range(4):
                    for hc in range(16):
                        sc.add("tensor", lambda e, i=i, b4=b4, hc=hc: nc.tensor.matmul(
                            pmm[:], xnT[:, hc, i * 128:(i + 1) * 128],
                            wt[:, hc, b4 * 512:(b4 + 1) * 512],
                            start=(hc == 0), stop=(hc == 15)))
                    sc.add("scalar", lambda e, b4=b4: nc.scalar.copy(qt[:, b4 * 512:(b4 + 1) * 512], pmm[:]))
                if is_q or is_k:
                    qv = qt[:].rearrange("p (h d) -> p h d", h=NH)
                    rv = qrot[:].rearrange("p (h d) -> p h d", h=NH)
                    tv = tmp64[:].rearrange("p (h d) -> p h d", h=NH)
                    cosb = scos[:].unsqueeze(1).broadcast_to([128, NH, 64])
                    sinb = ssin[:].unsqueeze(1).broadcast_to([128, NH, 64])
                    sc.add("vector", lambda e: nc.vector.tensor_mul(rv[:, :, 0:64], qv[:, :, 0:64], cosb))
                    sc.add("vector", lambda e: nc.vector.tensor_mul(tv, qv[:, :, 64:128], sinb))
                    sc.add("vector", lambda e: nc.vector.tensor_sub(rv[:, :, 0:64], rv[:, :, 0:64], tv))
                    sc.add("vector", lambda e: nc.vector.tensor_mul(rv[:, :, 64:128], qv[:, :, 64:128], cosb))
                    sc.add("vector", lambda e: nc.vector.tensor_mul(tv, qv[:, :, 0:64], sinb))
                    sc.add("vector", lambda e: nc.vector.tensor_add(rv[:, :, 64:128], rv[:, :, 64:128], tv))
                    for h in range(NH):
                        sc.add("scalar", lambda e, h=h: nc.scalar.activation(
                            ntrash[:], qrot[:, h * 128:(h + 1) * 128], AF.Square,
                            accum_out=nbuf[:, h:h + 1]))
                    sc.add("vector", lambda e: nc.vector.tensor_copy(outb[:], qrot[:]))
                    sc.add("gpsimd", lambda e, i=i: e.dma_start(
                        out=n2_out[i * 128:(i + 1) * 128, :], in_=nbuf[:]), dma=True)
                else:
                    sc.add("vector", lambda e: nc.vector.tensor_copy(outb[:], qt[:]))
                sc.add("gpsimd", lambda e, i=i: e.dma_start(
                    out=out_ap[i * 128:(i + 1) * 128, :], in_=outb[:]), dma=True)

        w_phase(wq, o_q, True, False, o_qn2)
        w_phase(wk, o_k, False, True, o_kn2)
        w_phase(wv, o_v, False, False, None)

        sc.emit(nc, sem)
    return nc


def build_l0passA():
    nc = bass.Bass("TRN2", target_bir_lowering=False)
    qt_in = nc.dram_tensor("qt", [4, 128, T], BF16, kind="ExternalInput").ap()
    kt_in = nc.dram_tensor("kt", [4, 128, T], BF16, kind="ExternalInput").ap()
    qn2_in = nc.dram_tensor("qn2", [4, 128, 64], F32, kind="ExternalInput").ap()
    kn2_in = nc.dram_tensor("kn2", [4, 128, 64], F32, kind="ExternalInput").ap()
    o_cs = nc.dram_tensor("colsum", [4, 16, 512], F32, kind="ExternalOutput").ap()

    sc = Sched()
    with ExitStack() as es:
        sb = lambda n, s, d: es.enter_context(nc.sbuf_tensor(n, s, d))
        ps = lambda n, s, d=F32: es.enter_context(nc.psum_tensor(n, s, d))
        identb = sb("identb", [128, 128], BF16)
        identf = sb("identf", [128, 128], F32)
        c01 = sb("c01", [128, 128], BF16)
        ktb = sb("ktb", [128, T], BF16)
        qtb = sb("qtb", [128, T], BF16)
        sqn2 = sb("sqn2", [128, 64], F32)
        skn2 = sb("skn2", [128, 64], F32)
        estrip = sb("estrip", [128, T], BF16)
        ediag = sb("ediag", [128, 128], BF16)
        dslots = sb("dslots", [128, 16], F32)
        st1 = sb("st1", [128, 1], F32)
        st2 = sb("st2", [128, 1], F32)
        st3 = sb("st3", [128, 1], F32)
        negmb = sb("negmb", [128, 1], F32)
        maxkb = sb("maxkb", [128, 1], F32)
        sones = sb("sones", [1, 128], F32)
        ones16 = sb("ones16", [128, 16], BF16)
        sbig = sb("sbig", [128, 272], BF16)
        str1 = sb("str1", [1, 128], F32)
        sm1 = sb("sm1", [1, 1], F32)
        scs = sb("scs", [16, 512], F32)
        pqk0 = ps("pqk0", [128, 512], F32)
        pqk1 = ps("pqk1", [128, 512], F32)
        pcs = ps("pcs", [16, 512], F32)
        ptr = ps("ptr", [1, 128], F32)
        pbc = ps("pbc", [128, 1], F32)
        sem = es.enter_context(nc.semaphore("sem"))
        _IDENT[0] = identb
        _mk_consts(nc, sc, identb, ident_f32=identf, causal01=c01)
        sc.add("gpsimd", lambda e: nc.gpsimd.memset(sones[:], 1.0))
        sc.add("gpsimd", lambda e: nc.gpsimd.memset(ones16[:], 1.0))
        sc.add("gpsimd", lambda e: nc.gpsimd.memset(sbig[:], 0.0))

        for hh in range(4):
            sc.add("sync", lambda e, hh=hh: e.dma_start(out=ktb[:], in_=kt_in[hh]), dma=True)
            sc.add("sync", lambda e, hh=hh: e.dma_start(out=qtb[:], in_=qt_in[hh]), dma=True)
            sc.add("sync", lambda e, hh=hh: e.dma_start(out=sqn2[:], in_=qn2_in[hh]), dma=True)
            sc.add("sync", lambda e, hh=hh: e.dma_start(out=skn2[:], in_=kn2_in[hh]), dma=True)
            sc.add("vector", lambda e: nc.vector.reduce_max(st1[:], skn2[:], axis=AX.X))
            sc.add("tensor", lambda e: nc.tensor.transpose(ptr[:], st1[:], identf[:]))
            sc.add("scalar", lambda e: nc.scalar.copy(str1[:], ptr[:]))
            sc.add("vector", lambda e: nc.vector.reduce_max(sm1[:], str1[:], axis=AX.X))
            sc.add("scalar", lambda e: nc.scalar.activation(sm1[:], sm1[:], AF.Sqrt))
            sc.add("tensor", lambda e: nc.tensor.matmul(pbc[:], sones[:], sm1[:], start=True, stop=True))
            sc.add("scalar", lambda e: nc.scalar.copy(maxkb[:], pbc[:]))

            first_cs = [True]
            for t in range(64):
                L = (t + 1) * 128
                nch = (L + 511) // 512
                sc.add("scalar", lambda e, t=t: nc.scalar.activation(st2[:], sqn2[:, t:t + 1], AF.Sqrt))
                sc.add("vector", lambda e: nc.vector.scalar_tensor_tensor(
                    negmb[:], st2[:], -ISCALE, maxkb[:], op0=ALU.mult, op1=ALU.mult))
                lhs = qtb[:, t * 128:(t + 1) * 128]
                for c in range(nch):
                    a = c * 512
                    w = min(512, L - a)
                    pq = pqk0 if (c % 2 == 0) else pqk1
                    sc.add("tensor", lambda e, a=a, w=w, pq=pq, lhs=lhs: nc.tensor.matmul(
                        pq[:, 0:w], lhs, ktb[:, a:a + w], start=True, stop=True))
                    dlim = L - 128
                    if a + w <= dlim:
                        sc.add("scalar", lambda e, a=a, w=w, pq=pq, c=c: nc.scalar.activation(
                            estrip[:, a:a + w], pq[:, 0:w], AF.Exp,
                            bias=negmb[:], scale=ISCALE, accum_out=dslots[:, c:c + 1]))
                    else:
                        wm = dlim - a
                        if wm > 0:
                            sc.add("scalar", lambda e, a=a, wm=wm, pq=pq, c=c: nc.scalar.activation(
                                estrip[:, a:a + wm], pq[:, 0:wm], AF.Exp,
                                bias=negmb[:], scale=ISCALE, accum_out=dslots[:, c:c + 1]))
                        else:
                            sc.add("vector", lambda e, c=c: nc.vector.memset(dslots[:, c:c + 1], 0.0))
                        sc.add("scalar", lambda e, wm=wm, w=w, pq=pq: nc.scalar.activation(
                            ediag[:], pq[:, wm:w], AF.Exp, bias=negmb[:], scale=ISCALE))
                        sc.add("vector", lambda e, a=a, wm=wm, w=w: nc.vector.tensor_mul(
                            estrip[:, a + wm:a + w], ediag[:], c01[:]))
                        sc.add("vector", lambda e, a=a, wm=wm, w=w, c=c: nc.vector.reduce_sum(
                            st3[:], estrip[:, a + wm:a + w], axis=AX.X))
                sc.add("vector", lambda e, nch=nch: nc.vector.reduce_sum(
                    st1[:], dslots[:, 0:nch], axis=AX.X))
                sc.add("vector", lambda e: nc.vector.tensor_add(st1[:], st1[:], st3[:]))
                sc.add("vector", lambda e: nc.vector.reciprocal(st2[:], st1[:]))
                sbv = sbig[:].rearrange("p (c x) -> p c x", c=16)
                sc.add("vector", lambda e: nc.vector.memset(sbig[:], 0.0))
                sc.add("vector", lambda e, sbv=sbv: nc.vector.tensor_scalar(
                    sbv[:, :, 0:1], ones16[:].unsqueeze(2), st2[:], None, op0=ALU.mult))
                for c in range(nch):
                    a = c * 512
                    w = min(512, L - a)
                    sc.add("tensor", lambda e, c=c, a=a, w=w, f=first_cs[0],
                           s=(t == 63 and c == nch - 1): nc.tensor.matmul(
                        pcs[:, 0:w], sbig[:, 16 * c:16 * c + 16], estrip[:, a:a + w],
                        start=f, stop=s, skip_group_check=True))
                    first_cs[0] = False
            sc.add("scalar", lambda e: nc.scalar.copy(scs[:], pcs[:]))
            sc.add("gpsimd", lambda e, hh=hh: e.dma_start(out=o_cs[hh], in_=scs[:]), dma=True)
        sc.emit(nc, sem)
    return nc


def build_l0passB():
    nc = bass.Bass("TRN2", target_bir_lowering=False)
    qt_in = nc.dram_tensor("qt", [4, 128, T], BF16, kind="ExternalInput").ap()
    kt_in = nc.dram_tensor("kt", [4, 128, T], BF16, kind="ExternalInput").ap()
    v_in = nc.dram_tensor("v", [4, 64, 128, 128], BF16, kind="ExternalInput").ap()
    qn2_in = nc.dram_tensor("qn2", [4, 128, 64], F32, kind="ExternalInput").ap()
    kn2_in = nc.dram_tensor("kn2", [4, 128, 64], F32, kind="ExternalInput").ap()
    cs_in = nc.dram_tensor("csbis", [128, 256], F32, kind="ExternalInput").ap()
    band_in = nc.dram_tensor("bandbit", [len(_BAND_DELTAS), 128, 512], BF16, kind="ExternalInput").ap()
    o_at = nc.dram_tensor("attn", [4, 64, 128, 128], BF16, kind="ExternalOutput").ap()
    hb_dram = nc.dram_tensor("hbscr", [4, 32, 256], BF16, kind="Internal").ap()

    sc = Sched()
    with ExitStack() as es:
        sb = lambda n, s, d: es.enter_context(nc.sbuf_tensor(n, s, d))
        ps = lambda n, s, d=F32: es.enter_context(nc.psum_tensor(n, s, d))
        identb = sb("identb", [128, 128], BF16)
        identf = sb("identf", [128, 128], F32)
        c01 = sb("c01", [128, 128], BF16)
        ktb = sb("ktb", [128, T], BF16)
        qtb = sb("qtb", [128, T], BF16)
        vext_ = sb("vext", [128, 64 * 132], BF16)
        sqn2 = sb("sqn2", [128, 64], F32)
        skn2 = sb("skn2", [128, 64], F32)
        scsb = sb("scsb", [128, 256], F32)
        sband = sb("sband", [128, len(_BAND_DELTAS) * 512], BF16)
        heavyb = sb("heavyb", [128, T], BF16)
        estrip = sb("estrip", [128, T], BF16)
        ediag = sb("ediag", [128, 128], BF16)
        cmb = sb("cmb", [128, 512], BF16)
        ptsb = sb("ptsb", [128, 512], BF16)
        st1 = sb("st1", [128, 1], F32)
        st2 = sb("st2", [128, 1], F32)
        negmb = sb("negmb", [128, 1], F32)
        maxkb = sb("maxkb", [128, 1], F32)
        sones = sb("sones", [1, 128], F32)
        str1 = sb("str1", [1, 128], F32)
        sm1 = sb("sm1", [1, 1], F32)
        slo = sb("slo", [128, 1], F32)
        shi = sb("shi", [128, 1], F32)
        smid = sb("smid", [128, 1], F32)
        cpart = sb("cpart", [128, 1], F32)
        scnt = sb("scnt", [4, 1], F32)
        srep = sb("srep", [128, 1], F32)
        condm = sb("condm", [128, 1], U8)
        getrash = sb("getrash", [128, 256], BF16)
        hbbis = sb("hbbis", [128, 256], BF16)
        selh = sb("selh", [128, 4], F32)
        selhT = sb("selhT", [4, 128], F32)
        attnt = sb("attnt", [128, 128], BF16)
        pqk0 = ps("pqk0", [128, 512])
        pqk1 = ps("pqk1", [128, 512])
        ptp = es.enter_context(nc.psum_tensor("ptp", [128, 512], BF16))
        ppv = ps("ppv", [128, 129])
        ptr = ps("ptr", [1, 128])
        pbc = ps("pbc", [128, 1])
        pcnt = ps("pcnt", [4, 1])
        sem = es.enter_context(nc.semaphore("sem"))
        _IDENT[0] = identb
        vext = vext_[:].rearrange("p (c x) -> p c x", c=64)
        _mk_consts(nc, sc, identb, ident_f32=identf, causal01=c01)
        sc.add("gpsimd", lambda e: nc.gpsimd.memset(sones[:], 1.0))
        sc.add("gpsimd", lambda e: nc.gpsimd.memset(selh[:], 1.0))
        sc.add("gpsimd", lambda e: nc.gpsimd.affine_select(
            out=selh[:], in_=selh[:], compare_op=ALU.is_ge,
            fill=0.0, base=0, pattern=[[-32, 4]], channel_multiplier=1))
        sc.add("gpsimd", lambda e: nc.gpsimd.affine_select(
            out=selh[:], in_=selh[:], compare_op=ALU.is_ge,
            fill=0.0, base=31, pattern=[[32, 4]], channel_multiplier=-1))
        sc.add("gpsimd", lambda e: nc.gpsimd.memset(selhT[:], 1.0))
        sc.add("gpsimd", lambda e: nc.gpsimd.affine_select(
            out=selhT[:], in_=selhT[:], compare_op=ALU.is_ge,
            fill=0.0, base=0, pattern=[[1, 128]], channel_multiplier=-32))
        sc.add("gpsimd", lambda e: nc.gpsimd.affine_select(
            out=selhT[:], in_=selhT[:], compare_op=ALU.is_ge,
            fill=0.0, base=31, pattern=[[-1, 128]], channel_multiplier=32))
        sc.add("sync", lambda e: e.dma_start(out=scsb[:], in_=cs_in), dma=True)
        for bi in range(len(_BAND_DELTAS)):
            sc.add("sync", lambda e, bi=bi: e.dma_start(
                out=sband[:, bi * 512:(bi + 1) * 512], in_=band_in[bi]), dma=True)
        sc.add("vector", lambda e: nc.vector.memset(slo[:], 0.0))
        sc.add("vector", lambda e: nc.vector.memset(shi[:], 8200.0))
        for _ in range(45):
            sc.add("vector", lambda e: nc.vector.tensor_add(smid[:], slo[:], shi[:]))
            sc.add("vector", lambda e: nc.vector.tensor_scalar_mul(smid[:], smid[:], 0.5))
            sc.add("vector", lambda e: nc.vector.tensor_scalar(
                getrash[:], scsb[:], smid[:], 0.0, op0=ALU.is_ge, op1=ALU.add,
                accum_out=cpart[:]))
            sc.add("tensor", lambda e: nc.tensor.matmul(pcnt[:], selh[:], cpart[:], start=True, stop=True))
            sc.add("scalar", lambda e: nc.scalar.copy(scnt[:], pcnt[:]))
            sc.add("tensor", lambda e: nc.tensor.matmul(pbc[:], selhT[:], scnt[:], start=True, stop=True))
            sc.add("scalar", lambda e: nc.scalar.copy(srep[:], pbc[:]))
            sc.add("vector", lambda e: nc.vector.tensor_scalar(
                condm[:], srep[:], 818.5, None, op0=ALU.is_ge))
            sc.add("vector", lambda e: nc.vector.select(slo[:], condm[:], smid[:], slo[:]))
            sc.add("vector", lambda e: nc.vector.select(shi[:], condm[:], shi[:], smid[:]))
        sc.add("vector", lambda e: nc.vector.tensor_scalar(
            hbbis[:], scsb[:], slo[:], None, op0=ALU.is_ge))
        sc.add("gpsimd", lambda e: e.dma_start(
            out=hb_dram.rearrange("h s j -> (h s) j"), in_=hbbis[:]), dma=True)

        for hh in range(4):
            sc.add("sync", lambda e, hh=hh: e.dma_start(out=ktb[:], in_=kt_in[hh]), dma=True)
            sc.add("sync", lambda e, hh=hh: e.dma_start(out=qtb[:], in_=qt_in[hh]), dma=True)
            sc.add("sync", lambda e, hh=hh: e.dma_start(out=sqn2[:], in_=qn2_in[hh]), dma=True)
            sc.add("sync", lambda e, hh=hh: e.dma_start(out=skn2[:], in_=kn2_in[hh]), dma=True)
            sc.add("sync", lambda e, hh=hh: e.dma_start(
                out=vext[:, :, 0:128], in_=v_in[hh].rearrange("c p d -> p c d")), dma=True)
            sc.add("vector", lambda e: nc.vector.memset(vext[:, :, 128:129], 1.0))
            sc.add("sync", lambda e, hh=hh: e.dma_start(
                out=heavyb[:],
                in_=hb_dram[hh].rearrange("s j -> (s j)").unsqueeze(0).broadcast_to([128, T])), dma=True)
            sc.add("vector", lambda e: nc.vector.reduce_max(st1[:], skn2[:], axis=AX.X))
            sc.add("tensor", lambda e: nc.tensor.transpose(ptr[:], st1[:], identf[:]))
            sc.add("scalar", lambda e: nc.scalar.copy(str1[:], ptr[:]))
            sc.add("vector", lambda e: nc.vector.reduce_max(sm1[:], str1[:], axis=AX.X))
            sc.add("scalar", lambda e: nc.scalar.activation(sm1[:], sm1[:], AF.Sqrt))
            sc.add("tensor", lambda e: nc.tensor.matmul(pbc[:], sones[:], sm1[:], start=True, stop=True))
            sc.add("scalar", lambda e: nc.scalar.copy(maxkb[:], pbc[:]))

            for t in range(64):
                L = (t + 1) * 128
                nch = (L + 511) // 512
                nkc = L // 128
                sc.add("scalar", lambda e, t=t: nc.scalar.activation(st2[:], sqn2[:, t:t + 1], AF.Sqrt))
                sc.add("vector", lambda e: nc.vector.scalar_tensor_tensor(
                    negmb[:], st2[:], -ISCALE, maxkb[:], op0=ALU.mult, op1=ALU.mult))
                lhs = qtb[:, t * 128:(t + 1) * 128]
                for c in range(nch):
                    a = c * 512
                    w = min(512, L - a)
                    pq = pqk0 if (c % 2 == 0) else pqk1
                    sc.add("tensor", lambda e, a=a, w=w, pq=pq, lhs=lhs: nc.tensor.matmul(
                        pq[:, 0:w], lhs, ktb[:, a:a + w], start=True, stop=True))
                    dlim = L - 128
                    if a + w <= dlim:
                        sc.add("scalar", lambda e, a=a, w=w, pq=pq: nc.scalar.activation(
                            estrip[:, a:a + w], pq[:, 0:w], AF.Exp,
                            bias=negmb[:], scale=ISCALE))
                    else:
                        wm = dlim - a
                        if wm > 0:
                            sc.add("scalar", lambda e, a=a, wm=wm, pq=pq: nc.scalar.activation(
                                estrip[:, a:a + wm], pq[:, 0:wm], AF.Exp,
                                bias=negmb[:], scale=ISCALE))
                        sc.add("scalar", lambda e, wm=wm, w=w, pq=pq: nc.scalar.activation(
                            ediag[:], pq[:, wm:w], AF.Exp, bias=negmb[:], scale=ISCALE))
                        sc.add("vector", lambda e, a=a, wm=wm, w=w: nc.vector.tensor_mul(
                            estrip[:, a + wm:a + w], ediag[:], c01[:]))
                    dt_ = 128 * t - a
                    if dt_ >= 1408:
                        sc.add("vector", lambda e, a=a, w=w: nc.vector.tensor_mul(
                            estrip[:, a:a + w], estrip[:, a:a + w], heavyb[:, a:a + w]))
                    elif dt_ in _BAND_DELTAS:
                        bi = _BAND_DELTAS.index(dt_)
                        sc.add("vector", lambda e, a=a, w=w, bi=bi: nc.vector.tensor_max(
                            cmb[:, 0:w], heavyb[:, a:a + w], sband[:, bi * 512:bi * 512 + w]))
                        sc.add("vector", lambda e, a=a, w=w: nc.vector.tensor_mul(
                            estrip[:, a:a + w], estrip[:, a:a + w], cmb[:, 0:w]))
                for k0 in range(0, nkc, 4):
                    n4 = min(4, nkc - k0)
                    for j in range(n4):
                        kc = k0 + j
                        sc.add("tensor", lambda e, kc=kc, j=j: nc.tensor.transpose(
                            ptp[:, j * 128:(j + 1) * 128], estrip[:, kc * 128:(kc + 1) * 128], identb[:]))
                    sc.add("scalar", lambda e, n4=n4: nc.scalar.copy(
                        ptsb[:, 0:n4 * 128], ptp[:, 0:n4 * 128]))
                    for j in range(n4):
                        kc = k0 + j
                        sc.add("tensor", lambda e, kc=kc, j=j, s=(kc == nkc - 1), f=(kc == 0):
                               nc.tensor.matmul(
                            ppv[:], ptsb[:, j * 128:(j + 1) * 128], vext[:, kc, 0:129],
                            start=f, stop=s, skip_group_check=True))
                sc.add("vector", lambda e: nc.vector.tensor_copy(st1[:], ppv[:, 128:129]))
                sc.add("vector", lambda e: nc.vector.reciprocal(st2[:], st1[:]))
                sc.add("vector", lambda e: nc.vector.tensor_scalar(
                    attnt[:], ppv[:, 0:128], st2[:], None, op0=ALU.mult))
                sc.add("gpsimd", lambda e, hh=hh, t=t: e.dma_start(
                    out=o_at[hh, t], in_=attnt[:]), dma=True)
        sc.emit(nc, sem)
    return nc


def run_l0passB(pre, cs):
    nc = build_l0passB()
    q = np.asarray(pre["q"]).reshape(2, T, NH, HD)
    k = np.asarray(pre["k"]).reshape(2, T, NH, HD)
    v = np.asarray(pre["v"]).reshape(2, T, NH, HD)
    qn2 = np.asarray(pre["qn2"])
    kn2 = np.asarray(pre["kn2"])
    band = np.ascontiguousarray(_band_tiles())
    in_maps = []
    for c in range(NCORE):
        qt = np.empty((4, 128, T), dtype=BF)
        kt = np.empty((4, 128, T), dtype=BF)
        vc = np.empty((4, 64, 128, 128), dtype=BF)
        q2 = np.empty((4, 128, 64), dtype=np.float32)
        k2 = np.empty((4, 128, 64), dtype=np.float32)
        csb = np.empty((128, 256), dtype=np.float32)
        for j in range(4):
            bh = 4 * c + j
            b, h = bh // NH, bh % NH
            qt[j] = q[b, :, h, :].T
            kt[j] = k[b, :, h, :].T
            vc[j] = v[b, :, h, :].reshape(64, 128, 128)
            q2[j] = qn2[b, :, h].reshape(64, 128).T
            k2[j] = kn2[b, :, h].reshape(64, 128).T
            csb[j * 32:(j + 1) * 32] = cs[b, h].reshape(32, 256)
        in_maps.append({"qt": qt, "kt": kt, "v": vc, "qn2": q2, "kn2": k2,
                        "csbis": csb, "bandbit": band})
    res = run_bass_kernel_spmd(nc, in_maps, core_ids=list(range(NCORE)))
    attn = np.zeros((2, T, NH, HD), dtype=BF)
    for c in range(NCORE):
        for j in range(4):
            bh = 4 * c + j
            b, h = bh // NH, bh % NH
            attn[b, :, h, :] = res.results[c]["attn"][j].reshape(T, HD)
    return attn.reshape(2, T, D)


def build_l0post():
    nc = bass.Bass("TRN2", target_bir_lowering=False)
    x_in = nc.dram_tensor("x", [TPC, D], F32, kind="ExternalInput").ap()
    at_in = nc.dram_tensor("attn", [TPC, D], BF16, kind="ExternalInput").ap()
    xn_in = nc.dram_tensor("xn", [TPC, D], BF16, kind="ExternalInput").ap()
    wo_in = nc.dram_tensor("wo", [D, D], F32, kind="ExternalInput").ap()
    wfc_in = nc.dram_tensor("wfc", [4 * D, D], F32, kind="ExternalInput").ap()
    wpr_in = nc.dram_tensor("wproj", [D, 4 * D], F32, kind="ExternalInput").ap()
    o_x1 = nc.dram_tensor("x1", [TPC, D], F32, kind="ExternalOutput").ap()
    h_scr = nc.dram_tensor("hscr", [TPC, 4 * D], BF16, kind="Internal").ap()
    t1t_scr = nc.dram_tensor("t1t", [16, 64, 128, 128], BF16, kind="Internal").ap()
    t2t_scr = nc.dram_tensor("t2t", [64, 16, 128, 128], BF16, kind="Internal").ap()
    x2t_scr = nc.dram_tensor("x2t", [64, 16, 128, 128], BF16, kind="Internal").ap()

    sc = Sched()
    with ExitStack() as es:
        sb = lambda n, s, d: es.enter_context(nc.sbuf_tensor(n, s, d))
        ps = lambda n, s, d=F32: es.enter_context(nc.psum_tensor(n, s, d))
        identb = sb("identb", [128, 128], BF16)
        big0_ = sb("big0", [128, 16 * D], BF16)
        wcL = sb("wcL", [128, 4 * D], F32)
        hb2 = sb("hb2", [128, 4 * D], BF16)
        x2tb_ = sb("x2tb", [128, 64 * 128], BF16)
        rhsb_ = sb("rhsb", [128, 16 * 512], BF16)
        sx = sb("sx", [128, D], F32)
        ab = sb("ab", [128, D], BF16)
        abT_ = sb("abT", [128, 16 * 128], BF16)
        xnb = sb("xnb", [128, D], BF16)
        xh = sb("xh", [128, D], F32)
        tb = sb("tb", [128, 512], BF16)
        g1 = sb("g1", [128, 4 * D], BF16)
        st1 = sb("st1", [128, 1], F32)
        st2 = sb("st2", [128, 1], F32)
        st3 = sb("st3", [128, 1], F32)
        st4 = sb("st4", [128, 1], F32)
        asum = sb("asum", [128, 1], F32)
        epsb = sb("epsb", [128, 1], F32)
        onescol = sb("onescol", [128, 1], F32)
        sones = sb("sones", [1, 128], F32)
        sg = sb("sg", [1, 1], F32)
        gb1 = sb("gb1", [128, 1], F32)
        gb2 = sb("gb2", [128, 1], F32)
        thrb = sb("thrb", [128, 1], F32)
        negthr = sb("negthr", [128, 1], F32)
        ob = sb("ob", [128, 512], F32)
        pT = es.enter_context(nc.psum_tensor("pT", [128, 512], BF16))
        pmm = ps("pmm", [128, 512])
        pg = ps("pg", [1, 1])
        pbc = ps("pbc", [128, 1])
        sem = es.enter_context(nc.semaphore("sem"))
        _IDENT[0] = identb
        wtbuf = big0_[:].rearrange("p (c t) -> p c t", c=16)
        rhs2 = big0_[:].rearrange("p (c t) -> p c t", c=64)
        x2tb = x2tb_[:].rearrange("p (c t) -> p c t", c=64)
        rhsb = rhsb_[:].rearrange("p (c t) -> p c t", c=16)
        abT = abT_[:].rearrange("p (c t) -> p c t", c=16)
        g3 = hb2
        _mk_consts(nc, sc, identb)
        sc.add("gpsimd", lambda e: nc.gpsimd.memset(epsb[:], 1e-6))
        sc.add("gpsimd", lambda e: nc.gpsimd.memset(onescol[:], 1.0))
        sc.add("gpsimd", lambda e: nc.gpsimd.memset(sones[:], 1.0))

        for oc in range(16):
            sc.add("gpsimd", lambda e, oc=oc: e.dma_start(
                out=ab[:], in_=wo_in[oc * 128:(oc + 1) * 128, :]), dma=True)
            for c0 in range(0, 16, 4):
                for j in range(4):
                    hc = c0 + j
                    sc.add("tensor", lambda e, hc=hc, j=j: nc.tensor.transpose(
                        pT[:, j * 128:(j + 1) * 128], ab[:, hc * 128:(hc + 1) * 128], identb[:]))
                for j in range(4):
                    hc = c0 + j
                    sc.add("scalar", lambda e, hc=hc, j=j, oc=oc: nc.scalar.copy(
                        wtbuf[:, hc, oc * 128:(oc + 1) * 128], pT[:, j * 128:(j + 1) * 128]))
        for i in range(NT):
            sc.add("sync", lambda e, i=i: e.dma_start(out=ab[:], in_=at_in[i * 128:(i + 1) * 128, :]), dma=True)
            sc.add("sync", lambda e, i=i: e.dma_start(out=sx[:], in_=x_in[i * 128:(i + 1) * 128, :]), dma=True)
            for c0 in range(0, 16, 4):
                for j in range(4):
                    hc = c0 + j
                    sc.add("tensor", lambda e, hc=hc, j=j: nc.tensor.transpose(
                        pT[:, j * 128:(j + 1) * 128], ab[:, hc * 128:(hc + 1) * 128], identb[:]))
                for j in range(4):
                    hc = c0 + j
                    sc.add("scalar", lambda e, hc=hc, j=j: nc.scalar.copy(
                        abT[:, hc, :], pT[:, j * 128:(j + 1) * 128]))
            for b4 in range(4):
                for hc in range(16):
                    sc.add("tensor", lambda e, b4=b4, hc=hc: nc.tensor.matmul(
                        pmm[:], abT[:, hc, :], wtbuf[:, hc, b4 * 512:(b4 + 1) * 512],
                        start=(hc == 0), stop=(hc == 15)))
                sc.add("vector", lambda e, b4=b4: nc.vector.tensor_add(
                    sx[:, b4 * 512:(b4 + 1) * 512], sx[:, b4 * 512:(b4 + 1) * 512], pmm[:]))
            sc.add("gpsimd", lambda e, i=i: e.dma_start(out=o_x1[i * 128:(i + 1) * 128, :], in_=sx[:]), dma=True)

        sc.add("vector", lambda e: nc.vector.memset(asum[:], 0.0))
        for c in range(64):
            sc.add("sync", lambda e, c=c: e.dma_start(out=wcL[:, 0:D], in_=wfc_in[c * 128:(c + 1) * 128, :]), dma=True)
            sc.add("vector", lambda e: nc.vector.reduce_sum(st1[:], wcL[:, 0:D], axis=AX.X, apply_absolute_value=True))
            sc.add("vector", lambda e: nc.vector.tensor_add(asum[:], asum[:], st1[:]))
        sc.add("tensor", lambda e: nc.tensor.matmul(pg[:], asum[:], onescol[:], start=True, stop=True))
        sc.add("scalar", lambda e: nc.scalar.copy(sg[:], pg[:]))
        sc.add("vector", lambda e: nc.vector.tensor_scalar_mul(sg[:], sg[:], 1.0 / (4 * D * D)))
        sc.add("tensor", lambda e: nc.tensor.matmul(pbc[:], sones[:], sg[:], start=True, stop=True))
        sc.add("scalar", lambda e: nc.scalar.copy(gb1[:], pbc[:]))
        sc.add("vector", lambda e: nc.vector.tensor_scalar(
            thrb[:], gb1[:], 0.5, 0.5e-6, op0=ALU.mult, op1=ALU.add))
        sc.add("vector", lambda e: nc.vector.tensor_scalar_mul(negthr[:], thrb[:], -1.0))

        for c in range(64):
            sc.add("sync", lambda e, c=c: e.dma_start(out=wcL[:, 0:D], in_=wfc_in[c * 128:(c + 1) * 128, :]), dma=True)
            sc.add("vector", lambda e: nc.vector.tensor_scalar(
                g1[:, 0:D], wcL[:, 0:D], thrb[:], None, op0=ALU.is_ge))
            sc.add("vector", lambda e: nc.vector.tensor_scalar(
                g3[:, 0:D], wcL[:, 0:D], negthr[:], None, op0=ALU.is_le))
            sc.add("vector", lambda e: nc.vector.tensor_sub(g1[:, 0:D], g1[:, 0:D], g3[:, 0:D]))
            for c0 in range(0, 16, 4):
                for j in range(4):
                    dc = c0 + j
                    sc.add("tensor", lambda e, dc=dc, j=j: nc.tensor.transpose(
                        pT[:, j * 128:(j + 1) * 128], g1[:, dc * 128:(dc + 1) * 128], identb[:]))
                sc.add("scalar", lambda e: nc.scalar.copy(tb[:], pT[:]))
                for j in range(4):
                    dc = c0 + j
                    sc.add("gpsimd", lambda e, dc=dc, j=j, c=c: e.dma_start(
                        out=t1t_scr[dc, c], in_=tb[:, j * 128:(j + 1) * 128]), dma=True)

        for i in range(NT):
            sc.add("sync", lambda e, i=i: e.dma_start(out=xnb[:], in_=xn_in[i * 128:(i + 1) * 128, :]), dma=True)
            sc.add("vector", lambda e: nc.vector.reduce_sum(st1[:], xnb[:], axis=AX.X))
            sc.add("scalar", lambda e: nc.scalar.activation(g1[:, 0:D], xnb[:], AF.Square, accum_out=st2[:]))
            sc.add("vector", lambda e: nc.vector.tensor_scalar_mul(st1[:], st1[:], 1.0 / D))
            sc.add("vector", lambda e: nc.vector.tensor_scalar_mul(st2[:], st2[:], 1.0 / D))
            sc.add("vector", lambda e: nc.vector.tensor_mul(st3[:], st1[:], st1[:]))
            sc.add("vector", lambda e: nc.vector.tensor_sub(st2[:], st2[:], st3[:]))
            sc.add("scalar", lambda e: nc.scalar.activation(st3[:], st2[:], AF.Sqrt, bias=epsb[:]))
            sc.add("vector", lambda e: nc.vector.reciprocal(st4[:], st3[:]))
            sc.add("vector", lambda e: nc.vector.scalar_tensor_tensor(
                st3[:], st1[:], -1.0, st4[:], op0=ALU.mult, op1=ALU.mult))
            sc.add("vector", lambda e: nc.vector.scalar_tensor_tensor(
                xh[:], xnb[:], st4[:], st3[:].broadcast_to([128, D]), op0=ALU.mult, op1=ALU.add))
            sc.add("vector", lambda e: nc.vector.reduce_max(st1[:], xh[:], axis=AX.X, apply_absolute_value=True))
            sc.add("vector", lambda e: nc.vector.tensor_scalar_mul(st1[:], st1[:], (128.0 - 1e-6) / 128.0))
            sc.add("vector", lambda e: nc.vector.tensor_scalar_mul(st2[:], st1[:], -1.0))
            sc.add("vector", lambda e: nc.vector.tensor_scalar(
                xnb[:], xh[:], st1[:], st2[:], op0=ALU.min, op1=ALU.max))
            for c0 in range(0, 16, 4):
                for j in range(4):
                    dc = c0 + j
                    sc.add("tensor", lambda e, dc=dc, j=j: nc.tensor.transpose(
                        pT[:, j * 128:(j + 1) * 128], xnb[:, dc * 128:(dc + 1) * 128], identb[:]))
                for j in range(4):
                    dc = c0 + j
                    sc.add("scalar", lambda e, dc=dc, j=j, i=i: nc.scalar.copy(
                        wtbuf[:, dc, i * 128:(i + 1) * 128], pT[:, j * 128:(j + 1) * 128]))

        for os_ in range(16):
            for dc in range(16):
                sc.add("sync", lambda e, os_=os_, dc=dc: e.dma_start(
                    out=rhsb[:, dc, :].rearrange("p (a o) -> p a o", a=4),
                    in_=t1t_scr[dc, os_ * 4:os_ * 4 + 4].rearrange("a p o -> p a o")), dma=True)
            for i in range(NT):
                for dc in range(16):
                    sc.add("tensor", lambda e, i=i, dc=dc: nc.tensor.matmul(
                        pmm[:], wtbuf[:, dc, i * 128:(i + 1) * 128], rhsb[:, dc, :],
                        start=(dc == 0), stop=(dc == 15)))
                sc.add("scalar", lambda e: nc.scalar.activation(
                    tb[:], pmm[:], AF.Gelu, scale=gb1[:]))
                sc.add("gpsimd", lambda e, i=i, os_=os_: e.dma_start(
                    out=h_scr[i * 128:(i + 1) * 128, os_ * 512:(os_ + 1) * 512], in_=tb[:]), dma=True)

        sc.add("vector", lambda e: nc.vector.memset(asum[:], 0.0))
        for c in range(16):
            sc.add("sync", lambda e, c=c: e.dma_start(out=wcL[:], in_=wpr_in[c * 128:(c + 1) * 128, :]), dma=True)
            sc.add("vector", lambda e: nc.vector.reduce_sum(st1[:], wcL[:], axis=AX.X, apply_absolute_value=True))
            sc.add("vector", lambda e: nc.vector.tensor_add(asum[:], asum[:], st1[:]))
        sc.add("tensor", lambda e: nc.tensor.matmul(pg[:], asum[:], onescol[:], start=True, stop=True))
        sc.add("scalar", lambda e: nc.scalar.copy(sg[:], pg[:]))
        sc.add("vector", lambda e: nc.vector.tensor_scalar_mul(sg[:], sg[:], 1.0 / (4 * D * D)))
        sc.add("tensor", lambda e: nc.tensor.matmul(pbc[:], sones[:], sg[:], start=True, stop=True))
        sc.add("scalar", lambda e: nc.scalar.copy(gb2[:], pbc[:]))
        sc.add("vector", lambda e: nc.vector.tensor_scalar(
            thrb[:], gb2[:], 0.5, 0.5e-6, op0=ALU.mult, op1=ALU.add))
        sc.add("vector", lambda e: nc.vector.tensor_scalar_mul(negthr[:], thrb[:], -1.0))
        for c in range(16):
            sc.add("sync", lambda e, c=c: e.dma_start(out=wcL[:], in_=wpr_in[c * 128:(c + 1) * 128, :]), dma=True)
            sc.add("vector", lambda e: nc.vector.tensor_scalar(
                g1[:], wcL[:], thrb[:], None, op0=ALU.is_ge))
            sc.add("vector", lambda e: nc.vector.tensor_scalar(
                g3[:], wcL[:], negthr[:], None, op0=ALU.is_le))
            sc.add("vector", lambda e: nc.vector.tensor_sub(g1[:], g1[:], g3[:]))
            for c0 in range(0, 64, 4):
                for j in range(4):
                    dc = c0 + j
                    sc.add("tensor", lambda e, dc=dc, j=j: nc.tensor.transpose(
                        pT[:, j * 128:(j + 1) * 128], g1[:, dc * 128:(dc + 1) * 128], identb[:]))
                sc.add("scalar", lambda e: nc.scalar.copy(tb[:], pT[:]))
                for j in range(4):
                    dc = c0 + j
                    sc.add("gpsimd", lambda e, dc=dc, j=j, c=c: e.dma_start(
                        out=t2t_scr[dc, c], in_=tb[:, j * 128:(j + 1) * 128]), dma=True)

        for i in range(NT):
            sc.add("sync", lambda e, i=i: e.dma_start(out=hb2[:], in_=h_scr[i * 128:(i + 1) * 128, :]), dma=True)
            sc.add("vector", lambda e: nc.vector.reduce_sum(st1[:], hb2[:], axis=AX.X))
            sc.add("scalar", lambda e: nc.scalar.activation(g1[:], hb2[:], AF.Square, accum_out=st2[:]))
            sc.add("vector", lambda e: nc.vector.tensor_scalar_mul(st1[:], st1[:], 1.0 / (4 * D)))
            sc.add("vector", lambda e: nc.vector.tensor_scalar_mul(st2[:], st2[:], 1.0 / (4 * D)))
            sc.add("vector", lambda e: nc.vector.tensor_mul(st3[:], st1[:], st1[:]))
            sc.add("vector", lambda e: nc.vector.tensor_sub(st2[:], st2[:], st3[:]))
            sc.add("scalar", lambda e: nc.scalar.activation(st3[:], st2[:], AF.Sqrt, bias=epsb[:]))
            sc.add("vector", lambda e: nc.vector.reciprocal(st4[:], st3[:]))
            sc.add("vector", lambda e: nc.vector.scalar_tensor_tensor(
                st3[:], st1[:], -1.0, st4[:], op0=ALU.mult, op1=ALU.mult))
            sc.add("vector", lambda e: nc.vector.scalar_tensor_tensor(
                wcL[:], hb2[:], st4[:], st3[:].broadcast_to([128, 4 * D]), op0=ALU.mult, op1=ALU.add))
            sc.add("vector", lambda e: nc.vector.reduce_max(st1[:], wcL[:], axis=AX.X, apply_absolute_value=True))
            sc.add("vector", lambda e: nc.vector.tensor_scalar_mul(st1[:], st1[:], (128.0 - 1e-6) / 128.0))
            sc.add("vector", lambda e: nc.vector.tensor_scalar_mul(st2[:], st1[:], -1.0))
            sc.add("vector", lambda e: nc.vector.tensor_scalar(
                hb2[:], wcL[:], st1[:], st2[:], op0=ALU.min, op1=ALU.max))
            for c0 in range(0, 64, 4):
                for j in range(4):
                    dc = c0 + j
                    sc.add("tensor", lambda e, dc=dc, j=j: nc.tensor.transpose(
                        pT[:, j * 128:(j + 1) * 128], hb2[:, dc * 128:(dc + 1) * 128], identb[:]))
                sc.add("scalar", lambda e: nc.scalar.copy(tb[:], pT[:]))
                for j in range(4):
                    dc = c0 + j
                    sc.add("gpsimd", lambda e, dc=dc, j=j, i=i: e.dma_start(
                        out=x2t_scr[dc, i], in_=tb[:, j * 128:(j + 1) * 128]), dma=True)

        for os_ in range(4):
            for dc in range(64):
                sc.add("sync", lambda e, os_=os_, dc=dc: e.dma_start(
                    out=rhs2[:, dc, :].rearrange("p (a o) -> p a o", a=4),
                    in_=t2t_scr[dc, os_ * 4:os_ * 4 + 4].rearrange("a p o -> p a o")), dma=True)
            for i in range(NT):
                sc.add("sync", lambda e, i=i: e.dma_start(
                    out=x2tb_[:].rearrange("p (a o) -> p a o", a=64),
                    in_=x2t_scr[:, i].rearrange("a p o -> p a o")), dma=True)
                for dc in range(64):
                    sc.add("tensor", lambda e, i=i, dc=dc: nc.tensor.matmul(
                        pmm[:], x2tb[:, dc, :], rhs2[:, dc, :],
                        start=(dc == 0), stop=(dc == 63)))
                sc.add("vector", lambda e: nc.vector.tensor_scalar(
                    ob[:], pmm[:], gb2[:], None, op0=ALU.mult))
                sc.add("gpsimd", lambda e, i=i, os_=os_: e.dma_start(
                    out=o_x1[i * 128:(i + 1) * 128, os_ * 512:(os_ + 1) * 512], in_=ob[:],
                    accum_op=ALU.add), dma=True)
        sc.emit(nc, sem)
    return nc


def run_l0post(inp, pre, attn):
    nc = build_l0post()
    xs = _slices(inp["L0_attn_hidden_states"])
    ats = _slices(np.asarray(attn).reshape(2, T, D))
    xns = _slices(np.asarray(pre["xn"]).reshape(2, T, D))
    in_maps = []
    for c in range(NCORE):
        in_maps.append({
            "x": xs[c],
            "attn": ats[c].astype(BF),
            "xn": xns[c].astype(BF),
            "wo": np.asarray(inp["L0_attn_Wo"]),
            "wfc": np.asarray(inp["L0_ffn_w_fc"]),
            "wproj": np.asarray(inp["L0_ffn_w_proj"]),
        })
    res = run_bass_kernel_spmd(nc, in_maps, core_ids=list(range(NCORE)))
    return _gather([res.results[c]["x1"] for c in range(NCORE)])


def build_l1pre():
    nc = bass.Bass("TRN2", target_bir_lowering=False)
    x_in = nc.dram_tensor("x1", [TPC, D], F32, kind="ExternalInput").ap()
    wqc_in = nc.dram_tensor("wqc", [D, 1024], F32, kind="ExternalInput").ap()
    wqup_in = nc.dram_tensor("wqup", [1024, D], F32, kind="ExternalInput").ap()
    wkvb_in = nc.dram_tensor("wkvb", [D, 128], F32, kind="ExternalInput").ap()
    wzb_in = nc.dram_tensor("wzb", [D, 128], F32, kind="ExternalInput").ap()
    biasb_in = nc.dram_tensor("biasb", [32, 128], F32, kind="ExternalInput").ap()
    wk_in = nc.dram_tensor("wk", [D, 128], F32, kind="ExternalInput").ap()
    wv_in = nc.dram_tensor("wv", [D, 128], F32, kind="ExternalInput").ap()
    wikv_in = nc.dram_tensor("wikv", [D, 64], F32, kind="ExternalInput").ap()
    wiz_in = nc.dram_tensor("wiz", [D, 64], F32, kind="ExternalInput").ap()
    biasi_in = nc.dram_tensor("biasi", [32, 64], F32, kind="ExternalInput").ap()
    wdq_in = nc.dram_tensor("wdq", [D, 64], F32, kind="ExternalInput").ap()
    wiuq_in = nc.dram_tensor("wiuq", [64, 256], F32, kind="ExternalInput").ap()
    ww_in = nc.dram_tensor("ww", [D, 4], F32, kind="ExternalInput").ap()
    qnw_in = nc.dram_tensor("qnw", [128], F32, kind="ExternalInput").ap()
    knw_in = nc.dram_tensor("knw", [128], F32, kind="ExternalInput").ap()
    cos_in = nc.dram_tensor("cos", [TPC, 32], F32, kind="ExternalInput").ap()
    sin_in = nc.dram_tensor("sin", [TPC, 32], F32, kind="ExternalInput").ap()
    o_xn1 = nc.dram_tensor("xn1", [TPC, D], BF16, kind="ExternalOutput").ap()
    o_q = nc.dram_tensor("q", [TPC, D], BF16, kind="ExternalOutput").ap()
    o_swk = nc.dram_tensor("swk", [TPC, 128], BF16, kind="ExternalOutput").ap()
    o_swv = nc.dram_tensor("swv", [TPC, 128], BF16, kind="ExternalOutput").ap()
    o_ckv = nc.dram_tensor("compkvT", [128, 64], F32, kind="ExternalOutput").ap()
    o_ci = nc.dram_tensor("compiT", [64, 64], F32, kind="ExternalOutput").ap()
    o_qi = nc.dram_tensor("qi", [TPC, 256], BF16, kind="ExternalOutput").ap()
    o_wgt = nc.dram_tensor("wgt", [TPC, 4], F32, kind="ExternalOutput").ap()

    sc = Sched()
    with ExitStack() as es:
        sb = lambda n, s, d: es.enter_context(nc.sbuf_tensor(n, s, d))
        ps = lambda n, s, d=F32: es.enter_context(nc.psum_tensor(n, s, d))
        identb = sb("identb", [128, 128], BF16)
        identf = sb("identf", [128, 128], F32)
        swqc_ = sb("swqc", [128, 16 * 1024], BF16)
        swqup_ = sb("swqup", [128, 8 * D], BF16)
        swkvb_ = sb("swkvb", [128, 16 * 128], BF16)
        swzb_ = sb("swzb", [128, 16 * 128], BF16)
        swk_ = sb("swkw", [128, 16 * 128], BF16)
        swv_ = sb("swvw", [128, 16 * 128], BF16)
        swikv_ = sb("swikv", [128, 16 * 64], BF16)
        swiz_ = sb("swiz", [128, 16 * 64], BF16)
        swdq_ = sb("swdq", [128, 16 * 64], BF16)
        swiuq = sb("swiuq", [64, 256], BF16)
        sww_ = sb("sww", [128, 16 * 4], BF16)
        biasb = sb("biasbt", [128, 128], F32)
        biasi = sb("biasit", [128, 64], F32)
        qnwb = sb("qnwb", [128, 128], F32)
        knwb = sb("knwb", [128, 128], F32)
        scos = sb("scos", [128, 32], F32)
        ssin = sb("ssin", [128, 32], F32)
        sx = sb("sx", [128, D], F32)
        xn1b = sb("xn1b", [128, D], BF16)
        xnTt_ = sb("xnTt", [128, 16 * 128], BF16)
        hqb = sb("hqb", [128, 1024], BF16)
        hqT_ = sb("hqT", [128, 8 * 128], BF16)
        qf = sb("qf", [128, D], F32)
        qrot = sb("qrot", [128, D], F32)
        tmp32 = sb("tmp32", [128, NH * 32], F32)
        nbuf = sb("nbuf", [128, NH], F32)
        rstd16 = sb("rstd16", [128, NH], F32)
        ntrash = sb("ntrash", [128, 128], F32)
        outb = sb("outb", [128, D], BF16)
        k0 = sb("k0", [128, 128], F32)
        k1 = sb("k1", [128, 128], F32)
        kb16 = sb("kb16", [128, 128], BF16)
        st1 = sb("st1", [128, 1], F32)
        st2 = sb("st2", [128, 1], F32)
        st3 = sb("st3", [128, 1], F32)
        st4 = sb("st4", [128, 1], F32)
        eps5 = sb("eps5", [128, 1], F32)
        eps6 = sb("eps6", [128, 1], F32)
        zf = sb("zf", [128, 128], F32)
        cf = sb("cf", [128, 128], F32)
        zT = sb("zT", [128, 128], F32)
        cT = sb("cT", [128, 128], F32)
        zm = sb("zm", [128, 4], F32)
        gs = sb("gs", [128, 4], F32)
        ckvT = sb("ckvT", [128, 64], F32)
        ciT = sb("ciT", [64, 64], F32)
        dqb = sb("dqb", [128, 64], BF16)
        dqT = sb("dqT", [64, 128], BF16)
        qib = sb("qib", [128, 256], BF16)
        wgtb = sb("wgtb", [128, 4], F32)
        pmm = ps("pmm", [128, 512])
        pc = ps("pc", [128, 128])
        pz = ps("pz", [128, 128])
        pT = es.enter_context(nc.psum_tensor("pT", [128, 512], BF16))
        pT2 = ps("pT2", [128, 256])
        pqi = ps("pqi", [128, 256])
        sem = es.enter_context(nc.semaphore("sem"))
        _IDENT[0] = identb
        swqc = swqc_[:].rearrange("p (c t) -> p c t", c=16)
        swqup = swqup_[:].rearrange("p (c t) -> p c t", c=8)
        swkvb = swkvb_[:].rearrange("p (c t) -> p c t", c=16)
        swzb = swzb_[:].rearrange("p (c t) -> p c t", c=16)
        swkw = swk_[:].rearrange("p (c t) -> p c t", c=16)
        swvw = swv_[:].rearrange("p (c t) -> p c t", c=16)
        swikv = swikv_[:].rearrange("p (c t) -> p c t", c=16)
        swiz = swiz_[:].rearrange("p (c t) -> p c t", c=16)
        swdq = swdq_[:].rearrange("p (c t) -> p c t", c=16)
        sww = sww_[:].rearrange("p (c t) -> p c t", c=16)
        xnTt = xnTt_[:].rearrange("p (c t) -> p c t", c=16)
        hqT = hqT_[:].rearrange("p (c t) -> p c t", c=8)
        _mk_consts(nc, sc, identb, ident_f32=identf)
        sc.add("gpsimd", lambda e: nc.gpsimd.memset(eps5[:], 1e-5))
        sc.add("gpsimd", lambda e: nc.gpsimd.memset(eps6[:], 1e-6))
        for dc in range(16):
            sc.add("gpsimd", lambda e, dc=dc: e.dma_start(out=swqc[:, dc, :], in_=wqc_in[dc * 128:(dc + 1) * 128, :]), dma=True)
            sc.add("gpsimd", lambda e, dc=dc: e.dma_start(out=swkvb[:, dc, :], in_=wkvb_in[dc * 128:(dc + 1) * 128, :]), dma=True)
            sc.add("gpsimd", lambda e, dc=dc: e.dma_start(out=swzb[:, dc, :], in_=wzb_in[dc * 128:(dc + 1) * 128, :]), dma=True)
            sc.add("gpsimd", lambda e, dc=dc: e.dma_start(out=swkw[:, dc, :], in_=wk_in[dc * 128:(dc + 1) * 128, :]), dma=True)
            sc.add("gpsimd", lambda e, dc=dc: e.dma_start(out=swvw[:, dc, :], in_=wv_in[dc * 128:(dc + 1) * 128, :]), dma=True)
            sc.add("gpsimd", lambda e, dc=dc: e.dma_start(out=swikv[:, dc, :], in_=wikv_in[dc * 128:(dc + 1) * 128, :]), dma=True)
            sc.add("gpsimd", lambda e, dc=dc: e.dma_start(out=swiz[:, dc, :], in_=wiz_in[dc * 128:(dc + 1) * 128, :]), dma=True)
            sc.add("gpsimd", lambda e, dc=dc: e.dma_start(out=swdq[:, dc, :], in_=wdq_in[dc * 128:(dc + 1) * 128, :]), dma=True)
            sc.add("gpsimd", lambda e, dc=dc: e.dma_start(out=sww[:, dc, :], in_=ww_in[dc * 128:(dc + 1) * 128, :]), dma=True)
        for dc in range(8):
            sc.add("gpsimd", lambda e, dc=dc: e.dma_start(out=swqup[:, dc, :], in_=wqup_in[dc * 128:(dc + 1) * 128, :]), dma=True)
        sc.add("gpsimd", lambda e: e.dma_start(out=swiuq[:], in_=wiuq_in), dma=True)
        for r in range(4):
            sc.add("sync", lambda e, r=r: e.dma_start(out=biasb[r * 32:(r + 1) * 32, :], in_=biasb_in), dma=True)
            sc.add("sync", lambda e, r=r: e.dma_start(out=biasi[r * 32:(r + 1) * 32, :], in_=biasi_in), dma=True)
        sc.add("sync", lambda e: e.dma_start(out=qnwb[:], in_=qnw_in.unsqueeze(0).broadcast_to([128, 128])), dma=True)
        sc.add("sync", lambda e: e.dma_start(out=knwb[:], in_=knw_in.unsqueeze(0).broadcast_to([128, 128])), dma=True)

        for i in range(NT):
            sl = slice(i * 128, (i + 1) * 128)
            sc.add("sync", lambda e, sl=sl: e.dma_start(out=sx[:], in_=x_in[sl, :]), dma=True)
            sc.add("sync", lambda e, sl=sl: e.dma_start(out=scos[:], in_=cos_in[sl, :]), dma=True)
            sc.add("sync", lambda e, sl=sl: e.dma_start(out=ssin[:], in_=sin_in[sl, :]), dma=True)
            sc.add("vector", lambda e: nc.vector.reduce_sum(st1[:], sx[:], axis=AX.X))
            sc.add("scalar", lambda e: nc.scalar.activation(ntrash[:], sx[:, 0:128], AF.Square, accum_out=st2[:]))
            for cq in range(1, 16):
                sc.add("scalar", lambda e, cq=cq: nc.scalar.activation(
                    ntrash[:], sx[:, cq * 128:(cq + 1) * 128], AF.Square, accum_out=nbuf[:, cq:cq + 1]))
            sc.add("vector", lambda e: nc.vector.reduce_sum(st3[:], nbuf[:, 1:16], axis=AX.X))
            sc.add("vector", lambda e: nc.vector.tensor_add(st2[:], st2[:], st3[:]))
            sc.add("vector", lambda e: nc.vector.tensor_scalar_mul(st1[:], st1[:], 1.0 / D))
            sc.add("vector", lambda e: nc.vector.tensor_scalar_mul(st2[:], st2[:], 1.0 / D))
            sc.add("vector", lambda e: nc.vector.tensor_mul(st3[:], st1[:], st1[:]))
            sc.add("vector", lambda e: nc.vector.tensor_sub(st2[:], st2[:], st3[:]))
            sc.add("scalar", lambda e: nc.scalar.activation(st3[:], st2[:], AF.Sqrt, bias=eps5[:]))
            sc.add("vector", lambda e: nc.vector.reciprocal(st4[:], st3[:]))
            sc.add("vector", lambda e: nc.vector.scalar_tensor_tensor(
                st3[:], st1[:], -1.0, st4[:], op0=ALU.mult, op1=ALU.mult))
            sc.add("vector", lambda e: nc.vector.scalar_tensor_tensor(
                xn1b[:], sx[:], st4[:], st3[:].broadcast_to([128, D]), op0=ALU.mult, op1=ALU.add))
            sc.add("gpsimd", lambda e, sl=sl: e.dma_start(out=o_xn1[sl, :], in_=xn1b[:]), dma=True)
            for c0 in range(0, 16, 4):
                for j in range(4):
                    dc = c0 + j
                    sc.add("tensor", lambda e, dc=dc, j=j: nc.tensor.transpose(
                        pT[:, j * 128:(j + 1) * 128], xn1b[:, dc * 128:(dc + 1) * 128], identb[:]))
                for j in range(4):
                    dc = c0 + j
                    sc.add("scalar", lambda e, dc=dc, j=j: nc.scalar.copy(
                        xnTt[:, dc, :], pT[:, j * 128:(j + 1) * 128]))
            for b2 in range(2):
                for dc in range(16):
                    sc.add("tensor", lambda e, b2=b2, dc=dc: nc.tensor.matmul(
                        pmm[:], xnTt[:, dc, :], swqc[:, dc, b2 * 512:(b2 + 1) * 512],
                        start=(dc == 0), stop=(dc == 15)))
                sc.add("scalar", lambda e, b2=b2: nc.scalar.copy(hqb[:, b2 * 512:(b2 + 1) * 512], pmm[:]))
            for c0 in range(0, 8, 4):
                for j in range(4):
                    dc = c0 + j
                    sc.add("tensor", lambda e, dc=dc, j=j: nc.tensor.transpose(
                        pT[:, j * 128:(j + 1) * 128], hqb[:, dc * 128:(dc + 1) * 128], identb[:]))
                for j in range(4):
                    dc = c0 + j
                    sc.add("scalar", lambda e, dc=dc, j=j: nc.scalar.copy(
                        hqT[:, dc, :], pT[:, j * 128:(j + 1) * 128]))
            for b4 in range(4):
                for dc in range(8):
                    sc.add("tensor", lambda e, b4=b4, dc=dc: nc.tensor.matmul(
                        pmm[:], hqT[:, dc, :], swqup[:, dc, b4 * 512:(b4 + 1) * 512],
                        start=(dc == 0), stop=(dc == 7)))
                sc.add("scalar", lambda e, b4=b4: nc.scalar.copy(qf[:, b4 * 512:(b4 + 1) * 512], pmm[:]))
            qv = qf[:].rearrange("p (h d) -> p h d", h=NH)
            rv = qrot[:].rearrange("p (h d) -> p h d", h=NH)
            tv = tmp32[:].rearrange("p (h d) -> p h d", h=NH)
            cosb = scos[:].unsqueeze(1).broadcast_to([128, NH, 32])
            sinb = ssin[:].unsqueeze(1).broadcast_to([128, NH, 32])
            sc.add("vector", lambda e: nc.vector.tensor_mul(rv[:, :, 0:32], qv[:, :, 0:32], cosb))
            sc.add("vector", lambda e: nc.vector.tensor_mul(tv, qv[:, :, 32:64], sinb))
            sc.add("vector", lambda e: nc.vector.tensor_sub(rv[:, :, 0:32], rv[:, :, 0:32], tv))
            sc.add("vector", lambda e: nc.vector.tensor_mul(rv[:, :, 32:64], qv[:, :, 32:64], cosb))
            sc.add("vector", lambda e: nc.vector.tensor_mul(tv, qv[:, :, 0:32], sinb))
            sc.add("vector", lambda e: nc.vector.tensor_add(rv[:, :, 32:64], rv[:, :, 32:64], tv))
            sc.add("vector", lambda e: nc.vector.tensor_copy(rv[:, :, 64:128], qv[:, :, 64:128]))
            for h in range(NH):
                sc.add("scalar", lambda e, h=h: nc.scalar.activation(
                    ntrash[:], qrot[:, h * 128:(h + 1) * 128], AF.Square, accum_out=nbuf[:, h:h + 1]))
            sc.add("vector", lambda e: nc.vector.tensor_scalar(
                rstd16[:], nbuf[:], 1.0 / 128.0, None, op0=ALU.mult))
            sc.add("scalar", lambda e: nc.scalar.activation(rstd16[:], rstd16[:], AF.Sqrt, bias=eps6[:]))
            sc.add("vector", lambda e: nc.vector.reciprocal(rstd16[:], rstd16[:]))
            for h in range(NH):
                sc.add("vector", lambda e, h=h: nc.vector.scalar_tensor_tensor(
                    outb[:, h * 128:(h + 1) * 128], qrot[:, h * 128:(h + 1) * 128],
                    rstd16[:, h:h + 1], qnwb[:], op0=ALU.mult, op1=ALU.mult))
            sc.add("gpsimd", lambda e, sl=sl: e.dma_start(out=o_q[sl, :], in_=outb[:]), dma=True)
            for dc in range(16):
                sc.add("tensor", lambda e, dc=dc: nc.tensor.matmul(
                    pc[:], xnTt[:, dc, :], swkw[:, dc, :], start=(dc == 0), stop=(dc == 15)))
            sc.add("scalar", lambda e: nc.scalar.copy(k0[:], pc[:]))
            sc.add("scalar", lambda e: nc.scalar.activation(ntrash[:], k0[:], AF.Square, accum_out=st1[:]))
            sc.add("vector", lambda e: nc.vector.tensor_scalar_mul(st1[:], st1[:], 1.0 / 128.0))
            sc.add("scalar", lambda e: nc.scalar.activation(st1[:], st1[:], AF.Sqrt, bias=eps6[:]))
            sc.add("vector", lambda e: nc.vector.reciprocal(st1[:], st1[:]))
            sc.add("vector", lambda e: nc.vector.scalar_tensor_tensor(
                k1[:], k0[:], st1[:], knwb[:], op0=ALU.mult, op1=ALU.mult))
            sc.add("vector", lambda e: nc.vector.tensor_mul(kb16[:, 0:32], k1[:, 0:32], scos[:]))
            sc.add("vector", lambda e: nc.vector.tensor_mul(tmp32[:, 0:32], k1[:, 32:64], ssin[:]))
            sc.add("vector", lambda e: nc.vector.tensor_sub(kb16[:, 0:32], kb16[:, 0:32], tmp32[:, 0:32]))
            sc.add("vector", lambda e: nc.vector.tensor_mul(kb16[:, 32:64], k1[:, 32:64], scos[:]))
            sc.add("vector", lambda e: nc.vector.tensor_mul(tmp32[:, 0:32], k1[:, 0:32], ssin[:]))
            sc.add("vector", lambda e: nc.vector.tensor_add(kb16[:, 32:64], kb16[:, 32:64], tmp32[:, 0:32]))
            sc.add("vector", lambda e: nc.vector.tensor_copy(kb16[:, 64:128], k1[:, 64:128]))
            sc.add("gpsimd", lambda e, sl=sl: e.dma_start(out=o_swk[sl, :], in_=kb16[:]), dma=True)
            for dc in range(16):
                sc.add("tensor", lambda e, dc=dc: nc.tensor.matmul(
                    pc[:], xnTt[:, dc, :], swvw[:, dc, :], start=(dc == 0), stop=(dc == 15)))
            sc.add("scalar", lambda e: nc.scalar.copy(kb16[:], pc[:]))
            sc.add("gpsimd", lambda e, sl=sl: e.dma_start(out=o_swv[sl, :], in_=kb16[:]), dma=True)
            for dc in range(16):
                sc.add("tensor", lambda e, dc=dc: nc.tensor.matmul(
                    pc[:], xnTt[:, dc, :], swkvb[:, dc, :], start=(dc == 0), stop=(dc == 15)))
            for dc in range(16):
                sc.add("tensor", lambda e, dc=dc: nc.tensor.matmul(
                    pz[:], xnTt[:, dc, :], swzb[:, dc, :], start=(dc == 0), stop=(dc == 15)))
            sc.add("vector", lambda e: nc.vector.tensor_add(zf[:], pz[:], biasb[:]))
            sc.add("scalar", lambda e: nc.scalar.copy(cf[:], pc[:]))
            sc.add("tensor", lambda e: nc.tensor.transpose(pT2[:, 0:128], zf[:], identf[:]))
            sc.add("tensor", lambda e: nc.tensor.transpose(pT2[:, 128:256], cf[:], identf[:]))
            sc.add("scalar", lambda e: nc.scalar.copy(zT[:], pT2[:, 0:128]))
            sc.add("scalar", lambda e: nc.scalar.copy(cT[:], pT2[:, 128:256]))
            zv = zT[:].rearrange("p (g m) -> p g m", g=4)
            cv = cT[:].rearrange("p (g m) -> p g m", g=4)
            sc.add("vector", lambda e: nc.vector.reduce_max(zm[:], zv, axis=AX.X))
            sc.add("vector", lambda e: nc.vector.tensor_sub(zv, zv, zm[:].unsqueeze(2).broadcast_to([128, 4, 32])))
            sc.add("scalar", lambda e: nc.scalar.activation(zT[:], zT[:], AF.Exp))
            sc.add("vector", lambda e: nc.vector.reduce_sum(gs[:], zv, axis=AX.X))
            sc.add("vector", lambda e: nc.vector.reciprocal(gs[:], gs[:]))
            sc.add("vector", lambda e: nc.vector.tensor_mul(zv, zv, gs[:].unsqueeze(2).broadcast_to([128, 4, 32])))
            sc.add("vector", lambda e: nc.vector.tensor_mul(cv, cv, zv))
            sc.add("vector", lambda e, i=i: nc.vector.reduce_sum(ckvT[:, i * 4:(i + 1) * 4], cv, axis=AX.X))
            for dc in range(16):
                sc.add("tensor", lambda e, dc=dc: nc.tensor.matmul(
                    pc[:, 0:64], xnTt[:, dc, :], swikv[:, dc, :], start=(dc == 0), stop=(dc == 15)))
            for dc in range(16):
                sc.add("tensor", lambda e, dc=dc: nc.tensor.matmul(
                    pz[:, 0:64], xnTt[:, dc, :], swiz[:, dc, :], start=(dc == 0), stop=(dc == 15)))
            sc.add("vector", lambda e: nc.vector.tensor_add(zf[:, 0:64], pz[:, 0:64], biasi[:]))
            sc.add("scalar", lambda e: nc.scalar.copy(cf[:, 0:64], pc[:, 0:64]))
            sc.add("tensor", lambda e: nc.tensor.transpose(pT2[0:64, 0:128], zf[:, 0:64], identf[:]))
            sc.add("tensor", lambda e: nc.tensor.transpose(pT2[0:64, 128:256], cf[:, 0:64], identf[:]))
            sc.add("scalar", lambda e: nc.scalar.copy(zT[0:64, 0:128], pT2[0:64, 0:128]))
            sc.add("scalar", lambda e: nc.scalar.copy(cT[0:64, 0:128], pT2[0:64, 128:256]))
            zv2 = zT[0:64, :].rearrange("p (g m) -> p g m", g=4)
            cv2 = cT[0:64, :].rearrange("p (g m) -> p g m", g=4)
            sc.add("vector", lambda e: nc.vector.reduce_max(zm[0:64, :], zv2, axis=AX.X))
            sc.add("vector", lambda e: nc.vector.tensor_sub(zv2, zv2, zm[0:64, :].unsqueeze(2).broadcast_to([64, 4, 32])))
            sc.add("scalar", lambda e: nc.scalar.activation(zT[0:64, :], zT[0:64, :], AF.Exp))
            sc.add("vector", lambda e: nc.vector.reduce_sum(gs[0:64, :], zv2, axis=AX.X))
            sc.add("vector", lambda e: nc.vector.reciprocal(gs[0:64, :], gs[0:64, :]))
            sc.add("vector", lambda e: nc.vector.tensor_mul(zv2, zv2, gs[0:64, :].unsqueeze(2).broadcast_to([64, 4, 32])))
            sc.add("vector", lambda e: nc.vector.tensor_mul(cv2, cv2, zv2))
            sc.add("vector", lambda e, i=i: nc.vector.reduce_sum(ciT[:, i * 4:(i + 1) * 4], cv2, axis=AX.X))
            for dc in range(16):
                sc.add("tensor", lambda e, dc=dc: nc.tensor.matmul(
                    pc[:, 0:64], xnTt[:, dc, :], swdq[:, dc, :], start=(dc == 0), stop=(dc == 15)))
            sc.add("scalar", lambda e: nc.scalar.copy(dqb[:], pc[:, 0:64]))
            sc.add("tensor", lambda e: nc.tensor.transpose(pT[0:64, 0:128], dqb[:], identb[:]))
            sc.add("scalar", lambda e: nc.scalar.copy(dqT[:], pT[0:64, 0:128]))
            sc.add("tensor", lambda e: nc.tensor.matmul(pqi[:], dqT[:], swiuq[:], start=True, stop=True))
            sc.add("scalar", lambda e: nc.scalar.copy(qib[:], pqi[:]))
            sc.add("gpsimd", lambda e, sl=sl: e.dma_start(out=o_qi[sl, :], in_=qib[:]), dma=True)
            for dc in range(16):
                sc.add("tensor", lambda e, dc=dc: nc.tensor.matmul(
                    pc[:, 0:4], xnTt[:, dc, :], sww[:, dc, :], start=(dc == 0), stop=(dc == 15)))
            sc.add("scalar", lambda e: nc.scalar.copy(wgtb[:], pc[:, 0:4]))
            sc.add("gpsimd", lambda e, sl=sl: e.dma_start(out=o_wgt[sl, :], in_=wgtb[:]), dma=True)
        sc.add("gpsimd", lambda e: e.dma_start(out=o_ckv, in_=ckvT[:]), dma=True)
        sc.add("gpsimd", lambda e: e.dma_start(out=o_ci, in_=ciT[:]), dma=True)
        sc.emit(nc, sem)
    return nc


def run_l1pre(inp, x1):
    cos, sin = _rope_l1()
    nc = build_l1pre()
    xs = _slices(np.asarray(x1).reshape(2, T, D))
    in_maps = []
    for c in range(NCORE):
        r = c % 4
        pos0 = r * TPC
        in_maps.append({
            "x1": xs[c].astype(np.float32),
            "wqc": np.asarray(inp["L1_attn_w_qc"]),
            "wqup": np.asarray(inp["L1_attn_w_qup"]),
            "wkvb": np.asarray(inp["L1_attn_w_kv_b"]),
            "wzb": np.asarray(inp["L1_attn_w_z_b"]),
            "biasb": np.asarray(inp["L1_attn_bias_b"]),
            "wk": np.asarray(inp["L1_attn_w_k"]),
            "wv": np.asarray(inp["L1_attn_w_v"]),
            "wikv": np.asarray(inp["L1_attn_wi_kv"]),
            "wiz": np.asarray(inp["L1_attn_wi_z"]),
            "biasi": np.asarray(inp["L1_attn_bias_i"]),
            "wdq": np.asarray(inp["L1_attn_w_dq"]),
            "wiuq": np.asarray(inp["L1_attn_w_iuq"]),
            "ww": np.asarray(inp["L1_attn_w_w"]),
            "qnw": np.asarray(inp["L1_attn_q_norm_w"]),
            "knw": np.asarray(inp["L1_attn_k_norm_w"]),
            "cos": np.ascontiguousarray(cos[pos0:pos0 + TPC]),
            "sin": np.ascontiguousarray(sin[pos0:pos0 + TPC]),
        })
    res = run_bass_kernel_spmd(nc, in_maps, core_ids=list(range(NCORE)))
    outs = {}
    for name in ("xn1", "q", "swk", "swv", "qi", "wgt"):
        outs[name] = _gather([res.results[c][name] for c in range(NCORE)])
    outs["compkvT"] = np.stack(
        [np.concatenate([res.results[b * 4 + r]["compkvT"] for r in range(4)], axis=1) for b in range(2)])
    outs["compiT"] = np.stack(
        [np.concatenate([res.results[b * 4 + r]["compiT"] for r in range(4)], axis=1) for b in range(2)])
    return outs


def build_l1post():
    nc = bass.Bass("TRN2", target_bir_lowering=False)
    x_in = nc.dram_tensor("x1", [TPC, D], F32, kind="ExternalInput").ap()
    xn_in = nc.dram_tensor("xn1", [TPC, D], BF16, kind="ExternalInput").ap()
    q_in = nc.dram_tensor("q", [TPC, D], BF16, kind="ExternalInput").ap()
    qi_in = nc.dram_tensor("qi", [TPC, 256], BF16, kind="ExternalInput").ap()
    wgt_in = nc.dram_tensor("wgt", [TPC, 4], F32, kind="ExternalInput").ap()
    ckv_in = nc.dram_tensor("ckvT", [128, NB], F32, kind="ExternalInput").ap()
    ci_in = nc.dram_tensor("ciT", [64, NB], F32, kind="ExternalInput").ap()
    swkt_in = nc.dram_tensor("swkt", [128, TPC + 512], BF16, kind="ExternalInput").ap()
    swv_in = nc.dram_tensor("swv", [20, 128, 128], BF16, kind="ExternalInput").ap()
    swmsk_in = nc.dram_tensor("swmsk", [NT, 5, 128, 128], BF16, kind="ExternalInput").ap()
    sink_in = nc.dram_tensor("sink", [1, NH], F32, kind="ExternalInput").ap()
    knw_in = nc.dram_tensor("knw", [128], F32, kind="ExternalInput").ap()
    gw_in = nc.dram_tensor("groupw", [4, 512, 512], F32, kind="ExternalInput").ap()
    fw_in = nc.dram_tensor("finalw", [D, D], F32, kind="ExternalInput").ap()
    wg_in = nc.dram_tensor("wgate", [D, 4 * D], F32, kind="ExternalInput").ap()
    wu_in = nc.dram_tensor("wup", [D, 4 * D], F32, kind="ExternalInput").ap()
    wd_in = nc.dram_tensor("wdown", [4 * D, D], F32, kind="ExternalInput").ap()
    madd_in = nc.dram_tensor("maskadd", [TPC, NB], F32, kind="ExternalInput").ap()
    guard_in = nc.dram_tensor("guard", [TPC, 1], F32, kind="ExternalInput").ap()
    hp_in = nc.dram_tensor("haspick", [TPC, 1], F32, kind="ExternalInput").ap()
    o_x2 = nc.dram_tensor("x2", [TPC, D], F32, kind="ExternalOutput").ap()

    sc = Sched()
    with ExitStack() as es:
        sb = lambda n, s, d: es.enter_context(nc.sbuf_tensor(n, s, d))
        ps = lambda n, s, d=F32: es.enter_context(nc.psum_tensor(n, s, d))
        identb = sb("identb", [128, 128], BF16)
        identf = sb("identf", [128, 128], F32)
        big1_ = sb("big1", [128, 16 * D], BF16)
        arena_ = sb("arena", [128, 24832], BF16)
        sckvT = sb("sckvT", [128, NB], F32)
        sciTb = sb("sciTb", [64, NB], BF16)
        crm0 = sb("crm0", [128, 128], BF16)
        crm1 = sb("crm1", [128, 128], BF16)
        crmT = sb("crmT", [128, NB], BF16)
        comp0 = sb("comp0", [128, 128], F32)
        knwb = sb("knwb", [128, 128], F32)
        sinkb = sb("sinkb", [128, NH], F32)
        sq = sb("sq", [128, D], BF16)
        sqi = sb("sqi", [128, 256], BF16)
        swgt = sb("swgt", [128, 4], F32)
        smadd = sb("smadd", [128, NB], F32)
        sguard = sb("sguard", [128, 1], F32)
        shp = sb("shp", [128, 1], F32)
        smsk = sb("smsk", [128, 5 * 128], BF16)
        qTb = sb("qTb", [128, 512], BF16)
        qiT = sb("qiT", [64, 512], BF16)
        rel0 = sb("rel0", [128, NB], F32)
        iscacc = sb("iscacc", [128, NB], F32)
        iscw = sb("iscw", [128, NB], F32)
        sm8 = sb("sm8", [128, 8], F32)
        sthr = sb("sthr", [128, 1], F32)
        vb = sb("vb", [128, NB], BF16)
        selbit = sb("selbit", [128, NB], BF16)
        sladd = sb("sladd", [128, NB], F32)
        scf = sb("scf", [128, NB], F32)
        esc = sb("esc", [128, NB], BF16)
        escT = sb("escT", [128, NB], BF16)
        esw = sb("esw", [128, 512], BF16)
        eswd = sb("eswd", [128, 128], BF16)
        eswT = sb("eswT", [128, 640], BF16)
        esk = sb("esk", [128, 1], F32)
        attnb = sb("attnb", [128, D], F32)
        attnbb = sb("attnbb", [128, D], BF16)
        aT_ = sb("aT", [128, 16 * 128], BF16)
        ogb = sb("ogb", [128, D], BF16)
        sx = sb("sx", [128, D], F32)
        st1 = sb("st1", [128, 1], F32)
        st2 = sb("st2", [128, 1], F32)
        st3 = sb("st3", [128, 1], F32)
        eps6 = sb("eps6", [128, 1], F32)
        hb = sb("hb", [128, 512], BF16)
        gb = sb("gb", [128, 512], BF16)
        hab = sb("hab", [128, 512], BF16)
        hsp = sb("hsp", [128, 512], BF16)
        hspT_ = sb("hspT", [128, 512], BF16)
        tscr_ = sb("tscr", [128, 6 * 128], BF16)
        obf = sb("obf", [128, 512], F32)
        pA = ps("pA", [128, 512])
        pB = ps("pB", [128, 512])
        pC = ps("pC", [128, 512])
        pD = ps("pD", [128, 129])
        pE = ps("pE", [128, 256])
        pF = ps("pF", [128, 128])
        pT = es.enter_context(nc.psum_tensor("pT", [128, 512], BF16))
        sem = es.enter_context(nc.semaphore("sem"))
        _IDENT[0] = identb
        sfw = big1_[:].rearrange("p (c t) -> p c t", c=16)
        xnT = big1_[:].rearrange("p (c t) -> p c t", c=16)
        swkT = arena_[:, 0:TPC + 512]
        swvext = arena_[:, 0:0 + 8448]
        swkT = arena_[:, 0:TPC + 512]
        swvext = arena_[:, 2560:2560 + 20 * 132].rearrange("p (c x) -> p c x", c=20)
        sgw = arena_[:, 5200:5200 + 16 * 512].rearrange("p (c x) -> p c x", c=16)
        sgate = arena_[:, 0:8192].rearrange("p (c x) -> p c x", c=16)
        sup = arena_[:, 8192:16384].rearrange("p (c x) -> p c x", c=16)
        sdown = arena_[:, 16384:16384 + 4 * 2048].rearrange("p (c x) -> p c x", c=4)
        aT = aT_[:].rearrange("p (c t) -> p c t", c=16)
        hspT = hspT_[:].rearrange("p (c t) -> p c t", c=4)
        tscr = tscr_[:].rearrange("p (c t) -> p c t", c=6)
        _mk_consts(nc, sc, identb, ident_f32=identf)
        sc.add("gpsimd", lambda e: nc.gpsimd.memset(eps6[:], 1e-6))
        sc.add("sync", lambda e: e.dma_start(out=sinkb[:], in_=sink_in.broadcast_to([128, NH])), dma=True)
        sc.add("sync", lambda e: e.dma_start(out=knwb[:], in_=knw_in.unsqueeze(0).broadcast_to([128, 128])), dma=True)
        sc.add("sync", lambda e: e.dma_start(out=sckvT[:], in_=ckv_in), dma=True)
        sc.add("gpsimd", lambda e: e.dma_start(out=sciTb[:], in_=ci_in), dma=True)
        for cchunk in range(2):
            sc.add("tensor", lambda e, cchunk=cchunk: nc.tensor.transpose(
                pE[:, 0:128], sckvT[:, cchunk * 128:(cchunk + 1) * 128], identf[:]))
            sc.add("scalar", lambda e: nc.scalar.copy(comp0[:], pE[:, 0:128]))
            sc.add("scalar", lambda e: nc.scalar.activation(comp0[:], comp0[:], AF.Square, accum_out=st1[:]))
            sc.add("scalar", lambda e: nc.scalar.copy(comp0[:], pE[:, 0:128]))
            sc.add("vector", lambda e: nc.vector.tensor_scalar_mul(st1[:], st1[:], 1.0 / 128.0))
            sc.add("scalar", lambda e: nc.scalar.activation(st1[:], st1[:], AF.Sqrt, bias=eps6[:]))
            sc.add("vector", lambda e: nc.vector.reciprocal(st1[:], st1[:]))
            crm = crm0 if cchunk == 0 else crm1
            sc.add("vector", lambda e, crm=crm: nc.vector.scalar_tensor_tensor(
                crm[:], comp0[:], st1[:], knwb[:], op0=ALU.mult, op1=ALU.mult))
            sc.add("tensor", lambda e, crm=crm: nc.tensor.transpose(pT[:, 0:128], crm[:], identb[:]))
            sc.add("scalar", lambda e, cchunk=cchunk: nc.scalar.copy(
                crmT[:, cchunk * 128:(cchunk + 1) * 128], pT[:, 0:128]))
        for g in range(4):
            for cc in range(4):
                sc.add("gpsimd", lambda e, g=g, cc=cc: e.dma_start(
                    out=sgw[:, g * 4 + cc, :], in_=gw_in[g, cc * 128:(cc + 1) * 128, :]), dma=True)
        for dc in range(16):
            sc.add("gpsimd", lambda e, dc=dc: e.dma_start(
                out=sfw[:, dc, :], in_=fw_in[dc * 128:(dc + 1) * 128, :]), dma=True)
        sc.add("sync", lambda e: e.dma_start(out=swkT[:], in_=swkt_in), dma=True)
        sc.add("sync", lambda e: e.dma_start(
            out=swvext[:, :, 0:128], in_=swv_in.rearrange("c p d -> p c d")), dma=True)
        sc.add("vector", lambda e: nc.vector.memset(swvext[:, :, 128:129], 1.0))

        for i in range(NT):
            sl = slice(i * 128, (i + 1) * 128)
            sc.add("sync", lambda e, sl=sl: e.dma_start(out=sq[:], in_=q_in[sl, :]), dma=True)
            sc.add("sync", lambda e, sl=sl: e.dma_start(out=sqi[:], in_=qi_in[sl, :]), dma=True)
            sc.add("sync", lambda e, sl=sl: e.dma_start(out=swgt[:], in_=wgt_in[sl, :]), dma=True)
            sc.add("sync", lambda e, sl=sl: e.dma_start(out=smadd[:], in_=madd_in[sl, :]), dma=True)
            sc.add("sync", lambda e, sl=sl: e.dma_start(out=sguard[:], in_=guard_in[sl, :]), dma=True)
            sc.add("sync", lambda e, sl=sl: e.dma_start(out=shp[:], in_=hp_in[sl, :]), dma=True)
            sc.add("sync", lambda e, i=i: e.dma_start(
                out=smsk[:].rearrange("p (c x) -> p c x", c=5),
                in_=swmsk_in[i].rearrange("c p x -> p c x")), dma=True)
            for j in range(4):
                sc.add("tensor", lambda e, j=j: nc.tensor.transpose(
                    pT[0:64, j * 128:(j + 1) * 128], sqi[:, j * 64:(j + 1) * 64], identb[:]))
            sc.add("scalar", lambda e: nc.scalar.copy(qiT[:], pT[0:64, :]))
            for h in range(4):
                sc.add("tensor", lambda e, h=h: nc.tensor.matmul(
                    pE[:], qiT[:, h * 128:(h + 1) * 128], sciTb[:], start=True, stop=True))
                sc.add("vector", lambda e: nc.vector.tensor_relu(rel0[:], pE[:]))
                if h == 0:
                    sc.add("vector", lambda e, h=h: nc.vector.tensor_scalar(
                        iscacc[:], rel0[:], swgt[:, h:h + 1], None, op0=ALU.mult))
                else:
                    sc.add("vector", lambda e, h=h: nc.vector.scalar_tensor_tensor(
                        iscacc[:], rel0[:], swgt[:, h:h + 1], iscacc[:], op0=ALU.mult, op1=ALU.add))
            sc.add("vector", lambda e: nc.vector.tensor_add(iscacc[:], iscacc[:], smadd[:]))
            sc.add("vector", lambda e: nc.vector.tensor_copy(iscw[:], iscacc[:]))
            for it in range(4):
                sc.add("vector", lambda e: nc.vector.max(out=sm8[:], in_=iscw[:]))
                sc.add("vector", lambda e: nc.vector.match_replace(
                    out=iscw[:], in_to_replace=sm8[:], in_values=iscw[:], imm_value=NEG))
            sc.add("vector", lambda e: nc.vector.tensor_reduce(
                sthr[:], sm8[:], axis=AX.X, op=ALU.min))
            sc.add("vector", lambda e: nc.vector.tensor_scalar(
                vb[:], smadd[:], -1.0, None, op0=ALU.is_ge))
            sc.add("vector", lambda e: nc.vector.tensor_scalar(
                selbit[:], iscacc[:], sthr[:], None, op0=ALU.is_ge))
            sc.add("vector", lambda e: nc.vector.tensor_mul(selbit[:], selbit[:], vb[:]))
            sc.add("vector", lambda e: nc.vector.tensor_scalar(
                sladd[:], selbit[:], 1.0, 30000.0, op0=ALU.subtract, op1=ALU.mult))
            for h0 in range(0, NH, 4):
                for j in range(4):
                    h = h0 + j
                    sc.add("tensor", lambda e, h=h, j=j: nc.tensor.transpose(
                        pT[:, j * 128:(j + 1) * 128], sq[:, h * 128:(h + 1) * 128], identb[:]))
                sc.add("scalar", lambda e: nc.scalar.copy(qTb[:], pT[:]))
                for j in range(4):
                    h = h0 + j
                    lhs = qTb[:, j * 128:(j + 1) * 128]
                    sc.add("tensor", lambda e, lhs=lhs: nc.tensor.matmul(
                        pE[:], lhs, crmT[:], start=True, stop=True))
                    sc.add("vector", lambda e: nc.vector.tensor_add(scf[:], pE[:], sladd[:]))
                    sc.add("vector", lambda e: nc.vector.reduce_max(st1[:], scf[:], axis=AX.X, negate=True))
                    sc.add("vector", lambda e: nc.vector.tensor_scalar_mul(st1[:], st1[:], ISCALE))
                    sc.add("scalar", lambda e: nc.scalar.activation(
                        esc[:], scf[:], AF.Exp, bias=st1[:], scale=ISCALE, accum_out=st2[:]))
                    sc.add("vector", lambda e: nc.vector.tensor_add(st2[:], st2[:], sguard[:]))
                    sc.add("vector", lambda e: nc.vector.reciprocal(st2[:], st2[:]))
                    sc.add("vector", lambda e: nc.vector.tensor_mul(st2[:], st2[:], shp[:]))
                    for cchunk in range(2):
                        sc.add("tensor", lambda e, cchunk=cchunk: nc.tensor.transpose(
                            pT[:, cchunk * 128:(cchunk + 1) * 128],
                            esc[:, cchunk * 128:(cchunk + 1) * 128], identb[:]))
                    sc.add("scalar", lambda e: nc.scalar.copy(escT[:], pT[:, 0:256]))
                    sc.add("tensor", lambda e: nc.tensor.matmul(
                        pD[:, 0:128], escT[:, 0:128], crm0[:], start=True, stop=False, skip_group_check=True))
                    sc.add("tensor", lambda e: nc.tensor.matmul(
                        pD[:, 0:128], escT[:, 128:256], crm1[:], start=False, stop=True, skip_group_check=True))
                    sc.add("vector", lambda e, h=h: nc.vector.tensor_scalar(
                        attnb[:, h * 128:(h + 1) * 128], pD[:, 0:128], st2[:], None, op0=ALU.mult))
                    for jj in range(4):
                        sc.add("tensor", lambda e, lhs=lhs, i=i, jj=jj: nc.tensor.matmul(
                            pA[:, jj * 128:(jj + 1) * 128], lhs,
                            swkT[:, (i + jj) * 128:(i + jj + 1) * 128], start=True, stop=True,
                            skip_group_check=True))
                    sc.add("tensor", lambda e, lhs=lhs, i=i: nc.tensor.matmul(
                        pF[:], lhs, swkT[:, (i + 4) * 128:(i + 5) * 128], start=True, stop=True))
                    sc.add("vector", lambda e: nc.vector.reduce_max(st1[:], pA[:], axis=AX.X))
                    sc.add("vector", lambda e: nc.vector.reduce_max(st3[:], pF[:], axis=AX.X))
                    sc.add("vector", lambda e: nc.vector.tensor_max(st1[:], st1[:], st3[:]))
                    sc.add("vector", lambda e: nc.vector.tensor_scalar_mul(st1[:], st1[:], ISCALE))
                    sc.add("vector", lambda e, h=h: nc.vector.tensor_max(st1[:], st1[:], sinkb[:, h:h + 1]))
                    sc.add("vector", lambda e: nc.vector.tensor_scalar_mul(st1[:], st1[:], -1.0))
                    sc.add("scalar", lambda e: nc.scalar.activation(
                        esw[:], pA[:], AF.Exp, bias=st1[:], scale=ISCALE))
                    sc.add("scalar", lambda e: nc.scalar.activation(
                        eswd[:], pF[:], AF.Exp, bias=st1[:], scale=ISCALE))
                    sc.add("scalar", lambda e, h=h: nc.scalar.activation(
                        esk[:], sinkb[:, h:h + 1], AF.Exp, bias=st1[:], scale=1.0))
                    smv = smsk[:].rearrange("p (c x) -> p c x", c=5)
                    for jj in range(4):
                        sc.add("vector", lambda e, jj=jj: nc.vector.tensor_mul(
                            esw[:, jj * 128:(jj + 1) * 128], esw[:, jj * 128:(jj + 1) * 128],
                            smv[:, jj, :]))
                    sc.add("vector", lambda e: nc.vector.tensor_mul(eswd[:], eswd[:], smv[:, 4, :]))
                    for jj in range(4):
                        sc.add("tensor", lambda e, jj=jj: nc.tensor.transpose(
                            pT[:, jj * 128:(jj + 1) * 128], esw[:, jj * 128:(jj + 1) * 128], identb[:]))
                    sc.add("scalar", lambda e: nc.scalar.copy(eswT[:, 0:512], pT[:]))
                    sc.add("tensor", lambda e: nc.tensor.transpose(pT[:, 0:128], eswd[:], identb[:]))
                    sc.add("scalar", lambda e: nc.scalar.copy(eswT[:, 512:640], pT[:, 0:128]))
                    for jj in range(5):
                        sc.add("tensor", lambda e, jj=jj, i=i: nc.tensor.matmul(
                            pD[:], eswT[:, jj * 128:(jj + 1) * 128], swvext[:, i + jj, 0:129],
                            start=(jj == 0), stop=(jj == 4), skip_group_check=True))
                    sc.add("vector", lambda e: nc.vector.tensor_copy(st1[:], pD[:, 128:129]))
                    sc.add("vector", lambda e: nc.vector.tensor_add(st1[:], st1[:], esk[:]))
                    sc.add("vector", lambda e: nc.vector.reciprocal(st1[:], st1[:]))
                    sc.add("vector", lambda e, h=h: nc.vector.scalar_tensor_tensor(
                        attnb[:, h * 128:(h + 1) * 128], pD[:, 0:128], st1[:],
                        attnb[:, h * 128:(h + 1) * 128], op0=ALU.mult, op1=ALU.add))
            sc.add("vector", lambda e: nc.vector.tensor_copy(attnbb[:], attnb[:]))
            for c0 in range(0, 16, 4):
                for j in range(4):
                    dc = c0 + j
                    sc.add("tensor", lambda e, dc=dc, j=j: nc.tensor.transpose(
                        pT[:, j * 128:(j + 1) * 128], attnbb[:, dc * 128:(dc + 1) * 128], identb[:]))
                for j in range(4):
                    dc = c0 + j
                    sc.add("scalar", lambda e, dc=dc, j=j: nc.scalar.copy(
                        aT[:, dc, :], pT[:, j * 128:(j + 1) * 128]))
            for g in range(4):
                for cc in range(4):
                    sc.add("tensor", lambda e, g=g, cc=cc: nc.tensor.matmul(
                        pB[:], aT[:, g * 4 + cc, :], sgw[:, g * 4 + cc, :],
                        start=(cc == 0), stop=(cc == 3)))
                sc.add("scalar", lambda e, g=g: nc.scalar.copy(ogb[:, g * 512:(g + 1) * 512], pB[:]))
            for c0 in range(0, 16, 4):
                for j in range(4):
                    dc = c0 + j
                    sc.add("tensor", lambda e, dc=dc, j=j: nc.tensor.transpose(
                        pT[:, j * 128:(j + 1) * 128], ogb[:, dc * 128:(dc + 1) * 128], identb[:]))
                for j in range(4):
                    dc = c0 + j
                    sc.add("scalar", lambda e, dc=dc, j=j: nc.scalar.copy(
                        aT[:, dc, :], pT[:, j * 128:(j + 1) * 128]))
            sc.add("sync", lambda e, sl=sl: e.dma_start(out=sx[:], in_=x_in[sl, :]), dma=True)
            for b4 in range(4):
                for dc in range(16):
                    sc.add("tensor", lambda e, b4=b4, dc=dc: nc.tensor.matmul(
                        pB[:], aT[:, dc, :], sfw[:, dc, b4 * 512:(b4 + 1) * 512],
                        start=(dc == 0), stop=(dc == 15)))
                sc.add("vector", lambda e, b4=b4: nc.vector.tensor_add(
                    sx[:, b4 * 512:(b4 + 1) * 512], sx[:, b4 * 512:(b4 + 1) * 512], pB[:]))
            sc.add("gpsimd", lambda e, sl=sl: e.dma_start(out=o_x2[sl, :], in_=sx[:]), dma=True)

        for i in range(NT):
            sc.add("sync", lambda e, i=i: e.dma_start(out=attnbb[:], in_=xn_in[i * 128:(i + 1) * 128, :]), dma=True)
            for c0 in range(0, 16, 4):
                for j in range(4):
                    dc = c0 + j
                    sc.add("tensor", lambda e, dc=dc, j=j: nc.tensor.transpose(
                        pT[:, j * 128:(j + 1) * 128], attnbb[:, dc * 128:(dc + 1) * 128], identb[:]))
                for j in range(4):
                    dc = c0 + j
                    sc.add("scalar", lambda e, dc=dc, j=j, i=i: nc.scalar.copy(
                        xnT[:, dc, i * 128:(i + 1) * 128], pT[:, j * 128:(j + 1) * 128]))
        for os_ in range(16):
            for dc in range(16):
                sc.add("gpsimd", lambda e, os_=os_, dc=dc: e.dma_start(
                    out=sgate[:, dc, :], in_=wg_in[dc * 128:(dc + 1) * 128, os_ * 512:(os_ + 1) * 512]), dma=True)
                sc.add("gpsimd", lambda e, os_=os_, dc=dc: e.dma_start(
                    out=sup[:, dc, :], in_=wu_in[dc * 128:(dc + 1) * 128, os_ * 512:(os_ + 1) * 512]), dma=True)
            for cc in range(4):
                sc.add("gpsimd", lambda e, os_=os_, cc=cc: e.dma_start(
                    out=sdown[:, cc, :], in_=wd_in[os_ * 512 + cc * 128:os_ * 512 + (cc + 1) * 128, :]), dma=True)
            for i in range(NT):
                for dc in range(16):
                    sc.add("tensor", lambda e, i=i, dc=dc: nc.tensor.matmul(
                        pA[:], xnT[:, dc, i * 128:(i + 1) * 128], sgate[:, dc, :],
                        start=(dc == 0), stop=(dc == 15)))
                sc.add("scalar", lambda e: nc.scalar.activation(gb[:], pA[:], AF.Silu))
                for dc in range(16):
                    sc.add("tensor", lambda e, i=i, dc=dc: nc.tensor.matmul(
                        pB[:], xnT[:, dc, i * 128:(i + 1) * 128], sup[:, dc, :],
                        start=(dc == 0), stop=(dc == 15)))
                sc.add("vector", lambda e: nc.vector.tensor_mul(hb[:], pB[:], gb[:]))
                sc.add("scalar", lambda e: nc.scalar.activation(hab[:], hb[:], AF.Abs))
                hv = hab[:].rearrange("p (g s) -> p g s", s=4)
                sc.add("vector", lambda e: nc.vector.tensor_max(tscr[:, 0, :], hv[:, :, 0], hv[:, :, 1]))
                sc.add("vector", lambda e: nc.vector.tensor_max(tscr[:, 1, :], hv[:, :, 2], hv[:, :, 3]))
                sc.add("vector", lambda e: nc.vector.tensor_tensor(
                    tscr[:, 2, :], hv[:, :, 0], hv[:, :, 1], op=ALU.min))
                sc.add("vector", lambda e: nc.vector.tensor_tensor(
                    tscr[:, 3, :], hv[:, :, 2], hv[:, :, 3], op=ALU.min))
                sc.add("vector", lambda e: nc.vector.tensor_tensor(
                    tscr[:, 4, :], tscr[:, 0, :], tscr[:, 1, :], op=ALU.min))
                sc.add("vector", lambda e: nc.vector.tensor_max(tscr[:, 5, :], tscr[:, 2, :], tscr[:, 3, :]))
                sc.add("vector", lambda e: nc.vector.tensor_max(tscr[:, 4, :], tscr[:, 4, :], tscr[:, 5, :]))
                hbv = hb[:].rearrange("p (g s) -> p g s", s=4)
                hsv = hsp[:].rearrange("p (g s) -> p g s", s=4)
                for s4 in range(4):
                    sc.add("vector", lambda e, s4=s4: nc.vector.tensor_tensor(
                        tscr[:, 5, :], hv[:, :, s4], tscr[:, 4, :], op=ALU.is_ge))
                    sc.add("vector", lambda e, s4=s4: nc.vector.tensor_mul(
                        hsv[:, :, s4], hbv[:, :, s4], tscr[:, 5, :]))
                for j in range(4):
                    sc.add("tensor", lambda e, j=j: nc.tensor.transpose(
                        pT[:, j * 128:(j + 1) * 128], hsp[:, j * 128:(j + 1) * 128], identb[:]))
                sc.add("scalar", lambda e: nc.scalar.copy(hspT_[:], pT[:]))
                for b4 in range(4):
                    for cc in range(4):
                        sc.add("tensor", lambda e, b4=b4, cc=cc: nc.tensor.matmul(
                            pC[:], hspT[:, cc, :], sdown[:, cc, b4 * 512:(b4 + 1) * 512],
                            start=(cc == 0), stop=(cc == 3)))
                    sc.add("vector", lambda e: nc.vector.tensor_copy(obf[:], pC[:]))
                    sc.add("gpsimd", lambda e, i=i, b4=b4: e.dma_start(
                        out=o_x2[i * 128:(i + 1) * 128, b4 * 512:(b4 + 1) * 512], in_=obf[:],
                        accum_op=ALU.add), dma=True)
        sc.emit(nc, sem)
    return nc


def run_l1post(inp, x1, pre1):
    nc = build_l1post()
    x1f = np.asarray(x1).reshape(2, T, D)
    xs = _slices(x1f)
    xns = _slices(np.asarray(pre1["xn1"]).reshape(2, T, D))
    qs = _slices(np.asarray(pre1["q"]).reshape(2, T, D))
    qis = _slices(np.asarray(pre1["qi"]).reshape(2, T, 256))
    wgs = _slices(np.asarray(pre1["wgt"]).reshape(2, T, 4))
    swk = np.asarray(pre1["swk"]).reshape(2, T, 128)
    swv = np.asarray(pre1["swv"]).reshape(2, T, 128)
    in_maps = []
    for c in range(NCORE):
        b = c // 4
        r = c % 4
        pos0 = r * TPC
        kloc = np.zeros((TPC + 512, 128), dtype=BF)
        a0 = max(0, pos0 - 512)
        kloc[512 - (pos0 - a0):512 + TPC] = swk[b, a0:pos0 + TPC].astype(BF)
        vloc = np.zeros((TPC + 512, 128), dtype=BF)
        vloc[512 - (pos0 - a0):512 + TPC] = swv[b, a0:pos0 + TPC].astype(BF)
        msk = np.zeros((NT, 5, 128, 128), dtype=BF)
        qpos = pos0 + np.arange(128)
        for i in range(NT):
            qp = (pos0 + i * 128 + np.arange(128))[:, None]
            for j in range(5):
                kp = (pos0 - 512 + (i + j) * 128 + np.arange(128))[None, :]
                valid = (kp >= 0) & (kp <= qp) & (qp - kp < L1_SW)
                msk[i, j] = valid.astype(np.float32)
        in_maps.append({
            "x1": xs[c].astype(np.float32),
            "xn1": xns[c].astype(BF),
            "q": qs[c].astype(BF),
            "qi": qis[c].astype(BF),
            "wgt": wgs[c].astype(np.float32),
            "ckvT": np.ascontiguousarray(np.asarray(pre1["compkvT"][b], np.float32)),
            "ciT": np.ascontiguousarray(np.asarray(pre1["compiT"][b], np.float32)),
            "swkt": np.ascontiguousarray(kloc.T),
            "swv": np.ascontiguousarray(vloc.reshape(20, 128, 128)),
            "swmsk": msk,
            "sink": np.asarray(inp["L1_attn_sink_logit"], np.float32).reshape(1, NH),
            "knw": np.asarray(inp["L1_attn_k_norm_w"]),
            "groupw": np.asarray(inp["L1_attn_group_w"]),
            "finalw": np.asarray(inp["L1_attn_final_w"]),
            "wgate": np.asarray(inp["L1_ffn_w_gate"]),
            "wup": np.asarray(inp["L1_ffn_w_up"]),
            "wdown": np.asarray(inp["L1_ffn_w_down"]),
            "maskadd": _l1_valid_maskadd(pos0),
            "guard": _l1_guard(pos0),
            "haspick": _l1_haspick(pos0),
        })
    res = run_bass_kernel_spmd(nc, in_maps, core_ids=list(range(NCORE)))
    return _gather([res.results[c]["x2"] for c in range(NCORE)])


def _slices(full):
    out = []
    for c in range(NCORE):
        b = c // 4
        r = c % 4
        out.append(np.ascontiguousarray(full[b, r * TPC:(r + 1) * TPC]))
    return out


def _gather(slc):
    per_b = [np.concatenate(slc[b * 4:(b + 1) * 4], axis=0) for b in range(2)]
    return np.stack(per_b, axis=0)


def run_l0pre(inp):
    cos, sin = _rope_l0()
    nc = build_l0pre()
    xs = _slices(inp["L0_attn_hidden_states"])
    in_maps = []
    for c in range(NCORE):
        r = c % 4
        pos0 = r * TPC
        in_maps.append({
            "x": xs[c],
            "wq": np.asarray(inp["L0_attn_Wq"]),
            "wk": np.asarray(inp["L0_attn_Wk"]),
            "wv": np.asarray(inp["L0_attn_Wv"]),
            "cos": np.ascontiguousarray(cos[pos0:pos0 + TPC]),
            "sin": np.ascontiguousarray(sin[pos0:pos0 + TPC]),
        })
    res = run_bass_kernel_spmd(nc, in_maps, core_ids=list(range(NCORE)))
    outs = {}
    for name in ("xn", "q", "k", "v", "qn2", "kn2"):
        outs[name] = _gather([res.results[c][name] for c in range(NCORE)])
    return outs


def run_l0passA(pre):
    nc = build_l0passA()
    q = np.asarray(pre["q"]).reshape(2, T, NH, HD)
    k = np.asarray(pre["k"]).reshape(2, T, NH, HD)
    qn2 = np.asarray(pre["qn2"])
    kn2 = np.asarray(pre["kn2"])
    in_maps = []
    for c in range(NCORE):
        qt = np.empty((4, 128, T), dtype=BF)
        kt = np.empty((4, 128, T), dtype=BF)
        q2 = np.empty((4, 128, 64), dtype=np.float32)
        k2 = np.empty((4, 128, 64), dtype=np.float32)
        for j in range(4):
            bh = 4 * c + j
            b, h = bh // NH, bh % NH
            qt[j] = q[b, :, h, :].T
            kt[j] = k[b, :, h, :].T
            q2[j] = qn2[b, :, h].reshape(64, 128).T
            k2[j] = kn2[b, :, h].reshape(64, 128).T
        in_maps.append({"qt": qt, "kt": kt, "qn2": q2, "kn2": k2})
    res = run_bass_kernel_spmd(nc, in_maps, core_ids=list(range(NCORE)))
    cs = np.zeros((2, NH, T), dtype=np.float32)
    for c in range(NCORE):
        for j in range(4):
            bh = 4 * c + j
            b, h = bh // NH, bh % NH
            cs[b, h] = res.results[c]["colsum"][j].reshape(T)
    return cs


def kernel(**inp):
    pre = run_l0pre(inp)
    cs = run_l0passA(pre)
    attn = run_l0passB(pre, cs)
    x1 = run_l0post(inp, pre, attn)
    pre1 = run_l1pre(inp, x1)
    x2 = run_l1post(inp, x1, pre1)
    return np.ascontiguousarray(x2.astype(np.float32))


# version 03
# speedup vs baseline: 1.1448x; 1.1448x over previous
import math
from contextlib import ExitStack

import numpy as np
import ml_dtypes
import concourse.bass as bass
import concourse.mybir as mybir
from concourse.bass_utils import run_bass_kernel_spmd
from concourse.aot_env import allow_ici_all_to_all

F32 = mybir.dt.float32
BF16 = mybir.dt.bfloat16
U8 = mybir.dt.uint8
AF = mybir.ActivationFunctionType
ALU = mybir.AluOpType
AX = mybir.AxisListType
BF = ml_dtypes.bfloat16

B = 2
T = 8192
D = 2048
NH = 16
HD = 128
NCORE = 8
TPC = 2048
NT = 16
HEAVY = 819
ISCALE = 1.0 / math.sqrt(128.0)
NEG = -1.0e38
_BAND_DELTAS = [768, 896, 1024, 1152, 1280]
WORLD = [[0, 1, 2, 3, 4, 5, 6, 7]]

L1_RD = 64
L1_SW = 512
L1_M = 32
L1_TOPK = 32
L1_G = 4
L1_DG = 512
L1_NIH = 4
L1_CI = 64
NB = T // L1_M

ENGINES = ("sync", "scalar", "vector", "tensor", "gpsimd")
_ALL = "__ALL__"


class Sched2:

    def __init__(self):
        self.ops = []
        self.n_coll = 0

    def add(self, engine, fn, r=None, w=None, dma=False):
        if r is None and w is None:
            r = [_ALL]
            w = [_ALL]
        self.ops.append(dict(engine=engine, fn=fn, kind=("d" if dma else "c"),
                             r=tuple(r or ()), w=tuple(w or ()), coll=-1))

    def coll(self, fn, r, w):
        self.ops.append(dict(engine="gpsimd", fn=fn, kind="coll",
                             r=tuple(r), w=tuple(w), coll=self.n_coll))
        self.n_coll += 1

    def barrier(self):
        self.ops.append(dict(engine="gpsimd", fn=None, kind="barrier",
                             r=(_ALL,), w=(_ALL,), coll=-1))

    def emit(self, nc, es, barrier_tile):
        ops = self.ops
        ccount = {e: 0 for e in ENGINES}
        dcount = {e: 0 for e in ENGINES}
        for op in ops:
            e = op["engine"]
            if op["kind"] == "d":
                dcount[e] += 1
                op["cnt"] = dcount[e]
            elif op["kind"] == "coll":
                op["cnt"] = 1
            else:
                ccount[e] += 1
                op["cnt"] = ccount[e]

        def opkey(op):
            if op["kind"] == "coll":
                return ("coll", op["coll"])
            if op["kind"] == "d":
                return ("d", op["engine"])
            return ("c", op["engine"])

        last_write = {}
        readers = {}
        last_per_stream = {}
        last_barrier = None
        for op in ops:
            need = {}

            def dep(d):
                if d is None or d is op:
                    return
                k = opkey(d)
                if need.get(k, 0) < d["cnt"]:
                    need[k] = d["cnt"]

            if _ALL in op["r"]:
                for d in last_per_stream.values():
                    dep(d)
            else:
                dep(last_barrier)
                for t in op["r"]:
                    dep(last_write.get(t))
                for t in op["w"]:
                    dep(last_write.get(t))
                    for rd in readers.get(t, ()):
                        dep(rd)
            op["need"] = need
            if _ALL in op["w"]:
                last_write = {t: op for t in last_write}
                readers = {}
                last_barrier = op
            else:
                for t in op["r"]:
                    readers.setdefault(t, []).append(op)
                for t in op["w"]:
                    last_write[t] = op
                    readers[t] = []
            last_per_stream[opkey(op)] = op

        csem = {}
        dsem = {}
        for e in ENGINES:
            if ccount[e]:
                csem[e] = es.enter_context(nc.semaphore(f"c_{e}"))
            if dcount[e]:
                dsem[e] = es.enter_context(nc.semaphore(f"d_{e}"))
        collsems = [es.enter_context(nc.semaphore(f"cc{i}")) for i in range(self.n_coll)]

        streams = {e: [o for o in ops if o["engine"] == e] for e in ENGINES}
        with nc.Block() as block:
            for ename in ENGINES:
                if not streams[ename]:
                    continue

                def body(eng, ename=ename):
                    watermark = {}
                    for op in streams[ename]:
                        for k in sorted(op["need"], key=str):
                            cnt = op["need"][k]
                            if k[0] == "c":
                                sem, val = csem[k[1]], cnt
                            elif k[0] == "d":
                                sem, val = dsem[k[1]], 16 * cnt
                            else:
                                sem, val = collsems[k[1]], 1
                            if watermark.get(k, -1) >= val:
                                continue
                            eng.wait_ge(sem, val)
                            watermark[k] = val
                        if op["kind"] == "barrier":
                            nc.gpsimd.memset(barrier_tile[:], 0.0).then_inc(csem[ename], 1)
                        elif op["kind"] == "coll":
                            op["fn"](eng).then_inc(collsems[op["coll"]], 1)
                        elif op["kind"] == "d":
                            op["fn"](eng).then_inc(dsem[ename], 16)
                        else:
                            op["fn"](eng).then_inc(csem[ename], 1)
                    if ename == "gpsimd":
                        for e2 in ENGINES:
                            if e2 != "gpsimd" and ccount[e2]:
                                eng.wait_ge(csem[e2], ccount[e2])
                            if dcount[e2]:
                                eng.wait_ge(dsem[e2], 16 * dcount[e2])
                        for cs_ in collsems:
                            eng.wait_ge(cs_, 1)

                getattr(block, ename)(body)


def _rope_l0():
    inv = 1.0 / (10000.0 ** (np.arange(0, HD, 2, dtype=np.float64) / HD))
    pos = np.arange(T, dtype=np.float64)
    fr = pos[:, None] * inv[None, :]
    return np.cos(fr).astype(np.float32), np.sin(fr).astype(np.float32)


def _band_tiles():
    out = np.zeros((len(_BAND_DELTAS), 128, 512), dtype=np.float32)
    pi = np.arange(128)[:, None]
    j = np.arange(512)[None, :]
    for kk, dl in enumerate(_BAND_DELTAS):
        out[kk] = ((dl + pi - j) <= HEAVY).astype(np.float32)
    return out.astype(BF)


def bslices(full):
    out = []
    for c in range(NCORE):
        out.append(np.ascontiguousarray(np.concatenate(
            [full[0][1024 * c:1024 * (c + 1)], full[1][1024 * c:1024 * (c + 1)]], axis=0)))
    return out


def bgather(slc, dtype=np.float32):
    shp = slc[0].shape[1:]
    out = np.empty((B, T) + shp, dtype=dtype)
    for c in range(NCORE):
        out[0][1024 * c:1024 * (c + 1)] = slc[c][:1024]
        out[1][1024 * c:1024 * (c + 1)] = slc[c][1024:]
    return out


def _a2a(e, ins, outs):
    with allow_ici_all_to_all():
        return e.collective_compute("AllToAll", ALU.bypass, replica_groups=WORLD,
                                    ins=ins, outs=outs)


def _ag(e, ins, outs):
    return e.collective_compute("AllGather", ALU.bypass, replica_groups=WORLD,
                                ins=ins, outs=outs)


def build_fused(phase_limit="p1", dbg=False):
    nc = bass.Bass("TRN2", target_bir_lowering=False)

    x_in = nc.dram_tensor("x", [TPC, D], F32, kind="ExternalInput").ap()
    wq_in = nc.dram_tensor("wq", [D, D], F32, kind="ExternalInput").ap()
    wk_in = nc.dram_tensor("wk", [D, D], F32, kind="ExternalInput").ap()
    wv_in = nc.dram_tensor("wv", [D, D], F32, kind="ExternalInput").ap()
    wo_in = nc.dram_tensor("wo", [D, D], F32, kind="ExternalInput").ap()
    wfc_in = nc.dram_tensor("wfc", [4 * D, D], F32, kind="ExternalInput").ap()
    wpr_in = nc.dram_tensor("wproj", [D, 4 * D], F32, kind="ExternalInput").ap()
    cos_in = nc.dram_tensor("cos", [TPC, 64], F32, kind="ExternalInput").ap()
    sin_in = nc.dram_tensor("sin", [TPC, 64], F32, kind="ExternalInput").ap()
    band_in = nc.dram_tensor("bandbit", [len(_BAND_DELTAS), 128, 512], BF16, kind="ExternalInput").ap()
    o_x1 = nc.dram_tensor("x1", [TPC, D], F32, kind="ExternalOutput").ap()

    xn_scr = nc.dram_tensor("xn_scr", [TPC, D], BF16).ap()
    qa_in = nc.dram_tensor("qa_in", [8 * 512, 1024], BF16).ap()
    qa_out = nc.dram_tensor("qa_out", [8 * 512, 1024], BF16).ap()
    ka_in = nc.dram_tensor("ka_in", [8 * 512, 1024], BF16).ap()
    ka_out = nc.dram_tensor("ka_out", [8 * 512, 1024], BF16).ap()
    va_in = nc.dram_tensor("va_in", [8 * 1024, 512], BF16).ap()
    va_out = nc.dram_tensor("va_out", [8 * 1024, 512], BF16).ap()
    qm_in = nc.dram_tensor("qm_in", [32, 8], F32).ap()
    qm_out = nc.dram_tensor("qm_out", [32, 8], F32).ap()
    kn_in = nc.dram_tensor("kn_in", [32, 1], F32).ap()
    kn_out = nc.dram_tensor("kn_out", [32, 1], F32).ap()
    at_in = nc.dram_tensor("at_in", [T, 512], BF16).ap()
    at_out = nc.dram_tensor("at_out", [T, 512], BF16).ap()
    cs_loc = nc.dram_tensor("cs_loc", [4, 16, 512], F32).ap()
    hb_dram = nc.dram_tensor("hb_scr", [4, 32, 256], BF16).ap()
    h_scr = nc.dram_tensor("h_scr", [TPC, 4 * D], BF16).ap()
    t1t_scr = nc.dram_tensor("t1t", [16, 64, 128, 128], BF16).ap()
    t2t_scr = nc.dram_tensor("t2t", [64, 16, 128, 128], BF16).ap()
    x2t_scr = nc.dram_tensor("x2t", [64, 16, 128, 128], BF16).ap()

    dbg_outs = {}
    if dbg:
        for nm, shp, dt in [("d_xn", [TPC, D], BF16),
                            ("d_cs", [4, 16, 512], F32),
                            ("d_qm", [32, 8], F32),
                            ("d_kn", [32, 1], F32),
                            ("d_at", [T, 512], BF16)]:
            dbg_outs[nm] = nc.dram_tensor(nm, shp, dt, kind="ExternalOutput").ap()

    sc = Sched2()
    with ExitStack() as es:
        sb = lambda n, s, d: es.enter_context(nc.sbuf_tensor(n, s, d))
        ps = lambda n, s, d=F32: es.enter_context(nc.psum_tensor(n, s, d))

        AR0_ = sb("AR0", [128, 32768], BF16)
        AR1_ = sb("AR1", [128, 32768], BF16)
        ARF_ = sb("ARF", [128, 8192], F32)
        identb = sb("identb", [128, 128], BF16)
        identf = sb("identf", [128, 128], F32)
        c01 = sb("c01", [128, 128], BF16)
        btile = sb("btile", [1, 1], F32)
        epsb5 = sb("epsb5", [128, 1], F32)
        epsb6 = sb("epsb6", [128, 1], F32)
        sones = sb("sones", [1, 128], F32)
        onescol = sb("onescol", [128, 1], F32)
        ones16 = sb("ones16", [128, 16], BF16)
        st1 = sb("st1", [128, 1], F32)
        st2 = sb("st2", [128, 1], F32)
        st3 = sb("st3", [128, 1], F32)
        st4 = sb("st4", [128, 1], F32)
        nbuf = sb("nbuf", [128, NH], F32)
        kn2acc = sb("kn2acc", [128, 2 * NH], F32)
        qmaxT = sb("qmaxT", [16, 16], F32)
        ntrash = sb("ntrash", [128, 128], F32)
        scos = sb("scos", [128, 64], F32)
        ssin = sb("ssin", [128, 64], F32)
        xnb = sb("xnb", [128, D], BF16)
        wload = sb("wload", [128, D], BF16)
        outb = sb("outb", [128, D], BF16)
        tmp64 = sb("tmp64", [128, NH * 64], F32)
        tpose = sb("tpose", [128, 512], BF16)

        PS0 = ps("PS0", [128, 512])
        PS1 = ps("PS1", [128, 512])
        PS2 = ps("PS2", [128, 512])
        PS3 = ps("PS3", [128, 512])
        PSB0 = es.enter_context(nc.psum_tensor("PSB0", [128, 512], BF16))
        PSB1 = es.enter_context(nc.psum_tensor("PSB1", [128, 512], BF16))
        PSCS = ps("PSCS", [16, 512])

        xnT = AR0_[:].rearrange("p (c t) -> p c t", c=16)
        wt = AR1_[:].rearrange("p (c t) -> p c t", c=16)
        sx = ARF_[:, 0:2048]
        strash = ARF_[:, 2048:4096]
        qt = ARF_[:, 4096:6144]
        qrot = ARF_[:, 6144:8192]

        def PSB(k):
            return (PSB0, "PSB0") if k % 2 == 0 else (PSB1, "PSB1")

        sc.add("gpsimd", lambda e: nc.gpsimd.memset(identb[:], 0.0), r=[], w=["identb"])
        sc.add("gpsimd", lambda e: nc.gpsimd.affine_select(
            out=identb[:], in_=identb[:], compare_op=ALU.not_equal,
            fill=1.0, base=0, pattern=[[-1, 128]], channel_multiplier=1),
            r=["identb"], w=["identb"])
        sc.add("gpsimd", lambda e: nc.gpsimd.memset(identf[:], 0.0), r=[], w=["identf"])
        sc.add("gpsimd", lambda e: nc.gpsimd.affine_select(
            out=identf[:], in_=identf[:], compare_op=ALU.not_equal,
            fill=1.0, base=0, pattern=[[-1, 128]], channel_multiplier=1),
            r=["identf"], w=["identf"])
        sc.add("gpsimd", lambda e: nc.gpsimd.memset(c01[:], 1.0), r=[], w=["c01"])
        sc.add("gpsimd", lambda e: nc.gpsimd.affine_select(
            out=c01[:], in_=c01[:], compare_op=ALU.is_ge,
            fill=0.0, base=0, pattern=[[-1, 128]], channel_multiplier=1),
            r=["c01"], w=["c01"])
        sc.add("gpsimd", lambda e: nc.gpsimd.memset(epsb5[:], 1e-5), r=[], w=["epsb5"])
        sc.add("gpsimd", lambda e: nc.gpsimd.memset(epsb6[:], 1e-6), r=[], w=["epsb6"])
        sc.add("gpsimd", lambda e: nc.gpsimd.memset(sones[:], 1.0), r=[], w=["sones"])
        sc.add("gpsimd", lambda e: nc.gpsimd.memset(onescol[:], 1.0), r=[], w=["onescol"])
        sc.add("gpsimd", lambda e: nc.gpsimd.memset(ones16[:], 1.0), r=[], w=["ones16"])
        sc.add("vector", lambda e: nc.vector.memset(kn2acc[:], 0.0), r=[], w=["kn2acc"])

        for i in range(NT):
            sc.add("sync", lambda e, i=i: e.dma_start(out=sx[:], in_=x_in[i * 128:(i + 1) * 128, :]),
                   r=["sx_done"], w=["sx"], dma=True)
            sc.add("vector", lambda e: nc.vector.reduce_sum(st1[:], sx[:], axis=AX.X),
                   r=["sx"], w=["st1"])
            sc.add("scalar", lambda e: nc.scalar.activation(strash[:], sx[:], AF.Square, accum_out=st2[:]),
                   r=["sx"], w=["strash", "st2"])
            sc.add("vector", lambda e: nc.vector.tensor_scalar_mul(st1[:], st1[:], 1.0 / D),
                   r=["st1"], w=["st1"])
            sc.add("vector", lambda e: nc.vector.tensor_scalar_mul(st2[:], st2[:], 1.0 / D),
                   r=["st2"], w=["st2"])
            sc.add("vector", lambda e: nc.vector.tensor_mul(st3[:], st1[:], st1[:]),
                   r=["st1"], w=["st3"])
            sc.add("vector", lambda e: nc.vector.tensor_sub(st2[:], st2[:], st3[:]),
                   r=["st2", "st3"], w=["st2"])
            sc.add("scalar", lambda e: nc.scalar.activation(st3[:], st2[:], AF.Sqrt, bias=epsb5[:]),
                   r=["st2", "epsb5"], w=["st3"])
            sc.add("vector", lambda e: nc.vector.reciprocal(st4[:], st3[:]),
                   r=["st3"], w=["st4"])
            sc.add("vector", lambda e: nc.vector.scalar_tensor_tensor(
                st3[:], st1[:], -1.0, st4[:], op0=ALU.mult, op1=ALU.mult),
                r=["st1", "st4"], w=["st3"])
            sc.add("vector", lambda e: nc.vector.scalar_tensor_tensor(
                xnb[:], sx[:], st4[:], st3[:].broadcast_to([128, D]),
                op0=ALU.mult, op1=ALU.add),
                r=["sx", "st4", "st3"], w=["xnb", "sx_done"])
            sc.add("gpsimd", lambda e, i=i: e.dma_start(
                out=xn_scr[i * 128:(i + 1) * 128, :], in_=xnb[:]),
                r=["xnb"], w=["xn_scr"], dma=True)
            for c0 in range(0, 16, 4):
                pb, pbt = PSB(c0 // 4)
                for j in range(4):
                    c = c0 + j
                    sc.add("tensor", lambda e, c=c, j=j, pb=pb: nc.tensor.transpose(
                        pb[:, j * 128:(j + 1) * 128], xnb[:, c * 128:(c + 1) * 128], identb[:]),
                        r=["xnb", "identb"], w=[pbt])
                for j in range(4):
                    c = c0 + j
                    sc.add("scalar", lambda e, c=c, j=j, i=i, pb=pb: nc.scalar.copy(
                        xnT[:, c, i * 128:(i + 1) * 128], pb[:, j * 128:(j + 1) * 128]),
                        r=[pbt], w=[f"xnT:{i}"])

        def w_phase(w_ap, kind):
            wtok = f"wt_{kind}"
            for oc in range(16):
                sc.add("gpsimd", lambda e, oc=oc: e.dma_start(
                    out=wload[:], in_=w_ap[oc * 128:(oc + 1) * 128, :]),
                    r=["wload_done"], w=["wload"], dma=True)
                for c0 in range(0, 16, 4):
                    pb, pbt = PSB(c0 // 4)
                    for j in range(4):
                        hc = c0 + j
                        sc.add("tensor", lambda e, hc=hc, j=j, pb=pb: nc.tensor.transpose(
                            pb[:, j * 128:(j + 1) * 128], wload[:, hc * 128:(hc + 1) * 128], identb[:]),
                            r=["wload", "identb"],
                            w=[pbt] + (["wload_done"] if c0 == 12 else []))
                    for j in range(4):
                        hc = c0 + j
                        sc.add("scalar", lambda e, hc=hc, j=j, oc=oc, pb=pb: nc.scalar.copy(
                            wt[:, hc, oc * 128:(oc + 1) * 128], pb[:, j * 128:(j + 1) * 128]),
                            r=[pbt], w=[wtok])
            for i in range(NT):
                bt = i // 8
                tl = i % 8
                if kind in ("q", "k"):
                    sc.add("sync", lambda e, i=i: e.dma_start(out=scos[:], in_=cos_in[i * 128:(i + 1) * 128, :]),
                           r=["rope_done"], w=["scos"], dma=True)
                    sc.add("sync", lambda e, i=i: e.dma_start(out=ssin[:], in_=sin_in[i * 128:(i + 1) * 128, :]),
                           r=["rope_done"], w=["ssin"], dma=True)
                for b4 in range(4):
                    pm = PS0 if b4 % 2 == 0 else PS1
                    pmt = "PS0" if b4 % 2 == 0 else "PS1"
                    for hc in range(16):
                        sc.add("tensor", lambda e, i=i, b4=b4, hc=hc, pm=pm: nc.tensor.matmul(
                            pm[:], xnT[:, hc, i * 128:(i + 1) * 128],
                            wt[:, hc, b4 * 512:(b4 + 1) * 512],
                            start=(hc == 0), stop=(hc == 15)),
                            r=[f"xnT:{i}", wtok], w=[pmt])
                    sc.add("scalar", lambda e, b4=b4, pm=pm: nc.scalar.copy(
                        qt[:, b4 * 512:(b4 + 1) * 512], pm[:]),
                        r=[pmt], w=["qt"])
                if kind in ("q", "k"):
                    qv = qt.rearrange("p (h d2) -> p h d2", h=NH)
                    rv = qrot.rearrange("p (h d2) -> p h d2", h=NH)
                    tv = tmp64[:].rearrange("p (h d2) -> p h d2", h=NH)
                    cosb = scos[:].unsqueeze(1).broadcast_to([128, NH, 64])
                    sinb = ssin[:].unsqueeze(1).broadcast_to([128, NH, 64])
                    sc.add("vector", lambda e: nc.vector.tensor_mul(rv[:, :, 0:64], qv[:, :, 0:64], cosb),
                           r=["qt", "scos"], w=["qrot"])
                    sc.add("vector", lambda e: nc.vector.tensor_mul(tv, qv[:, :, 64:128], sinb),
                           r=["qt", "ssin"], w=["tmp64"])
                    sc.add("vector", lambda e: nc.vector.tensor_sub(rv[:, :, 0:64], rv[:, :, 0:64], tv),
                           r=["qrot", "tmp64"], w=["qrot"])
                    sc.add("vector", lambda e: nc.vector.tensor_mul(rv[:, :, 64:128], qv[:, :, 64:128], cosb),
                           r=["qt", "scos"], w=["qrot"])
                    sc.add("vector", lambda e: nc.vector.tensor_mul(tv, qv[:, :, 0:64], sinb),
                           r=["qt", "ssin"], w=["tmp64", "rope_done"])
                    sc.add("vector", lambda e: nc.vector.tensor_add(rv[:, :, 64:128], rv[:, :, 64:128], tv),
                           r=["qrot", "tmp64"], w=["qrot"])
                    for h in range(NH):
                        sc.add("scalar", lambda e, h=h: nc.scalar.activation(
                            ntrash[:], qrot[:, h * 128:(h + 1) * 128], AF.Square,
                            accum_out=nbuf[:, h:h + 1]),
                            r=["qrot"], w=["ntrash", "nbuf"])
                    if kind == "q":
                        sc.add("tensor", lambda e: nc.tensor.transpose(
                            PS2[0:16, 0:128], nbuf[:], identf[:]),
                            r=["nbuf", "identf"], w=["PS2"])
                        sc.add("scalar", lambda e: nc.scalar.copy(ntrash[0:16, 0:128], PS2[0:16, 0:128]),
                               r=["PS2"], w=["ntrash"])
                        sc.add("vector", lambda e, i=i: nc.vector.reduce_max(
                            qmaxT[:, i:i + 1], ntrash[0:16, 0:128], axis=AX.X),
                            r=["ntrash"], w=["qmaxT"])
                    else:
                        sc.add("vector", lambda e, bt=bt: nc.vector.tensor_max(
                            kn2acc[:, bt * 16:(bt + 1) * 16], kn2acc[:, bt * 16:(bt + 1) * 16], nbuf[:]),
                            r=["nbuf", "kn2acc"], w=["kn2acc"])
                    sc.add("vector", lambda e: nc.vector.tensor_copy(outb[:], qrot[:]),
                           r=["qrot"], w=["outb"])
                else:
                    sc.add("vector", lambda e: nc.vector.tensor_copy(outb[:], qt[:]),
                           r=["qt"], w=["outb"])
                if kind == "v":
                    for g in range(4):
                        sc.add("gpsimd", lambda e, i=i, bt=bt, tl=tl, g=g: e.dma_start(
                            out=va_in[(4 * bt + g) * 1024 + tl * 128:(4 * bt + g) * 1024 + (tl + 1) * 128,
                                      0:512],
                            in_=outb[:, g * 512:(g + 1) * 512]),
                            r=["outb"], w=["va_in"], dma=True)
                else:
                    dst = qa_in if kind == "q" else ka_in
                    dtok = "qa_in" if kind == "q" else "ka_in"
                    for c0 in range(0, 16, 4):
                        pb, pbt = PSB(c0 // 4)
                        for j in range(4):
                            hc = c0 + j
                            sc.add("tensor", lambda e, hc=hc, j=j, pb=pb: nc.tensor.transpose(
                                pb[:, j * 128:(j + 1) * 128], outb[:, hc * 128:(hc + 1) * 128], identb[:]),
                                r=["outb", "identb"], w=[pbt])
                        sc.add("scalar", lambda e, pb=pb: nc.scalar.copy(
                            tpose[:], pb[:]), r=[pbt], w=["tpose"])
                        for j in range(4):
                            hc = c0 + j
                            g, hj = hc // 4, hc % 4
                            row0 = (4 * bt + g) * 512 + hj * 128
                            sc.add("gpsimd", lambda e, row0=row0, j=j, tl=tl, dst=dst: e.dma_start(
                                out=dst[row0:row0 + 128, tl * 128:(tl + 1) * 128],
                                in_=tpose[:, j * 128:(j + 1) * 128]),
                                r=["tpose"], w=[dtok], dma=True)

        w_phase(wq_in, "q")
        w_phase(wk_in, "k")
        w_phase(wv_in, "v")

        for j8 in range(8):
            g, bj = j8 % 4, j8 // 4
            sc.add("gpsimd", lambda e, j8=j8, g=g, bj=bj: e.dma_start(
                out=qm_in[j8 * 4:(j8 + 1) * 4, :],
                in_=qmaxT[4 * g:4 * g + 4, bj * 8:(bj + 1) * 8]),
                r=["qmaxT"], w=["qm_in"], dma=True)
        sc.add("tensor", lambda e: nc.tensor.transpose(PS2[0:32, 0:128], kn2acc[:, 0:32], identf[:]),
               r=["kn2acc", "identf"], w=["PS2"])
        sc.add("scalar", lambda e: nc.scalar.copy(ntrash[0:32, 0:128], PS2[0:32, 0:128]),
               r=["PS2"], w=["ntrash"])
        sc.add("vector", lambda e: nc.vector.reduce_max(nbuf[0:32, 0:1], ntrash[0:32, 0:128], axis=AX.X),
               r=["ntrash"], w=["nbuf"])
        for j8 in range(8):
            g, bj = j8 % 4, j8 // 4
            sc.add("gpsimd", lambda e, j8=j8, g=g, bj=bj: e.dma_start(
                out=kn_in[j8 * 4:(j8 + 1) * 4, :],
                in_=nbuf[bj * 16 + 4 * g: bj * 16 + 4 * g + 4, 0:1]),
                r=["nbuf"], w=["kn_in"], dma=True)

        sc.coll(lambda e: _a2a(e, [qa_in.opt()], [qa_out.opt()]), r=["qa_in"], w=["qa_out"])
        sc.coll(lambda e: _a2a(e, [ka_in.opt()], [ka_out.opt()]), r=["ka_in"], w=["ka_out"])
        sc.coll(lambda e: _a2a(e, [va_in.opt()], [va_out.opt()]), r=["va_in"], w=["va_out"])
        sc.coll(lambda e: _a2a(e, [qm_in.opt()], [qm_out.opt()]), r=["qm_in"], w=["qm_out"])
        sc.coll(lambda e: _a2a(e, [kn_in.opt()], [kn_out.opt()]), r=["kn_in"], w=["kn_out"])

        if dbg:
            sc.add("gpsimd", lambda e: e.dma_start(out=dbg_outs["d_xn"], in_=xn_scr),
                   r=["xn_scr"], w=[], dma=True)
        if phase_limit == "p0":
            sc.add("gpsimd", lambda e: e.dma_start(out=o_x1[0:128, :], in_=sx[:]),
                   r=["sx"], w=["o_x1"], dma=True)
            sc.emit(nc, es, btile)
            return nc

        sc.barrier()

        ktb = AR1_[:, 0:8192]
        qtb = AR1_[:, 8192:16384]
        heavyb = AR1_[:, 16384:24576]
        estrip = AR0_[:, 0:8192]
        vext_ = AR0_[:, 8192:16640]
        vext = vext_.rearrange("p (c xx) -> p c xx", c=64)
        ediag = sb("ediag", [128, 128], BF16)
        sbig = sb("sbig", [128, 272], BF16)
        cmb = sb("cmb", [128, 512], BF16)
        ptsb = sb("ptsb", [128, 512], BF16)
        attnt = sb("attnt", [128, 128], BF16)
        dslots = sb("dslots", [128, 16], F32)
        scs = sb("scs", [16, 512], F32)
        qmrow = sb("qmrow", [1, 256], F32)
        knrow = sb("knrow", [1, 32], F32)
        kmax4 = sb("kmax4", [1, 4], F32)
        s11 = sb("s11", [1, 1], F32)
        negmb = st3
        dsum3 = sb("dsum3", [128, 1], F32)

        sc.add("sync", lambda e: e.dma_start(
            out=qmrow[:], in_=qm_out.rearrange("a b -> (a b)").unsqueeze(0)),
            r=["qm_out"], w=["qmrow"], dma=True)
        sc.add("sync", lambda e: e.dma_start(
            out=knrow[:], in_=kn_out.rearrange("a b -> (a b)").unsqueeze(0)),
            r=["kn_out"], w=["knrow"], dma=True)
        knv = knrow[:].rearrange("o (r j) -> o j r", r=8)
        sc.add("vector", lambda e: nc.vector.reduce_max(kmax4[:], knv, axis=AX.X),
               r=["knrow"], w=["kmax4"])
        sc.add("scalar", lambda e: nc.scalar.activation(kmax4[:], kmax4[:], AF.Sqrt),
               r=["kmax4"], w=["kmax4"])

        def mk_negm(jj, t, rtok="negmb"):
            idx = (4 * (t // 8) + jj) * 8 + (t % 8)
            sc.add("scalar", lambda e, idx=idx: nc.scalar.activation(
                s11[:], qmrow[0:1, idx:idx + 1], AF.Sqrt),
                r=["qmrow"], w=["s11"])
            sc.add("vector", lambda e, jj=jj: nc.vector.scalar_tensor_tensor(
                s11[:], s11[:], -ISCALE, kmax4[0:1, jj:jj + 1], op0=ALU.mult, op1=ALU.mult),
                r=["s11", "kmax4"], w=["s11"])
            sc.add("tensor", lambda e: nc.tensor.matmul(PS3[:, 0:1], sones[:], s11[:], start=True, stop=True),
                   r=["sones", "s11"], w=["PS3"])
            sc.add("scalar", lambda e: nc.scalar.copy(negmb[:], PS3[:, 0:1]),
                   r=["PS3"], w=[rtok])

        def load_pair(jj, dst, dtok, src, stok):
            for rho in range(8):
                sc.add("sync", lambda e, rho=rho, jj=jj, dst=dst, src=src: e.dma_start(
                    out=dst[:, rho * 1024:(rho + 1) * 1024],
                    in_=src[rho * 512 + jj * 128: rho * 512 + (jj + 1) * 128, :]),
                    r=[stok], w=[dtok], dma=True)

        for jj in range(4):
            load_pair(jj, ktb, "ktb", ka_out, "ka_out")
            load_pair(jj, qtb, "qtb", qa_out, "qa_out")
            first_cs = [True]
            for t in range(64):
                L = (t + 1) * 128
                nch = (L + 511) // 512
                dlim = L - 128
                mk_negm(jj, t)
                lhs = qtb[:, t * 128:(t + 1) * 128]
                for c in range(nch):
                    a = c * 512
                    w_ = min(512, L - a)
                    pq = PS0 if (c % 2 == 0) else PS1
                    pqt = "PS0" if (c % 2 == 0) else "PS1"
                    sc.add("tensor", lambda e, a=a, w_=w_, pq=pq, lhs=lhs: nc.tensor.matmul(
                        pq[:, 0:w_], lhs, ktb[:, a:a + w_], start=True, stop=True),
                        r=["qtb", "ktb"], w=[pqt])
                    if a + w_ <= dlim:
                        sc.add("scalar", lambda e, a=a, w_=w_, pq=pq, c=c: nc.scalar.activation(
                            estrip[:, a:a + w_], pq[:, 0:w_], AF.Exp,
                            bias=negmb[:], scale=ISCALE, accum_out=dslots[:, c:c + 1]),
                            r=[pqt, "negmb"], w=["estrip", "dslots"])
                    else:
                        wm = dlim - a
                        if wm > 0:
                            sc.add("scalar", lambda e, a=a, wm=wm, pq=pq, c=c: nc.scalar.activation(
                                estrip[:, a:a + wm], pq[:, 0:wm], AF.Exp,
                                bias=negmb[:], scale=ISCALE, accum_out=dslots[:, c:c + 1]),
                                r=[pqt, "negmb"], w=["estrip", "dslots"])
                        else:
                            sc.add("vector", lambda e, c=c: nc.vector.memset(dslots[:, c:c + 1], 0.0),
                                   r=[], w=["dslots"])
                        sc.add("scalar", lambda e, wm=wm, w_=w_, pq=pq: nc.scalar.activation(
                            ediag[:], pq[:, wm:w_], AF.Exp, bias=negmb[:], scale=ISCALE),
                            r=[pqt, "negmb"], w=["ediag"])
                        sc.add("vector", lambda e, a=a, wm=wm, w_=w_: nc.vector.tensor_mul(
                            estrip[:, a + wm:a + w_], ediag[:], c01[:]),
                            r=["ediag", "c01"], w=["estrip"])
                        sc.add("vector", lambda e, a=a, wm=wm, w_=w_: nc.vector.reduce_sum(
                            dsum3[:], estrip[:, a + wm:a + w_], axis=AX.X),
                            r=["estrip"], w=["dsum3"])
                sc.add("vector", lambda e, nch=nch: nc.vector.reduce_sum(
                    st1[:], dslots[:, 0:nch], axis=AX.X), r=["dslots"], w=["st1"])
                sc.add("vector", lambda e: nc.vector.tensor_add(st1[:], st1[:], dsum3[:]),
                       r=["st1", "dsum3"], w=["st1"])
                sc.add("vector", lambda e: nc.vector.reciprocal(st2[:], st1[:]),
                       r=["st1"], w=["st2"])
                sbv = sbig[:].rearrange("p (c xx) -> p c xx", c=16)
                sc.add("vector", lambda e: nc.vector.memset(sbig[:], 0.0), r=[], w=["sbig"])
                sc.add("vector", lambda e, sbv=sbv: nc.vector.tensor_scalar(
                    sbv[:, :, 0:1], ones16[:].unsqueeze(2), st2[:], None, op0=ALU.mult),
                    r=["ones16", "st2"], w=["sbig"])
                for c in range(nch):
                    a = c * 512
                    w_ = min(512, L - a)
                    sc.add("tensor", lambda e, c=c, a=a, w_=w_, f=first_cs[0],
                           s=(t == 63 and c == nch - 1): nc.tensor.matmul(
                        PSCS[:, 0:w_], sbig[:, 16 * c:16 * c + 16], estrip[:, a:a + w_],
                        start=f, stop=s, skip_group_check=True),
                        r=["sbig", "estrip"], w=["PSCS"])
                    first_cs[0] = False
            sc.add("scalar", lambda e: nc.scalar.copy(scs[:], PSCS[:]), r=["PSCS"], w=["scs"])
            sc.add("gpsimd", lambda e, jj=jj: e.dma_start(out=cs_loc[jj], in_=scs[:]),
                   r=["scs"], w=["cs_loc"], dma=True)

        selh = sb("selh", [128, 4], F32)
        selhT = sb("selhT", [4, 128], F32)
        scsb = sb("scsb", [128, 256], F32)
        slo = sb("slo", [128, 1], F32)
        shi = sb("shi", [128, 1], F32)
        smid = sb("smid", [128, 1], F32)
        cpart = sb("cpart", [128, 1], F32)
        scnt = sb("scnt", [4, 1], F32)
        srep = sb("srep", [128, 1], F32)
        condm = sb("condm", [128, 1], U8)
        getrash = sb("getrash", [128, 256], BF16)
        hbbis = sb("hbbis", [128, 256], BF16)
        sc.add("gpsimd", lambda e: nc.gpsimd.memset(selh[:], 1.0), r=[], w=["selh"])
        sc.add("gpsimd", lambda e: nc.gpsimd.affine_select(
            out=selh[:], in_=selh[:], compare_op=ALU.is_ge,
            fill=0.0, base=0, pattern=[[-32, 4]], channel_multiplier=1),
            r=["selh"], w=["selh"])
        sc.add("gpsimd", lambda e: nc.gpsimd.affine_select(
            out=selh[:], in_=selh[:], compare_op=ALU.is_ge,
            fill=0.0, base=31, pattern=[[32, 4]], channel_multiplier=-1),
            r=["selh"], w=["selh"])
        sc.add("gpsimd", lambda e: nc.gpsimd.memset(selhT[:], 1.0), r=[], w=["selhT"])
        sc.add("gpsimd", lambda e: nc.gpsimd.affine_select(
            out=selhT[:], in_=selhT[:], compare_op=ALU.is_ge,
            fill=0.0, base=0, pattern=[[1, 128]], channel_multiplier=-32),
            r=["selhT"], w=["selhT"])
        sc.add("gpsimd", lambda e: nc.gpsimd.affine_select(
            out=selhT[:], in_=selhT[:], compare_op=ALU.is_ge,
            fill=0.0, base=31, pattern=[[-1, 128]], channel_multiplier=32),
            r=["selhT"], w=["selhT"])
        for jj in range(4):
            sc.add("sync", lambda e, jj=jj: e.dma_start(
                out=scsb[jj * 32:(jj + 1) * 32, :],
                in_=cs_loc[jj].rearrange("p (a xx) -> (p a) xx", a=2)),
                r=["cs_loc"], w=["scsb"], dma=True)
        sc.add("vector", lambda e: nc.vector.memset(slo[:], 0.0), r=[], w=["slo"])
        sc.add("vector", lambda e: nc.vector.memset(shi[:], 8200.0), r=[], w=["shi"])
        for _ in range(45):
            sc.add("vector", lambda e: nc.vector.tensor_add(smid[:], slo[:], shi[:]),
                   r=["slo", "shi"], w=["smid"])
            sc.add("vector", lambda e: nc.vector.tensor_scalar_mul(smid[:], smid[:], 0.5),
                   r=["smid"], w=["smid"])
            sc.add("vector", lambda e: nc.vector.tensor_scalar(
                getrash[:], scsb[:], smid[:], 0.0, op0=ALU.is_ge, op1=ALU.add,
                accum_out=cpart[:]),
                r=["scsb", "smid"], w=["getrash", "cpart"])
            sc.add("tensor", lambda e: nc.tensor.matmul(PSCS[0:4, 0:1], selh[:], cpart[:], start=True, stop=True),
                   r=["selh", "cpart"], w=["PSCS"])
            sc.add("scalar", lambda e: nc.scalar.copy(scnt[:], PSCS[0:4, 0:1]), r=["PSCS"], w=["scnt"])
            sc.add("tensor", lambda e: nc.tensor.matmul(PS3[:, 0:1], selhT[:], scnt[:], start=True, stop=True),
                   r=["selhT", "scnt"], w=["PS3"])
            sc.add("scalar", lambda e: nc.scalar.copy(srep[:], PS3[:, 0:1]), r=["PS3"], w=["srep"])
            sc.add("vector", lambda e: nc.vector.tensor_scalar(
                condm[:], srep[:], 818.5, None, op0=ALU.is_ge), r=["srep"], w=["condm"])
            sc.add("vector", lambda e: nc.vector.select(slo[:], condm[:], smid[:], slo[:]),
                   r=["condm", "smid", "slo"], w=["slo"])
            sc.add("vector", lambda e: nc.vector.select(shi[:], condm[:], shi[:], smid[:]),
                   r=["condm", "smid", "shi"], w=["shi"])
        sc.add("vector", lambda e: nc.vector.tensor_scalar(
            hbbis[:], scsb[:], slo[:], None, op0=ALU.is_ge), r=["scsb", "slo"], w=["hbbis"])
        sc.add("gpsimd", lambda e: e.dma_start(
            out=hb_dram.rearrange("h s j -> (h s) j"), in_=hbbis[:]),
            r=["hbbis"], w=["hb_dram"], dma=True)

        if dbg:
            sc.add("gpsimd", lambda e: e.dma_start(out=dbg_outs["d_cs"], in_=cs_loc),
                   r=["cs_loc"], w=[], dma=True)
            sc.add("gpsimd", lambda e: e.dma_start(out=dbg_outs["d_qm"], in_=qm_out),
                   r=["qm_out"], w=[], dma=True)
            sc.add("gpsimd", lambda e: e.dma_start(out=dbg_outs["d_kn"], in_=kn_out),
                   r=["kn_out"], w=[], dma=True)
        if phase_limit == "pa":
            sc.add("gpsimd", lambda e: e.dma_start(out=o_x1[0:128, :], in_=sx[:]),
                   r=["sx"], w=["o_x1"], dma=True)
            sc.emit(nc, es, btile)
            return nc

        sband = sb("sband", [128, len(_BAND_DELTAS) * 512], BF16)
        for bi in range(len(_BAND_DELTAS)):
            sc.add("sync", lambda e, bi=bi: e.dma_start(
                out=sband[:, bi * 512:(bi + 1) * 512], in_=band_in[bi]),
                r=[], w=["sband"], dma=True)

        for jj in range(4):
            load_pair(jj, ktb, "ktb", ka_out, "ka_out")
            load_pair(jj, qtb, "qtb", qa_out, "qa_out")
            sc.add("sync", lambda e, jj=jj: e.dma_start(
                out=heavyb,
                in_=hb_dram[jj].rearrange("s j -> (s j)").unsqueeze(0).broadcast_to([128, T])),
                r=["hb_dram"], w=["heavyb"], dma=True)
            for rho in range(8):
                sc.add("sync", lambda e, rho=rho, jj=jj: e.dma_start(
                    out=vext[:, rho * 8:(rho + 1) * 8, 0:128],
                    in_=va_out[rho * 1024:(rho + 1) * 1024,
                               jj * 128:(jj + 1) * 128].rearrange("(c p) d2 -> p c d2", p=128)),
                    r=["va_out"], w=["vext"], dma=True)
            sc.add("vector", lambda e: nc.vector.memset(vext[:, :, 128:129], 1.0),
                   r=[], w=["vext"])

            for t in range(64):
                L = (t + 1) * 128
                nch = (L + 511) // 512
                nkc = L // 128
                dlim = L - 128
                mk_negm(jj, t)
                lhs = qtb[:, t * 128:(t + 1) * 128]
                for c in range(nch):
                    a = c * 512
                    w_ = min(512, L - a)
                    pq = PS0 if (c % 2 == 0) else PS1
                    pqt = "PS0" if (c % 2 == 0) else "PS1"
                    sc.add("tensor", lambda e, a=a, w_=w_, pq=pq, lhs=lhs: nc.tensor.matmul(
                        pq[:, 0:w_], lhs, ktb[:, a:a + w_], start=True, stop=True),
                        r=["qtb", "ktb"], w=[pqt])
                    if a + w_ <= dlim:
                        sc.add("scalar", lambda e, a=a, w_=w_, pq=pq: nc.scalar.activation(
                            estrip[:, a:a + w_], pq[:, 0:w_], AF.Exp,
                            bias=negmb[:], scale=ISCALE),
                            r=[pqt, "negmb"], w=["estrip"])
                    else:
                        wm = dlim - a
                        if wm > 0:
                            sc.add("scalar", lambda e, a=a, wm=wm, pq=pq: nc.scalar.activation(
                                estrip[:, a:a + wm], pq[:, 0:wm], AF.Exp,
                                bias=negmb[:], scale=ISCALE),
                                r=[pqt, "negmb"], w=["estrip"])
                        sc.add("scalar", lambda e, wm=wm, w_=w_, pq=pq: nc.scalar.activation(
                            ediag[:], pq[:, wm:w_], AF.Exp, bias=negmb[:], scale=ISCALE),
                            r=[pqt, "negmb"], w=["ediag"])
                        sc.add("vector", lambda e, a=a, wm=wm, w_=w_: nc.vector.tensor_mul(
                            estrip[:, a + wm:a + w_], ediag[:], c01[:]),
                            r=["ediag", "c01"], w=["estrip"])
                    dt_ = 128 * t - a
                    if dt_ >= 1408:
                        sc.add("vector", lambda e, a=a, w_=w_: nc.vector.tensor_mul(
                            estrip[:, a:a + w_], estrip[:, a:a + w_], heavyb[:, a:a + w_]),
                            r=["estrip", "heavyb"], w=["estrip"])
                    elif dt_ in _BAND_DELTAS:
                        bi = _BAND_DELTAS.index(dt_)
                        sc.add("vector", lambda e, a=a, w_=w_, bi=bi: nc.vector.tensor_max(
                            cmb[:, 0:w_], heavyb[:, a:a + w_], sband[:, bi * 512:bi * 512 + w_]),
                            r=["heavyb", "sband"], w=["cmb"])
                        sc.add("vector", lambda e, a=a, w_=w_: nc.vector.tensor_mul(
                            estrip[:, a:a + w_], estrip[:, a:a + w_], cmb[:, 0:w_]),
                            r=["estrip", "cmb"], w=["estrip"])
                for k0 in range(0, nkc, 4):
                    n4 = min(4, nkc - k0)
                    pb, pbt = PSB(k0 // 4)
                    for j in range(n4):
                        kc = k0 + j
                        sc.add("tensor", lambda e, kc=kc, j=j, pb=pb: nc.tensor.transpose(
                            pb[:, j * 128:(j + 1) * 128], estrip[:, kc * 128:(kc + 1) * 128], identb[:]),
                            r=["estrip", "identb"], w=[pbt])
                    sc.add("scalar", lambda e, n4=n4, pb=pb: nc.scalar.copy(
                        ptsb[:, 0:n4 * 128], pb[:, 0:n4 * 128]),
                        r=[pbt], w=["ptsb"])
                    for j in range(n4):
                        kc = k0 + j
                        sc.add("tensor", lambda e, kc=kc, j=j, s=(kc == nkc - 1), f=(kc == 0):
                               nc.tensor.matmul(
                            PS2[:, 0:129], ptsb[:, j * 128:(j + 1) * 128], vext[:, kc, 0:129],
                            start=f, stop=s, skip_group_check=True),
                            r=["ptsb", "vext"], w=["PS2"])
                sc.add("vector", lambda e: nc.vector.tensor_copy(st1[:], PS2[:, 128:129]),
                       r=["PS2"], w=["st1"])
                sc.add("vector", lambda e: nc.vector.reciprocal(st2[:], st1[:]),
                       r=["st1"], w=["st2"])
                sc.add("vector", lambda e: nc.vector.tensor_scalar(
                    attnt[:], PS2[:, 0:128], st2[:], None, op0=ALU.mult),
                    r=["PS2", "st2"], w=["attnt"])
                sc.add("gpsimd", lambda e, jj=jj, t=t: e.dma_start(
                    out=at_in[t * 128:(t + 1) * 128, jj * 128:(jj + 1) * 128], in_=attnt[:]),
                    r=["attnt"], w=["at_in"], dma=True)

        sc.coll(lambda e: _a2a(e, [at_in.opt()], [at_out.opt()]), r=["at_in"], w=["at_out"])

        if dbg:
            sc.add("gpsimd", lambda e: e.dma_start(out=dbg_outs["d_at"], in_=at_out),
                   r=["at_out"], w=[], dma=True)
        if phase_limit == "pb":
            sc.add("gpsimd", lambda e: e.dma_start(out=o_x1[0:128, :], in_=sx[:]),
                   r=["sx"], w=["o_x1"], dma=True)
            sc.emit(nc, es, btile)
            return nc

        sc.barrier()

        wtbuf = AR0_[:].rearrange("p (c t) -> p c t", c=16)
        rhs2 = AR0_[:].rearrange("p (c t) -> p c t", c=64)
        x2rawA = AR1_[:, 0:8192]
        x2rawB = AR1_[:, 8192:16384]
        rhsbA = AR1_[:, 0:8192].rearrange("p (c t) -> p c t", c=16)
        rhsbB = AR1_[:, 8192:16384].rearrange("p (c t) -> p c t", c=16)
        x2tbA = AR1_[:, 0:8192].rearrange("p (c t) -> p c t", c=64)
        x2tbB = AR1_[:, 8192:16384].rearrange("p (c t) -> p c t", c=64)
        hb2 = AR1_[:, 16384:24576]
        g1 = AR1_[:, 24576:32768]
        wcL = ARF_[:]
        ab = xnb
        abT = AR1_[:, 0:2048].rearrange("p (c t) -> p c t", c=16)
        sxf = ARF_[:, 0:2048]
        tb = tpose
        asum = sb("asum", [128, 1], F32)
        sg = sb("sg", [1, 1], F32)
        gb1 = sb("gb1", [128, 1], F32)
        gb2 = sb("gb2", [128, 1], F32)
        thrb = sb("thrb", [128, 1], F32)
        negthr = sb("negthr", [128, 1], F32)
        obf = ARF_[:, 2048:2560]
        g3 = hb2

        for oc in range(16):
            sc.add("gpsimd", lambda e, oc=oc: e.dma_start(
                out=ab[:], in_=wo_in[oc * 128:(oc + 1) * 128, :]),
                r=["ab_done"], w=["ab"], dma=True)
            for c0 in range(0, 16, 4):
                pb, pbt = PSB(c0 // 4)
                for j in range(4):
                    hc = c0 + j
                    sc.add("tensor", lambda e, hc=hc, j=j, pb=pb: nc.tensor.transpose(
                        pb[:, j * 128:(j + 1) * 128], ab[:, hc * 128:(hc + 1) * 128], identb[:]),
                        r=["ab", "identb"],
                        w=[pbt] + (["ab_done"] if c0 == 12 else []))
                for j in range(4):
                    hc = c0 + j
                    sc.add("scalar", lambda e, hc=hc, j=j, oc=oc, pb=pb: nc.scalar.copy(
                        wtbuf[:, hc, oc * 128:(oc + 1) * 128], pb[:, j * 128:(j + 1) * 128]),
                        r=[pbt], w=["wtbuf"])
        for i in range(NT):
            bt, tl = i // 8, i % 8
            for g in range(4):
                row0 = (4 * bt + g) * 1024 + tl * 128
                sc.add("sync", lambda e, row0=row0, g=g: e.dma_start(
                    out=ab[:, g * 512:(g + 1) * 512], in_=at_out[row0:row0 + 128, 0:512]),
                    r=["at_out", "ab_done"], w=["ab"], dma=True)
            sc.add("sync", lambda e, i=i: e.dma_start(out=sxf[:], in_=x_in[i * 128:(i + 1) * 128, :]),
                   r=["sxf_done"], w=["sxf"], dma=True)
            for c0 in range(0, 16, 4):
                pb, pbt = PSB(c0 // 4)
                for j in range(4):
                    hc = c0 + j
                    sc.add("tensor", lambda e, hc=hc, j=j, pb=pb: nc.tensor.transpose(
                        pb[:, j * 128:(j + 1) * 128], ab[:, hc * 128:(hc + 1) * 128], identb[:]),
                        r=["ab", "identb"],
                        w=[pbt] + (["ab_done"] if c0 == 12 else []))
                for j in range(4):
                    hc = c0 + j
                    sc.add("scalar", lambda e, hc=hc, j=j, pb=pb: nc.scalar.copy(
                        abT[:, hc, :], pb[:, j * 128:(j + 1) * 128]),
                        r=[pbt], w=["abT"])
            for b4 in range(4):
                pm = PS0 if b4 % 2 == 0 else PS1
                pmt = "PS0" if b4 % 2 == 0 else "PS1"
                for hc in range(16):
                    sc.add("tensor", lambda e, b4=b4, hc=hc, pm=pm: nc.tensor.matmul(
                        pm[:], abT[:, hc, :], wtbuf[:, hc, b4 * 512:(b4 + 1) * 512],
                        start=(hc == 0), stop=(hc == 15)),
                        r=["abT", "wtbuf"], w=[pmt])
                sc.add("vector", lambda e, b4=b4, pm=pm: nc.vector.tensor_add(
                    sxf[:, b4 * 512:(b4 + 1) * 512], sxf[:, b4 * 512:(b4 + 1) * 512], pm[:]),
                    r=[pmt, "sxf"], w=["sxf"])
            sc.add("gpsimd", lambda e, i=i: e.dma_start(out=o_x1[i * 128:(i + 1) * 128, :], in_=sxf[:]),
                   r=["sxf"], w=["o_x1", "sxf_done"], dma=True)

        sc.barrier()

        sc.add("vector", lambda e: nc.vector.memset(asum[:], 0.0), r=[], w=["asum"])
        for c in range(64):
            sc.add("sync", lambda e, c=c: e.dma_start(out=wcL[:, 0:D], in_=wfc_in[c * 128:(c + 1) * 128, :]),
                   r=["wcL_done"], w=["wcL"], dma=True)
            sc.add("vector", lambda e: nc.vector.reduce_sum(st1[:], wcL[:, 0:D], axis=AX.X, apply_absolute_value=True),
                   r=["wcL"], w=["st1", "wcL_done"])
            sc.add("vector", lambda e: nc.vector.tensor_add(asum[:], asum[:], st1[:]),
                   r=["asum", "st1"], w=["asum"])
        sc.add("tensor", lambda e: nc.tensor.matmul(PS3[0:1, 0:1], asum[:], onescol[:], start=True, stop=True),
               r=["asum", "onescol"], w=["PS3"])
        sc.add("scalar", lambda e: nc.scalar.copy(sg[:], PS3[0:1, 0:1]), r=["PS3"], w=["sg"])
        sc.add("vector", lambda e: nc.vector.tensor_scalar_mul(sg[:], sg[:], 1.0 / (4 * D * D)),
               r=["sg"], w=["sg"])
        sc.add("tensor", lambda e: nc.tensor.matmul(PS3[:, 0:1], sones[:], sg[:], start=True, stop=True),
               r=["sones", "sg"], w=["PS3"])
        sc.add("scalar", lambda e: nc.scalar.copy(gb1[:], PS3[:, 0:1]), r=["PS3"], w=["gb1"])
        sc.add("vector", lambda e: nc.vector.tensor_scalar(
            thrb[:], gb1[:], 0.5, 0.5e-6, op0=ALU.mult, op1=ALU.add), r=["gb1"], w=["thrb"])
        sc.add("vector", lambda e: nc.vector.tensor_scalar_mul(negthr[:], thrb[:], -1.0),
               r=["thrb"], w=["negthr"])
        for c in range(64):
            sc.add("sync", lambda e, c=c: e.dma_start(out=wcL[:, 0:D], in_=wfc_in[c * 128:(c + 1) * 128, :]),
                   r=["wcL_done"], w=["wcL"], dma=True)
            sc.add("vector", lambda e: nc.vector.tensor_scalar(
                g1[:, 0:D], wcL[:, 0:D], thrb[:], None, op0=ALU.is_ge),
                r=["wcL", "thrb"], w=["g1"])
            sc.add("vector", lambda e: nc.vector.tensor_scalar(
                g3[:, 0:D], wcL[:, 0:D], negthr[:], None, op0=ALU.is_le),
                r=["wcL", "negthr"], w=["hb2", "wcL_done"])
            sc.add("vector", lambda e: nc.vector.tensor_sub(g1[:, 0:D], g1[:, 0:D], g3[:, 0:D]),
                   r=["g1", "hb2"], w=["g1"])
            for c0 in range(0, 16, 4):
                pb, pbt = PSB(c0 // 4)
                for j in range(4):
                    dc = c0 + j
                    sc.add("tensor", lambda e, dc=dc, j=j, pb=pb: nc.tensor.transpose(
                        pb[:, j * 128:(j + 1) * 128], g1[:, dc * 128:(dc + 1) * 128], identb[:]),
                        r=["g1", "identb"], w=[pbt])
                sc.add("scalar", lambda e, pb=pb: nc.scalar.copy(tb[:], pb[:]), r=[pbt], w=["tb"])
                for j in range(4):
                    dc = c0 + j
                    sc.add("gpsimd", lambda e, dc=dc, j=j, c=c: e.dma_start(
                        out=t1t_scr[dc, c], in_=tb[:, j * 128:(j + 1) * 128]),
                        r=["tb"], w=["t1t"], dma=True)

        sc.barrier()

        for i in range(NT):
            sc.add("sync", lambda e, i=i: e.dma_start(out=ab[:], in_=xn_scr[i * 128:(i + 1) * 128, :]),
                   r=["ab_done"], w=["ab"], dma=True)
            sc.add("vector", lambda e: nc.vector.reduce_sum(st1[:], ab[:], axis=AX.X),
                   r=["ab"], w=["st1"])
            sc.add("scalar", lambda e: nc.scalar.activation(wcL[:, 0:D], ab[:], AF.Square, accum_out=st2[:]),
                   r=["ab"], w=["wcL", "st2"])
            sc.add("vector", lambda e: nc.vector.tensor_scalar_mul(st1[:], st1[:], 1.0 / D),
                   r=["st1"], w=["st1"])
            sc.add("vector", lambda e: nc.vector.tensor_scalar_mul(st2[:], st2[:], 1.0 / D),
                   r=["st2"], w=["st2"])
            sc.add("vector", lambda e: nc.vector.tensor_mul(st3[:], st1[:], st1[:]),
                   r=["st1"], w=["st3"])
            sc.add("vector", lambda e: nc.vector.tensor_sub(st2[:], st2[:], st3[:]),
                   r=["st2", "st3"], w=["st2"])
            sc.add("scalar", lambda e: nc.scalar.activation(st3[:], st2[:], AF.Sqrt, bias=epsb6[:]),
                   r=["st2", "epsb6"], w=["st3"])
            sc.add("vector", lambda e: nc.vector.reciprocal(st4[:], st3[:]),
                   r=["st3"], w=["st4"])
            sc.add("vector", lambda e: nc.vector.scalar_tensor_tensor(
                st3[:], st1[:], -1.0, st4[:], op0=ALU.mult, op1=ALU.mult),
                r=["st1", "st4"], w=["st3"])
            sc.add("vector", lambda e: nc.vector.scalar_tensor_tensor(
                wcL[:, 0:D], ab[:], st4[:], st3[:].broadcast_to([128, D]), op0=ALU.mult, op1=ALU.add),
                r=["ab", "st4", "st3"], w=["wcL"])
            sc.add("vector", lambda e: nc.vector.reduce_max(st1[:], wcL[:, 0:D], axis=AX.X, apply_absolute_value=True),
                   r=["wcL"], w=["st1"])
            sc.add("vector", lambda e: nc.vector.tensor_scalar_mul(st1[:], st1[:], (128.0 - 1e-6) / 128.0),
                   r=["st1"], w=["st1"])
            sc.add("vector", lambda e: nc.vector.tensor_scalar_mul(st2[:], st1[:], -1.0),
                   r=["st1"], w=["st2"])
            sc.add("vector", lambda e: nc.vector.tensor_scalar(
                ab[:], wcL[:, 0:D], st1[:], st2[:], op0=ALU.min, op1=ALU.max),
                r=["wcL", "st1", "st2"], w=["ab"])
            for c0 in range(0, 16, 4):
                pb, pbt = PSB(c0 // 4)
                for j in range(4):
                    dc = c0 + j
                    sc.add("tensor", lambda e, dc=dc, j=j, pb=pb: nc.tensor.transpose(
                        pb[:, j * 128:(j + 1) * 128], ab[:, dc * 128:(dc + 1) * 128], identb[:]),
                        r=["ab", "identb"],
                        w=[pbt] + (["ab_done"] if c0 == 12 else []))
                for j in range(4):
                    dc = c0 + j
                    sc.add("scalar", lambda e, dc=dc, j=j, i=i, pb=pb: nc.scalar.copy(
                        wtbuf[:, dc, i * 128:(i + 1) * 128], pb[:, j * 128:(j + 1) * 128]),
                        r=[pbt], w=["wtbuf"])

        sc.barrier()

        for os_ in range(16):
            rb = rhsbA if os_ % 2 == 0 else rhsbB
            rt = f"rhsb{os_ % 2}"
            for dc in range(16):
                sc.add("sync", lambda e, os_=os_, dc=dc, rb=rb: e.dma_start(
                    out=rb[:, dc, :].rearrange("p (a o) -> p a o", a=4),
                    in_=t1t_scr[dc, os_ * 4:os_ * 4 + 4].rearrange("a p o -> p a o")),
                    r=["t1t", rt + "_done"], w=[rt], dma=True)
            for i in range(NT):
                pm = PS0 if i % 2 == 0 else PS1
                pmt = "PS0" if i % 2 == 0 else "PS1"
                for dc in range(16):
                    sc.add("tensor", lambda e, i=i, dc=dc, rb=rb, pm=pm: nc.tensor.matmul(
                        pm[:], wtbuf[:, dc, i * 128:(i + 1) * 128], rb[:, dc, :],
                        start=(dc == 0), stop=(dc == 15)),
                        r=["wtbuf", rt], w=[pmt] + ([rt + "_done"] if i == NT - 1 and dc == 15 else []))
                sc.add("scalar", lambda e, pm=pm: nc.scalar.activation(
                    tb[:], pm[:], AF.Gelu, scale=gb1[:]),
                    r=[pmt, "gb1"], w=["tb"])
                sc.add("gpsimd", lambda e, i=i, os_=os_: e.dma_start(
                    out=h_scr[i * 128:(i + 1) * 128, os_ * 512:(os_ + 1) * 512], in_=tb[:]),
                    r=["tb"], w=["h_scr"], dma=True)

        sc.barrier()

        sc.add("vector", lambda e: nc.vector.memset(asum[:], 0.0), r=[], w=["asum"])
        for c in range(16):
            sc.add("sync", lambda e, c=c: e.dma_start(out=wcL[:], in_=wpr_in[c * 128:(c + 1) * 128, :]),
                   r=["wcL_done"], w=["wcL"], dma=True)
            sc.add("vector", lambda e: nc.vector.reduce_sum(st1[:], wcL[:], axis=AX.X, apply_absolute_value=True),
                   r=["wcL"], w=["st1", "wcL_done"])
            sc.add("vector", lambda e: nc.vector.tensor_add(asum[:], asum[:], st1[:]),
                   r=["asum", "st1"], w=["asum"])
        sc.add("tensor", lambda e: nc.tensor.matmul(PS3[0:1, 0:1], asum[:], onescol[:], start=True, stop=True),
               r=["asum", "onescol"], w=["PS3"])
        sc.add("scalar", lambda e: nc.scalar.copy(sg[:], PS3[0:1, 0:1]), r=["PS3"], w=["sg"])
        sc.add("vector", lambda e: nc.vector.tensor_scalar_mul(sg[:], sg[:], 1.0 / (4 * D * D)),
               r=["sg"], w=["sg"])
        sc.add("tensor", lambda e: nc.tensor.matmul(PS3[:, 0:1], sones[:], sg[:], start=True, stop=True),
               r=["sones", "sg"], w=["PS3"])
        sc.add("scalar", lambda e: nc.scalar.copy(gb2[:], PS3[:, 0:1]), r=["PS3"], w=["gb2"])
        sc.add("vector", lambda e: nc.vector.tensor_scalar(
            thrb[:], gb2[:], 0.5, 0.5e-6, op0=ALU.mult, op1=ALU.add), r=["gb2"], w=["thrb"])
        sc.add("vector", lambda e: nc.vector.tensor_scalar_mul(negthr[:], thrb[:], -1.0),
               r=["thrb"], w=["negthr"])
        for c in range(16):
            sc.add("sync", lambda e, c=c: e.dma_start(out=wcL[:], in_=wpr_in[c * 128:(c + 1) * 128, :]),
                   r=["wcL_done"], w=["wcL"], dma=True)
            sc.add("vector", lambda e: nc.vector.tensor_scalar(
                g1[:], wcL[:], thrb[:], None, op0=ALU.is_ge),
                r=["wcL", "thrb"], w=["g1"])
            sc.add("vector", lambda e: nc.vector.tensor_scalar(
                g3[:], wcL[:], negthr[:], None, op0=ALU.is_le),
                r=["wcL", "negthr"], w=["hb2", "wcL_done"])
            sc.add("vector", lambda e: nc.vector.tensor_sub(g1[:], g1[:], g3[:]),
                   r=["g1", "hb2"], w=["g1"])
            for c0 in range(0, 64, 4):
                pb, pbt = PSB(c0 // 4)
                for j in range(4):
                    dc = c0 + j
                    sc.add("tensor", lambda e, dc=dc, j=j, pb=pb: nc.tensor.transpose(
                        pb[:, j * 128:(j + 1) * 128], g1[:, dc * 128:(dc + 1) * 128], identb[:]),
                        r=["g1", "identb"], w=[pbt])
                sc.add("scalar", lambda e, pb=pb: nc.scalar.copy(tb[:], pb[:]), r=[pbt], w=["tb"])
                for j in range(4):
                    dc = c0 + j
                    sc.add("gpsimd", lambda e, dc=dc, j=j, c=c: e.dma_start(
                        out=t2t_scr[dc, c], in_=tb[:, j * 128:(j + 1) * 128]),
                        r=["tb"], w=["t2t"], dma=True)

        sc.barrier()

        for i in range(NT):
            sc.add("sync", lambda e, i=i: e.dma_start(out=hb2[:], in_=h_scr[i * 128:(i + 1) * 128, :]),
                   r=["hb2_done"], w=["hb2"], dma=True)
            sc.add("vector", lambda e: nc.vector.reduce_sum(st1[:], hb2[:], axis=AX.X),
                   r=["hb2"], w=["st1"])
            sc.add("scalar", lambda e: nc.scalar.activation(g1[:], hb2[:], AF.Square, accum_out=st2[:]),
                   r=["hb2"], w=["g1", "st2"])
            sc.add("vector", lambda e: nc.vector.tensor_scalar_mul(st1[:], st1[:], 1.0 / (4 * D)),
                   r=["st1"], w=["st1"])
            sc.add("vector", lambda e: nc.vector.tensor_scalar_mul(st2[:], st2[:], 1.0 / (4 * D)),
                   r=["st2"], w=["st2"])
            sc.add("vector", lambda e: nc.vector.tensor_mul(st3[:], st1[:], st1[:]),
                   r=["st1"], w=["st3"])
            sc.add("vector", lambda e: nc.vector.tensor_sub(st2[:], st2[:], st3[:]),
                   r=["st2", "st3"], w=["st2"])
            sc.add("scalar", lambda e: nc.scalar.activation(st3[:], st2[:], AF.Sqrt, bias=epsb6[:]),
                   r=["st2", "epsb6"], w=["st3"])
            sc.add("vector", lambda e: nc.vector.reciprocal(st4[:], st3[:]),
                   r=["st3"], w=["st4"])
            sc.add("vector", lambda e: nc.vector.scalar_tensor_tensor(
                st3[:], st1[:], -1.0, st4[:], op0=ALU.mult, op1=ALU.mult),
                r=["st1", "st4"], w=["st3"])
            sc.add("vector", lambda e: nc.vector.scalar_tensor_tensor(
                wcL[:], hb2[:], st4[:], st3[:].broadcast_to([128, 4 * D]), op0=ALU.mult, op1=ALU.add),
                r=["hb2", "st4", "st3"], w=["wcL"])
            sc.add("vector", lambda e: nc.vector.reduce_max(st1[:], wcL[:], axis=AX.X, apply_absolute_value=True),
                   r=["wcL"], w=["st1"])
            sc.add("vector", lambda e: nc.vector.tensor_scalar_mul(st1[:], st1[:], (128.0 - 1e-6) / 128.0),
                   r=["st1"], w=["st1"])
            sc.add("vector", lambda e: nc.vector.tensor_scalar_mul(st2[:], st1[:], -1.0),
                   r=["st1"], w=["st2"])
            sc.add("vector", lambda e: nc.vector.tensor_scalar(
                hb2[:], wcL[:], st1[:], st2[:], op0=ALU.min, op1=ALU.max),
                r=["wcL", "st1", "st2"], w=["hb2"])
            for c0 in range(0, 64, 4):
                pb, pbt = PSB(c0 // 4)
                for j in range(4):
                    dc = c0 + j
                    sc.add("tensor", lambda e, dc=dc, j=j, pb=pb: nc.tensor.transpose(
                        pb[:, j * 128:(j + 1) * 128], hb2[:, dc * 128:(dc + 1) * 128], identb[:]),
                        r=["hb2", "identb"],
                        w=[pbt] + (["hb2_done"] if c0 == 60 else []))
                sc.add("scalar", lambda e, pb=pb: nc.scalar.copy(tb[:], pb[:]), r=[pbt], w=["tb"])
                for j in range(4):
                    dc = c0 + j
                    sc.add("gpsimd", lambda e, dc=dc, j=j, i=i: e.dma_start(
                        out=x2t_scr[dc, i], in_=tb[:, j * 128:(j + 1) * 128]),
                        r=["tb"], w=["x2t"], dma=True)

        sc.barrier()

        for os_ in range(4):
            for dc in range(64):
                sc.add("sync", lambda e, os_=os_, dc=dc: e.dma_start(
                    out=rhs2[:, dc, :].rearrange("p (a o) -> p a o", a=4),
                    in_=t2t_scr[dc, os_ * 4:os_ * 4 + 4].rearrange("a p o -> p a o")),
                    r=["t2t", "rhs2_done"], w=["rhs2"], dma=True)
            for i in range(NT):
                xb = x2tbA if i % 2 == 0 else x2tbB
                xraw = x2rawA if i % 2 == 0 else x2rawB
                xt = f"x2tb{i % 2}"
                sc.add("sync", lambda e, i=i, xraw=xraw: e.dma_start(
                    out=xraw.rearrange("p (a o) -> p a o", a=64),
                    in_=x2t_scr[:, i].rearrange("a p o -> p a o")),
                    r=["x2t", xt + "_done"], w=[xt], dma=True)
                pm = PS0 if i % 2 == 0 else PS1
                pmt = "PS0" if i % 2 == 0 else "PS1"
                for dc in range(64):
                    sc.add("tensor", lambda e, i=i, dc=dc, xb=xb, pm=pm: nc.tensor.matmul(
                        pm[:], xb[:, dc, :], rhs2[:, dc, :],
                        start=(dc == 0), stop=(dc == 63)),
                        r=[xt, "rhs2"],
                        w=[pmt] + ([xt + "_done"] if dc == 63 else [])
                          + (["rhs2_done"] if i == NT - 1 and dc == 63 else []))
                sc.add("vector", lambda e, pm=pm: nc.vector.tensor_scalar(
                    obf[:], pm[:], gb2[:], None, op0=ALU.mult),
                    r=[pmt, "gb2"], w=["obf"])
                sc.add("gpsimd", lambda e, i=i, os_=os_: e.dma_start(
                    out=o_x1[i * 128:(i + 1) * 128, os_ * 512:(os_ + 1) * 512], in_=obf[:],
                    accum_op=ALU.add),
                    r=["obf", "o_x1"], w=["o_x1"], dma=True)

        sc.emit(nc, es, btile)
    return nc


def run_fused(inp, phase_limit="p1", dbg=False):
    cos, sin = _rope_l0()
    band = np.ascontiguousarray(_band_tiles())
    nc = build_fused(phase_limit=phase_limit, dbg=dbg)
    xs = bslices(np.asarray(inp["L0_attn_hidden_states"]))
    in_maps = []
    for c in range(NCORE):
        rows = np.arange(1024 * c, 1024 * (c + 1))
        cosc = np.ascontiguousarray(np.concatenate([cos[rows], cos[rows]], axis=0))
        sinc = np.ascontiguousarray(np.concatenate([sin[rows], sin[rows]], axis=0))
        in_maps.append({
            "x": xs[c],
            "wq": np.asarray(inp["L0_attn_Wq"]),
            "wk": np.asarray(inp["L0_attn_Wk"]),
            "wv": np.asarray(inp["L0_attn_Wv"]),
            "wo": np.asarray(inp["L0_attn_Wo"]),
            "wfc": np.asarray(inp["L0_ffn_w_fc"]),
            "wproj": np.asarray(inp["L0_ffn_w_proj"]),
            "cos": cosc,
            "sin": sinc,
            "bandbit": band,
        })
    res = run_bass_kernel_spmd(nc, in_maps, core_ids=list(range(NCORE)))
    return res


ENGINES = ("sync", "scalar", "vector", "tensor", "gpsimd")


class Sched:

    def __init__(self):
        self.steps = {e: [] for e in ENGINES}
        self.total = 0

    def add(self, engine, fn, dma=False):
        inc = 16 if dma else 1
        self.steps[engine].append((self.total, fn, inc))
        self.total += inc

    def emit(self, nc, sem):
        with nc.Block() as block:
            for ename in ENGINES:
                steps = self.steps[ename]
                if not steps and ename != "gpsimd":
                    continue

                def body(eng, steps=steps, ename=ename):
                    last_wait = -1
                    for (w, fn, inc) in steps:
                        if w > last_wait:
                            eng.wait_ge(sem, w)
                        fn(eng).then_inc(sem, inc)
                        last_wait = w
                    if ename == "gpsimd":
                        eng.wait_ge(sem, self.total)

                getattr(block, ename)(body)


def _mk_consts(nc, sc, ident_bf, ident_f32=None, causal01=None, upper01=None):
    sc.add("gpsimd", lambda g: nc.gpsimd.memset(ident_bf[:], 0.0))
    sc.add("gpsimd", lambda g: nc.gpsimd.affine_select(
        out=ident_bf[:], in_=ident_bf[:], compare_op=ALU.not_equal,
        fill=1.0, base=0, pattern=[[-1, 128]], channel_multiplier=1))
    if ident_f32 is not None:
        sc.add("gpsimd", lambda g: nc.gpsimd.memset(ident_f32[:], 0.0))
        sc.add("gpsimd", lambda g: nc.gpsimd.affine_select(
            out=ident_f32[:], in_=ident_f32[:], compare_op=ALU.not_equal,
            fill=1.0, base=0, pattern=[[-1, 128]], channel_multiplier=1))
    if causal01 is not None:
        sc.add("gpsimd", lambda g: nc.gpsimd.memset(causal01[:], 1.0))
        sc.add("gpsimd", lambda g: nc.gpsimd.affine_select(
            out=causal01[:], in_=causal01[:], compare_op=ALU.is_ge,
            fill=0.0, base=0, pattern=[[-1, 128]], channel_multiplier=1))
    if upper01 is not None:
        sc.add("gpsimd", lambda g: nc.gpsimd.memset(upper01[:], 1.0))
        sc.add("gpsimd", lambda g: nc.gpsimd.affine_select(
            out=upper01[:], in_=upper01[:], compare_op=ALU.is_lt,
            fill=0.0, base=0, pattern=[[-1, 128]], channel_multiplier=1))


_IDENT = [None]


def _rope_l1():
    inv = 1.0 / (10000.0 ** (np.arange(0, L1_RD, 2, dtype=np.float64) / L1_RD))
    pos = np.arange(T, dtype=np.float64)
    fr = pos[:, None] * inv[None, :]
    return np.cos(fr).astype(np.float32), np.sin(fr).astype(np.float32)




def _l1_valid_maskadd(pos0):
    t = (pos0 + np.arange(TPC))[:, None]
    n = np.arange(NB)[None, :]
    valid = (n * L1_M + (L1_M - 1)) < t
    return np.where(valid, 0.0, NEG).astype(np.float32)


def _l1_guard(pos0):
    t = pos0 + np.arange(TPC)
    return (t < L1_M + 0.5).astype(np.float32)[:, None]


def _l1_haspick(pos0):
    t = pos0 + np.arange(TPC)
    return (t >= L1_M).astype(np.float32)[:, None]


def build_l1pre():
    nc = bass.Bass("TRN2", target_bir_lowering=False)
    x_in = nc.dram_tensor("x1", [TPC, D], F32, kind="ExternalInput").ap()
    wqc_in = nc.dram_tensor("wqc", [D, 1024], F32, kind="ExternalInput").ap()
    wqup_in = nc.dram_tensor("wqup", [1024, D], F32, kind="ExternalInput").ap()
    wkvb_in = nc.dram_tensor("wkvb", [D, 128], F32, kind="ExternalInput").ap()
    wzb_in = nc.dram_tensor("wzb", [D, 128], F32, kind="ExternalInput").ap()
    biasb_in = nc.dram_tensor("biasb", [32, 128], F32, kind="ExternalInput").ap()
    wk_in = nc.dram_tensor("wk", [D, 128], F32, kind="ExternalInput").ap()
    wv_in = nc.dram_tensor("wv", [D, 128], F32, kind="ExternalInput").ap()
    wikv_in = nc.dram_tensor("wikv", [D, 64], F32, kind="ExternalInput").ap()
    wiz_in = nc.dram_tensor("wiz", [D, 64], F32, kind="ExternalInput").ap()
    biasi_in = nc.dram_tensor("biasi", [32, 64], F32, kind="ExternalInput").ap()
    wdq_in = nc.dram_tensor("wdq", [D, 64], F32, kind="ExternalInput").ap()
    wiuq_in = nc.dram_tensor("wiuq", [64, 256], F32, kind="ExternalInput").ap()
    ww_in = nc.dram_tensor("ww", [D, 4], F32, kind="ExternalInput").ap()
    qnw_in = nc.dram_tensor("qnw", [128], F32, kind="ExternalInput").ap()
    knw_in = nc.dram_tensor("knw", [128], F32, kind="ExternalInput").ap()
    cos_in = nc.dram_tensor("cos", [TPC, 32], F32, kind="ExternalInput").ap()
    sin_in = nc.dram_tensor("sin", [TPC, 32], F32, kind="ExternalInput").ap()
    o_xn1 = nc.dram_tensor("xn1", [TPC, D], BF16, kind="ExternalOutput").ap()
    o_q = nc.dram_tensor("q", [TPC, D], BF16, kind="ExternalOutput").ap()
    o_swk = nc.dram_tensor("swk", [TPC, 128], BF16, kind="ExternalOutput").ap()
    o_swv = nc.dram_tensor("swv", [TPC, 128], BF16, kind="ExternalOutput").ap()
    o_ckv = nc.dram_tensor("compkvT", [128, 64], F32, kind="ExternalOutput").ap()
    o_ci = nc.dram_tensor("compiT", [64, 64], F32, kind="ExternalOutput").ap()
    o_qi = nc.dram_tensor("qi", [TPC, 256], BF16, kind="ExternalOutput").ap()
    o_wgt = nc.dram_tensor("wgt", [TPC, 4], F32, kind="ExternalOutput").ap()

    sc = Sched()
    with ExitStack() as es:
        sb = lambda n, s, d: es.enter_context(nc.sbuf_tensor(n, s, d))
        ps = lambda n, s, d=F32: es.enter_context(nc.psum_tensor(n, s, d))
        identb = sb("identb", [128, 128], BF16)
        identf = sb("identf", [128, 128], F32)
        swqc_ = sb("swqc", [128, 16 * 1024], BF16)
        swqup_ = sb("swqup", [128, 8 * D], BF16)
        swkvb_ = sb("swkvb", [128, 16 * 128], BF16)
        swzb_ = sb("swzb", [128, 16 * 128], BF16)
        swk_ = sb("swkw", [128, 16 * 128], BF16)
        swv_ = sb("swvw", [128, 16 * 128], BF16)
        swikv_ = sb("swikv", [128, 16 * 64], BF16)
        swiz_ = sb("swiz", [128, 16 * 64], BF16)
        swdq_ = sb("swdq", [128, 16 * 64], BF16)
        swiuq = sb("swiuq", [64, 256], BF16)
        sww_ = sb("sww", [128, 16 * 4], BF16)
        biasb = sb("biasbt", [128, 128], F32)
        biasi = sb("biasit", [128, 64], F32)
        qnwb = sb("qnwb", [128, 128], F32)
        knwb = sb("knwb", [128, 128], F32)
        scos = sb("scos", [128, 32], F32)
        ssin = sb("ssin", [128, 32], F32)
        sx = sb("sx", [128, D], F32)
        xn1b = sb("xn1b", [128, D], BF16)
        xnTt_ = sb("xnTt", [128, 16 * 128], BF16)
        hqb = sb("hqb", [128, 1024], BF16)
        hqT_ = sb("hqT", [128, 8 * 128], BF16)
        qf = sb("qf", [128, D], F32)
        qrot = sb("qrot", [128, D], F32)
        tmp32 = sb("tmp32", [128, NH * 32], F32)
        nbuf = sb("nbuf", [128, NH], F32)
        rstd16 = sb("rstd16", [128, NH], F32)
        ntrash = sb("ntrash", [128, 128], F32)
        outb = sb("outb", [128, D], BF16)
        k0 = sb("k0", [128, 128], F32)
        k1 = sb("k1", [128, 128], F32)
        kb16 = sb("kb16", [128, 128], BF16)
        st1 = sb("st1", [128, 1], F32)
        st2 = sb("st2", [128, 1], F32)
        st3 = sb("st3", [128, 1], F32)
        st4 = sb("st4", [128, 1], F32)
        eps5 = sb("eps5", [128, 1], F32)
        eps6 = sb("eps6", [128, 1], F32)
        zf = sb("zf", [128, 128], F32)
        cf = sb("cf", [128, 128], F32)
        zT = sb("zT", [128, 128], F32)
        cT = sb("cT", [128, 128], F32)
        zm = sb("zm", [128, 4], F32)
        gs = sb("gs", [128, 4], F32)
        ckvT = sb("ckvT", [128, 64], F32)
        ciT = sb("ciT", [64, 64], F32)
        dqb = sb("dqb", [128, 64], BF16)
        dqT = sb("dqT", [64, 128], BF16)
        qib = sb("qib", [128, 256], BF16)
        wgtb = sb("wgtb", [128, 4], F32)
        pmm = ps("pmm", [128, 512])
        pc = ps("pc", [128, 128])
        pz = ps("pz", [128, 128])
        pT = es.enter_context(nc.psum_tensor("pT", [128, 512], BF16))
        pT2 = ps("pT2", [128, 256])
        pqi = ps("pqi", [128, 256])
        sem = es.enter_context(nc.semaphore("sem"))
        _IDENT[0] = identb
        swqc = swqc_[:].rearrange("p (c t) -> p c t", c=16)
        swqup = swqup_[:].rearrange("p (c t) -> p c t", c=8)
        swkvb = swkvb_[:].rearrange("p (c t) -> p c t", c=16)
        swzb = swzb_[:].rearrange("p (c t) -> p c t", c=16)
        swkw = swk_[:].rearrange("p (c t) -> p c t", c=16)
        swvw = swv_[:].rearrange("p (c t) -> p c t", c=16)
        swikv = swikv_[:].rearrange("p (c t) -> p c t", c=16)
        swiz = swiz_[:].rearrange("p (c t) -> p c t", c=16)
        swdq = swdq_[:].rearrange("p (c t) -> p c t", c=16)
        sww = sww_[:].rearrange("p (c t) -> p c t", c=16)
        xnTt = xnTt_[:].rearrange("p (c t) -> p c t", c=16)
        hqT = hqT_[:].rearrange("p (c t) -> p c t", c=8)
        _mk_consts(nc, sc, identb, ident_f32=identf)
        sc.add("gpsimd", lambda e: nc.gpsimd.memset(eps5[:], 1e-5))
        sc.add("gpsimd", lambda e: nc.gpsimd.memset(eps6[:], 1e-6))
        for dc in range(16):
            sc.add("gpsimd", lambda e, dc=dc: e.dma_start(out=swqc[:, dc, :], in_=wqc_in[dc * 128:(dc + 1) * 128, :]), dma=True)
            sc.add("gpsimd", lambda e, dc=dc: e.dma_start(out=swkvb[:, dc, :], in_=wkvb_in[dc * 128:(dc + 1) * 128, :]), dma=True)
            sc.add("gpsimd", lambda e, dc=dc: e.dma_start(out=swzb[:, dc, :], in_=wzb_in[dc * 128:(dc + 1) * 128, :]), dma=True)
            sc.add("gpsimd", lambda e, dc=dc: e.dma_start(out=swkw[:, dc, :], in_=wk_in[dc * 128:(dc + 1) * 128, :]), dma=True)
            sc.add("gpsimd", lambda e, dc=dc: e.dma_start(out=swvw[:, dc, :], in_=wv_in[dc * 128:(dc + 1) * 128, :]), dma=True)
            sc.add("gpsimd", lambda e, dc=dc: e.dma_start(out=swikv[:, dc, :], in_=wikv_in[dc * 128:(dc + 1) * 128, :]), dma=True)
            sc.add("gpsimd", lambda e, dc=dc: e.dma_start(out=swiz[:, dc, :], in_=wiz_in[dc * 128:(dc + 1) * 128, :]), dma=True)
            sc.add("gpsimd", lambda e, dc=dc: e.dma_start(out=swdq[:, dc, :], in_=wdq_in[dc * 128:(dc + 1) * 128, :]), dma=True)
            sc.add("gpsimd", lambda e, dc=dc: e.dma_start(out=sww[:, dc, :], in_=ww_in[dc * 128:(dc + 1) * 128, :]), dma=True)
        for dc in range(8):
            sc.add("gpsimd", lambda e, dc=dc: e.dma_start(out=swqup[:, dc, :], in_=wqup_in[dc * 128:(dc + 1) * 128, :]), dma=True)
        sc.add("gpsimd", lambda e: e.dma_start(out=swiuq[:], in_=wiuq_in), dma=True)
        for r in range(4):
            sc.add("sync", lambda e, r=r: e.dma_start(out=biasb[r * 32:(r + 1) * 32, :], in_=biasb_in), dma=True)
            sc.add("sync", lambda e, r=r: e.dma_start(out=biasi[r * 32:(r + 1) * 32, :], in_=biasi_in), dma=True)
        sc.add("sync", lambda e: e.dma_start(out=qnwb[:], in_=qnw_in.unsqueeze(0).broadcast_to([128, 128])), dma=True)
        sc.add("sync", lambda e: e.dma_start(out=knwb[:], in_=knw_in.unsqueeze(0).broadcast_to([128, 128])), dma=True)

        for i in range(NT):
            sl = slice(i * 128, (i + 1) * 128)
            sc.add("sync", lambda e, sl=sl: e.dma_start(out=sx[:], in_=x_in[sl, :]), dma=True)
            sc.add("sync", lambda e, sl=sl: e.dma_start(out=scos[:], in_=cos_in[sl, :]), dma=True)
            sc.add("sync", lambda e, sl=sl: e.dma_start(out=ssin[:], in_=sin_in[sl, :]), dma=True)
            sc.add("vector", lambda e: nc.vector.reduce_sum(st1[:], sx[:], axis=AX.X))
            sc.add("scalar", lambda e: nc.scalar.activation(ntrash[:], sx[:, 0:128], AF.Square, accum_out=st2[:]))
            for cq in range(1, 16):
                sc.add("scalar", lambda e, cq=cq: nc.scalar.activation(
                    ntrash[:], sx[:, cq * 128:(cq + 1) * 128], AF.Square, accum_out=nbuf[:, cq:cq + 1]))
            sc.add("vector", lambda e: nc.vector.reduce_sum(st3[:], nbuf[:, 1:16], axis=AX.X))
            sc.add("vector", lambda e: nc.vector.tensor_add(st2[:], st2[:], st3[:]))
            sc.add("vector", lambda e: nc.vector.tensor_scalar_mul(st1[:], st1[:], 1.0 / D))
            sc.add("vector", lambda e: nc.vector.tensor_scalar_mul(st2[:], st2[:], 1.0 / D))
            sc.add("vector", lambda e: nc.vector.tensor_mul(st3[:], st1[:], st1[:]))
            sc.add("vector", lambda e: nc.vector.tensor_sub(st2[:], st2[:], st3[:]))
            sc.add("scalar", lambda e: nc.scalar.activation(st3[:], st2[:], AF.Sqrt, bias=eps5[:]))
            sc.add("vector", lambda e: nc.vector.reciprocal(st4[:], st3[:]))
            sc.add("vector", lambda e: nc.vector.scalar_tensor_tensor(
                st3[:], st1[:], -1.0, st4[:], op0=ALU.mult, op1=ALU.mult))
            sc.add("vector", lambda e: nc.vector.scalar_tensor_tensor(
                xn1b[:], sx[:], st4[:], st3[:].broadcast_to([128, D]), op0=ALU.mult, op1=ALU.add))
            sc.add("gpsimd", lambda e, sl=sl: e.dma_start(out=o_xn1[sl, :], in_=xn1b[:]), dma=True)
            for c0 in range(0, 16, 4):
                for j in range(4):
                    dc = c0 + j
                    sc.add("tensor", lambda e, dc=dc, j=j: nc.tensor.transpose(
                        pT[:, j * 128:(j + 1) * 128], xn1b[:, dc * 128:(dc + 1) * 128], identb[:]))
                for j in range(4):
                    dc = c0 + j
                    sc.add("scalar", lambda e, dc=dc, j=j: nc.scalar.copy(
                        xnTt[:, dc, :], pT[:, j * 128:(j + 1) * 128]))
            for b2 in range(2):
                for dc in range(16):
                    sc.add("tensor", lambda e, b2=b2, dc=dc: nc.tensor.matmul(
                        pmm[:], xnTt[:, dc, :], swqc[:, dc, b2 * 512:(b2 + 1) * 512],
                        start=(dc == 0), stop=(dc == 15)))
                sc.add("scalar", lambda e, b2=b2: nc.scalar.copy(hqb[:, b2 * 512:(b2 + 1) * 512], pmm[:]))
            for c0 in range(0, 8, 4):
                for j in range(4):
                    dc = c0 + j
                    sc.add("tensor", lambda e, dc=dc, j=j: nc.tensor.transpose(
                        pT[:, j * 128:(j + 1) * 128], hqb[:, dc * 128:(dc + 1) * 128], identb[:]))
                for j in range(4):
                    dc = c0 + j
                    sc.add("scalar", lambda e, dc=dc, j=j: nc.scalar.copy(
                        hqT[:, dc, :], pT[:, j * 128:(j + 1) * 128]))
            for b4 in range(4):
                for dc in range(8):
                    sc.add("tensor", lambda e, b4=b4, dc=dc: nc.tensor.matmul(
                        pmm[:], hqT[:, dc, :], swqup[:, dc, b4 * 512:(b4 + 1) * 512],
                        start=(dc == 0), stop=(dc == 7)))
                sc.add("scalar", lambda e, b4=b4: nc.scalar.copy(qf[:, b4 * 512:(b4 + 1) * 512], pmm[:]))
            qv = qf[:].rearrange("p (h d) -> p h d", h=NH)
            rv = qrot[:].rearrange("p (h d) -> p h d", h=NH)
            tv = tmp32[:].rearrange("p (h d) -> p h d", h=NH)
            cosb = scos[:].unsqueeze(1).broadcast_to([128, NH, 32])
            sinb = ssin[:].unsqueeze(1).broadcast_to([128, NH, 32])
            sc.add("vector", lambda e: nc.vector.tensor_mul(rv[:, :, 0:32], qv[:, :, 0:32], cosb))
            sc.add("vector", lambda e: nc.vector.tensor_mul(tv, qv[:, :, 32:64], sinb))
            sc.add("vector", lambda e: nc.vector.tensor_sub(rv[:, :, 0:32], rv[:, :, 0:32], tv))
            sc.add("vector", lambda e: nc.vector.tensor_mul(rv[:, :, 32:64], qv[:, :, 32:64], cosb))
            sc.add("vector", lambda e: nc.vector.tensor_mul(tv, qv[:, :, 0:32], sinb))
            sc.add("vector", lambda e: nc.vector.tensor_add(rv[:, :, 32:64], rv[:, :, 32:64], tv))
            sc.add("vector", lambda e: nc.vector.tensor_copy(rv[:, :, 64:128], qv[:, :, 64:128]))
            for h in range(NH):
                sc.add("scalar", lambda e, h=h: nc.scalar.activation(
                    ntrash[:], qrot[:, h * 128:(h + 1) * 128], AF.Square, accum_out=nbuf[:, h:h + 1]))
            sc.add("vector", lambda e: nc.vector.tensor_scalar(
                rstd16[:], nbuf[:], 1.0 / 128.0, None, op0=ALU.mult))
            sc.add("scalar", lambda e: nc.scalar.activation(rstd16[:], rstd16[:], AF.Sqrt, bias=eps6[:]))
            sc.add("vector", lambda e: nc.vector.reciprocal(rstd16[:], rstd16[:]))
            for h in range(NH):
                sc.add("vector", lambda e, h=h: nc.vector.scalar_tensor_tensor(
                    outb[:, h * 128:(h + 1) * 128], qrot[:, h * 128:(h + 1) * 128],
                    rstd16[:, h:h + 1], qnwb[:], op0=ALU.mult, op1=ALU.mult))
            sc.add("gpsimd", lambda e, sl=sl: e.dma_start(out=o_q[sl, :], in_=outb[:]), dma=True)
            for dc in range(16):
                sc.add("tensor", lambda e, dc=dc: nc.tensor.matmul(
                    pc[:], xnTt[:, dc, :], swkw[:, dc, :], start=(dc == 0), stop=(dc == 15)))
            sc.add("scalar", lambda e: nc.scalar.copy(k0[:], pc[:]))
            sc.add("scalar", lambda e: nc.scalar.activation(ntrash[:], k0[:], AF.Square, accum_out=st1[:]))
            sc.add("vector", lambda e: nc.vector.tensor_scalar_mul(st1[:], st1[:], 1.0 / 128.0))
            sc.add("scalar", lambda e: nc.scalar.activation(st1[:], st1[:], AF.Sqrt, bias=eps6[:]))
            sc.add("vector", lambda e: nc.vector.reciprocal(st1[:], st1[:]))
            sc.add("vector", lambda e: nc.vector.scalar_tensor_tensor(
                k1[:], k0[:], st1[:], knwb[:], op0=ALU.mult, op1=ALU.mult))
            sc.add("vector", lambda e: nc.vector.tensor_mul(kb16[:, 0:32], k1[:, 0:32], scos[:]))
            sc.add("vector", lambda e: nc.vector.tensor_mul(tmp32[:, 0:32], k1[:, 32:64], ssin[:]))
            sc.add("vector", lambda e: nc.vector.tensor_sub(kb16[:, 0:32], kb16[:, 0:32], tmp32[:, 0:32]))
            sc.add("vector", lambda e: nc.vector.tensor_mul(kb16[:, 32:64], k1[:, 32:64], scos[:]))
            sc.add("vector", lambda e: nc.vector.tensor_mul(tmp32[:, 0:32], k1[:, 0:32], ssin[:]))
            sc.add("vector", lambda e: nc.vector.tensor_add(kb16[:, 32:64], kb16[:, 32:64], tmp32[:, 0:32]))
            sc.add("vector", lambda e: nc.vector.tensor_copy(kb16[:, 64:128], k1[:, 64:128]))
            sc.add("gpsimd", lambda e, sl=sl: e.dma_start(out=o_swk[sl, :], in_=kb16[:]), dma=True)
            for dc in range(16):
                sc.add("tensor", lambda e, dc=dc: nc.tensor.matmul(
                    pc[:], xnTt[:, dc, :], swvw[:, dc, :], start=(dc == 0), stop=(dc == 15)))
            sc.add("scalar", lambda e: nc.scalar.copy(kb16[:], pc[:]))
            sc.add("gpsimd", lambda e, sl=sl: e.dma_start(out=o_swv[sl, :], in_=kb16[:]), dma=True)
            for dc in range(16):
                sc.add("tensor", lambda e, dc=dc: nc.tensor.matmul(
                    pc[:], xnTt[:, dc, :], swkvb[:, dc, :], start=(dc == 0), stop=(dc == 15)))
            for dc in range(16):
                sc.add("tensor", lambda e, dc=dc: nc.tensor.matmul(
                    pz[:], xnTt[:, dc, :], swzb[:, dc, :], start=(dc == 0), stop=(dc == 15)))
            sc.add("vector", lambda e: nc.vector.tensor_add(zf[:], pz[:], biasb[:]))
            sc.add("scalar", lambda e: nc.scalar.copy(cf[:], pc[:]))
            sc.add("tensor", lambda e: nc.tensor.transpose(pT2[:, 0:128], zf[:], identf[:]))
            sc.add("tensor", lambda e: nc.tensor.transpose(pT2[:, 128:256], cf[:], identf[:]))
            sc.add("scalar", lambda e: nc.scalar.copy(zT[:], pT2[:, 0:128]))
            sc.add("scalar", lambda e: nc.scalar.copy(cT[:], pT2[:, 128:256]))
            zv = zT[:].rearrange("p (g m) -> p g m", g=4)
            cv = cT[:].rearrange("p (g m) -> p g m", g=4)
            sc.add("vector", lambda e: nc.vector.reduce_max(zm[:], zv, axis=AX.X))
            sc.add("vector", lambda e: nc.vector.tensor_sub(zv, zv, zm[:].unsqueeze(2).broadcast_to([128, 4, 32])))
            sc.add("scalar", lambda e: nc.scalar.activation(zT[:], zT[:], AF.Exp))
            sc.add("vector", lambda e: nc.vector.reduce_sum(gs[:], zv, axis=AX.X))
            sc.add("vector", lambda e: nc.vector.reciprocal(gs[:], gs[:]))
            sc.add("vector", lambda e: nc.vector.tensor_mul(zv, zv, gs[:].unsqueeze(2).broadcast_to([128, 4, 32])))
            sc.add("vector", lambda e: nc.vector.tensor_mul(cv, cv, zv))
            sc.add("vector", lambda e, i=i: nc.vector.reduce_sum(ckvT[:, i * 4:(i + 1) * 4], cv, axis=AX.X))
            for dc in range(16):
                sc.add("tensor", lambda e, dc=dc: nc.tensor.matmul(
                    pc[:, 0:64], xnTt[:, dc, :], swikv[:, dc, :], start=(dc == 0), stop=(dc == 15)))
            for dc in range(16):
                sc.add("tensor", lambda e, dc=dc: nc.tensor.matmul(
                    pz[:, 0:64], xnTt[:, dc, :], swiz[:, dc, :], start=(dc == 0), stop=(dc == 15)))
            sc.add("vector", lambda e: nc.vector.tensor_add(zf[:, 0:64], pz[:, 0:64], biasi[:]))
            sc.add("scalar", lambda e: nc.scalar.copy(cf[:, 0:64], pc[:, 0:64]))
            sc.add("tensor", lambda e: nc.tensor.transpose(pT2[0:64, 0:128], zf[:, 0:64], identf[:]))
            sc.add("tensor", lambda e: nc.tensor.transpose(pT2[0:64, 128:256], cf[:, 0:64], identf[:]))
            sc.add("scalar", lambda e: nc.scalar.copy(zT[0:64, 0:128], pT2[0:64, 0:128]))
            sc.add("scalar", lambda e: nc.scalar.copy(cT[0:64, 0:128], pT2[0:64, 128:256]))
            zv2 = zT[0:64, :].rearrange("p (g m) -> p g m", g=4)
            cv2 = cT[0:64, :].rearrange("p (g m) -> p g m", g=4)
            sc.add("vector", lambda e: nc.vector.reduce_max(zm[0:64, :], zv2, axis=AX.X))
            sc.add("vector", lambda e: nc.vector.tensor_sub(zv2, zv2, zm[0:64, :].unsqueeze(2).broadcast_to([64, 4, 32])))
            sc.add("scalar", lambda e: nc.scalar.activation(zT[0:64, :], zT[0:64, :], AF.Exp))
            sc.add("vector", lambda e: nc.vector.reduce_sum(gs[0:64, :], zv2, axis=AX.X))
            sc.add("vector", lambda e: nc.vector.reciprocal(gs[0:64, :], gs[0:64, :]))
            sc.add("vector", lambda e: nc.vector.tensor_mul(zv2, zv2, gs[0:64, :].unsqueeze(2).broadcast_to([64, 4, 32])))
            sc.add("vector", lambda e: nc.vector.tensor_mul(cv2, cv2, zv2))
            sc.add("vector", lambda e, i=i: nc.vector.reduce_sum(ciT[:, i * 4:(i + 1) * 4], cv2, axis=AX.X))
            for dc in range(16):
                sc.add("tensor", lambda e, dc=dc: nc.tensor.matmul(
                    pc[:, 0:64], xnTt[:, dc, :], swdq[:, dc, :], start=(dc == 0), stop=(dc == 15)))
            sc.add("scalar", lambda e: nc.scalar.copy(dqb[:], pc[:, 0:64]))
            sc.add("tensor", lambda e: nc.tensor.transpose(pT[0:64, 0:128], dqb[:], identb[:]))
            sc.add("scalar", lambda e: nc.scalar.copy(dqT[:], pT[0:64, 0:128]))
            sc.add("tensor", lambda e: nc.tensor.matmul(pqi[:], dqT[:], swiuq[:], start=True, stop=True))
            sc.add("scalar", lambda e: nc.scalar.copy(qib[:], pqi[:]))
            sc.add("gpsimd", lambda e, sl=sl: e.dma_start(out=o_qi[sl, :], in_=qib[:]), dma=True)
            for dc in range(16):
                sc.add("tensor", lambda e, dc=dc: nc.tensor.matmul(
                    pc[:, 0:4], xnTt[:, dc, :], sww[:, dc, :], start=(dc == 0), stop=(dc == 15)))
            sc.add("scalar", lambda e: nc.scalar.copy(wgtb[:], pc[:, 0:4]))
            sc.add("gpsimd", lambda e, sl=sl: e.dma_start(out=o_wgt[sl, :], in_=wgtb[:]), dma=True)
        sc.add("gpsimd", lambda e: e.dma_start(out=o_ckv, in_=ckvT[:]), dma=True)
        sc.add("gpsimd", lambda e: e.dma_start(out=o_ci, in_=ciT[:]), dma=True)
        sc.emit(nc, sem)
    return nc


def run_l1pre(inp, x1):
    cos, sin = _rope_l1()
    nc = build_l1pre()
    xs = _slices(np.asarray(x1).reshape(2, T, D))
    in_maps = []
    for c in range(NCORE):
        r = c % 4
        pos0 = r * TPC
        in_maps.append({
            "x1": xs[c].astype(np.float32),
            "wqc": np.asarray(inp["L1_attn_w_qc"]),
            "wqup": np.asarray(inp["L1_attn_w_qup"]),
            "wkvb": np.asarray(inp["L1_attn_w_kv_b"]),
            "wzb": np.asarray(inp["L1_attn_w_z_b"]),
            "biasb": np.asarray(inp["L1_attn_bias_b"]),
            "wk": np.asarray(inp["L1_attn_w_k"]),
            "wv": np.asarray(inp["L1_attn_w_v"]),
            "wikv": np.asarray(inp["L1_attn_wi_kv"]),
            "wiz": np.asarray(inp["L1_attn_wi_z"]),
            "biasi": np.asarray(inp["L1_attn_bias_i"]),
            "wdq": np.asarray(inp["L1_attn_w_dq"]),
            "wiuq": np.asarray(inp["L1_attn_w_iuq"]),
            "ww": np.asarray(inp["L1_attn_w_w"]),
            "qnw": np.asarray(inp["L1_attn_q_norm_w"]),
            "knw": np.asarray(inp["L1_attn_k_norm_w"]),
            "cos": np.ascontiguousarray(cos[pos0:pos0 + TPC]),
            "sin": np.ascontiguousarray(sin[pos0:pos0 + TPC]),
        })
    res = run_bass_kernel_spmd(nc, in_maps, core_ids=list(range(NCORE)))
    outs = {}
    for name in ("xn1", "q", "swk", "swv", "qi", "wgt"):
        outs[name] = _gather([res.results[c][name] for c in range(NCORE)])
    outs["compkvT"] = np.stack(
        [np.concatenate([res.results[b * 4 + r]["compkvT"] for r in range(4)], axis=1) for b in range(2)])
    outs["compiT"] = np.stack(
        [np.concatenate([res.results[b * 4 + r]["compiT"] for r in range(4)], axis=1) for b in range(2)])
    return outs


def build_l1post():
    nc = bass.Bass("TRN2", target_bir_lowering=False)
    x_in = nc.dram_tensor("x1", [TPC, D], F32, kind="ExternalInput").ap()
    xn_in = nc.dram_tensor("xn1", [TPC, D], BF16, kind="ExternalInput").ap()
    q_in = nc.dram_tensor("q", [TPC, D], BF16, kind="ExternalInput").ap()
    qi_in = nc.dram_tensor("qi", [TPC, 256], BF16, kind="ExternalInput").ap()
    wgt_in = nc.dram_tensor("wgt", [TPC, 4], F32, kind="ExternalInput").ap()
    ckv_in = nc.dram_tensor("ckvT", [128, NB], F32, kind="ExternalInput").ap()
    ci_in = nc.dram_tensor("ciT", [64, NB], F32, kind="ExternalInput").ap()
    swkt_in = nc.dram_tensor("swkt", [128, TPC + 512], BF16, kind="ExternalInput").ap()
    swv_in = nc.dram_tensor("swv", [20, 128, 128], BF16, kind="ExternalInput").ap()
    swmsk_in = nc.dram_tensor("swmsk", [NT, 5, 128, 128], BF16, kind="ExternalInput").ap()
    sink_in = nc.dram_tensor("sink", [1, NH], F32, kind="ExternalInput").ap()
    knw_in = nc.dram_tensor("knw", [128], F32, kind="ExternalInput").ap()
    gw_in = nc.dram_tensor("groupw", [4, 512, 512], F32, kind="ExternalInput").ap()
    fw_in = nc.dram_tensor("finalw", [D, D], F32, kind="ExternalInput").ap()
    wg_in = nc.dram_tensor("wgate", [D, 4 * D], F32, kind="ExternalInput").ap()
    wu_in = nc.dram_tensor("wup", [D, 4 * D], F32, kind="ExternalInput").ap()
    wd_in = nc.dram_tensor("wdown", [4 * D, D], F32, kind="ExternalInput").ap()
    madd_in = nc.dram_tensor("maskadd", [TPC, NB], F32, kind="ExternalInput").ap()
    guard_in = nc.dram_tensor("guard", [TPC, 1], F32, kind="ExternalInput").ap()
    hp_in = nc.dram_tensor("haspick", [TPC, 1], F32, kind="ExternalInput").ap()
    o_x2 = nc.dram_tensor("x2", [TPC, D], F32, kind="ExternalOutput").ap()

    sc = Sched()
    with ExitStack() as es:
        sb = lambda n, s, d: es.enter_context(nc.sbuf_tensor(n, s, d))
        ps = lambda n, s, d=F32: es.enter_context(nc.psum_tensor(n, s, d))
        identb = sb("identb", [128, 128], BF16)
        identf = sb("identf", [128, 128], F32)
        big1_ = sb("big1", [128, 16 * D], BF16)
        arena_ = sb("arena", [128, 24832], BF16)
        sckvT = sb("sckvT", [128, NB], F32)
        sciTb = sb("sciTb", [64, NB], BF16)
        crm0 = sb("crm0", [128, 128], BF16)
        crm1 = sb("crm1", [128, 128], BF16)
        crmT = sb("crmT", [128, NB], BF16)
        comp0 = sb("comp0", [128, 128], F32)
        knwb = sb("knwb", [128, 128], F32)
        sinkb = sb("sinkb", [128, NH], F32)
        sq = sb("sq", [128, D], BF16)
        sqi = sb("sqi", [128, 256], BF16)
        swgt = sb("swgt", [128, 4], F32)
        smadd = sb("smadd", [128, NB], F32)
        sguard = sb("sguard", [128, 1], F32)
        shp = sb("shp", [128, 1], F32)
        smsk = sb("smsk", [128, 5 * 128], BF16)
        qTb = sb("qTb", [128, 512], BF16)
        qiT = sb("qiT", [64, 512], BF16)
        rel0 = sb("rel0", [128, NB], F32)
        iscacc = sb("iscacc", [128, NB], F32)
        iscw = sb("iscw", [128, NB], F32)
        sm8 = sb("sm8", [128, 8], F32)
        sthr = sb("sthr", [128, 1], F32)
        vb = sb("vb", [128, NB], BF16)
        selbit = sb("selbit", [128, NB], BF16)
        sladd = sb("sladd", [128, NB], F32)
        scf = sb("scf", [128, NB], F32)
        esc = sb("esc", [128, NB], BF16)
        escT = sb("escT", [128, NB], BF16)
        esw = sb("esw", [128, 512], BF16)
        eswd = sb("eswd", [128, 128], BF16)
        eswT = sb("eswT", [128, 640], BF16)
        esk = sb("esk", [128, 1], F32)
        attnb = sb("attnb", [128, D], F32)
        attnbb = sb("attnbb", [128, D], BF16)
        aT_ = sb("aT", [128, 16 * 128], BF16)
        ogb = sb("ogb", [128, D], BF16)
        sx = sb("sx", [128, D], F32)
        st1 = sb("st1", [128, 1], F32)
        st2 = sb("st2", [128, 1], F32)
        st3 = sb("st3", [128, 1], F32)
        eps6 = sb("eps6", [128, 1], F32)
        hb = sb("hb", [128, 512], BF16)
        gb = sb("gb", [128, 512], BF16)
        hab = sb("hab", [128, 512], BF16)
        hsp = sb("hsp", [128, 512], BF16)
        hspT_ = sb("hspT", [128, 512], BF16)
        tscr_ = sb("tscr", [128, 6 * 128], BF16)
        obf = sb("obf", [128, 512], F32)
        pA = ps("pA", [128, 512])
        pB = ps("pB", [128, 512])
        pC = ps("pC", [128, 512])
        pD = ps("pD", [128, 129])
        pE = ps("pE", [128, 256])
        pF = ps("pF", [128, 128])
        pT = es.enter_context(nc.psum_tensor("pT", [128, 512], BF16))
        sem = es.enter_context(nc.semaphore("sem"))
        _IDENT[0] = identb
        sfw = big1_[:].rearrange("p (c t) -> p c t", c=16)
        xnT = big1_[:].rearrange("p (c t) -> p c t", c=16)
        swkT = arena_[:, 0:TPC + 512]
        swvext = arena_[:, 0:0 + 8448]
        swkT = arena_[:, 0:TPC + 512]
        swvext = arena_[:, 2560:2560 + 20 * 132].rearrange("p (c x) -> p c x", c=20)
        sgw = arena_[:, 5200:5200 + 16 * 512].rearrange("p (c x) -> p c x", c=16)
        sgate = arena_[:, 0:8192].rearrange("p (c x) -> p c x", c=16)
        sup = arena_[:, 8192:16384].rearrange("p (c x) -> p c x", c=16)
        sdown = arena_[:, 16384:16384 + 4 * 2048].rearrange("p (c x) -> p c x", c=4)
        aT = aT_[:].rearrange("p (c t) -> p c t", c=16)
        hspT = hspT_[:].rearrange("p (c t) -> p c t", c=4)
        tscr = tscr_[:].rearrange("p (c t) -> p c t", c=6)
        _mk_consts(nc, sc, identb, ident_f32=identf)
        sc.add("gpsimd", lambda e: nc.gpsimd.memset(eps6[:], 1e-6))
        sc.add("sync", lambda e: e.dma_start(out=sinkb[:], in_=sink_in.broadcast_to([128, NH])), dma=True)
        sc.add("sync", lambda e: e.dma_start(out=knwb[:], in_=knw_in.unsqueeze(0).broadcast_to([128, 128])), dma=True)
        sc.add("sync", lambda e: e.dma_start(out=sckvT[:], in_=ckv_in), dma=True)
        sc.add("gpsimd", lambda e: e.dma_start(out=sciTb[:], in_=ci_in), dma=True)
        for cchunk in range(2):
            sc.add("tensor", lambda e, cchunk=cchunk: nc.tensor.transpose(
                pE[:, 0:128], sckvT[:, cchunk * 128:(cchunk + 1) * 128], identf[:]))
            sc.add("scalar", lambda e: nc.scalar.copy(comp0[:], pE[:, 0:128]))
            sc.add("scalar", lambda e: nc.scalar.activation(comp0[:], comp0[:], AF.Square, accum_out=st1[:]))
            sc.add("scalar", lambda e: nc.scalar.copy(comp0[:], pE[:, 0:128]))
            sc.add("vector", lambda e: nc.vector.tensor_scalar_mul(st1[:], st1[:], 1.0 / 128.0))
            sc.add("scalar", lambda e: nc.scalar.activation(st1[:], st1[:], AF.Sqrt, bias=eps6[:]))
            sc.add("vector", lambda e: nc.vector.reciprocal(st1[:], st1[:]))
            crm = crm0 if cchunk == 0 else crm1
            sc.add("vector", lambda e, crm=crm: nc.vector.scalar_tensor_tensor(
                crm[:], comp0[:], st1[:], knwb[:], op0=ALU.mult, op1=ALU.mult))
            sc.add("tensor", lambda e, crm=crm: nc.tensor.transpose(pT[:, 0:128], crm[:], identb[:]))
            sc.add("scalar", lambda e, cchunk=cchunk: nc.scalar.copy(
                crmT[:, cchunk * 128:(cchunk + 1) * 128], pT[:, 0:128]))
        for g in range(4):
            for cc in range(4):
                sc.add("gpsimd", lambda e, g=g, cc=cc: e.dma_start(
                    out=sgw[:, g * 4 + cc, :], in_=gw_in[g, cc * 128:(cc + 1) * 128, :]), dma=True)
        for dc in range(16):
            sc.add("gpsimd", lambda e, dc=dc: e.dma_start(
                out=sfw[:, dc, :], in_=fw_in[dc * 128:(dc + 1) * 128, :]), dma=True)
        sc.add("sync", lambda e: e.dma_start(out=swkT[:], in_=swkt_in), dma=True)
        sc.add("sync", lambda e: e.dma_start(
            out=swvext[:, :, 0:128], in_=swv_in.rearrange("c p d -> p c d")), dma=True)
        sc.add("vector", lambda e: nc.vector.memset(swvext[:, :, 128:129], 1.0))

        for i in range(NT):
            sl = slice(i * 128, (i + 1) * 128)
            sc.add("sync", lambda e, sl=sl: e.dma_start(out=sq[:], in_=q_in[sl, :]), dma=True)
            sc.add("sync", lambda e, sl=sl: e.dma_start(out=sqi[:], in_=qi_in[sl, :]), dma=True)
            sc.add("sync", lambda e, sl=sl: e.dma_start(out=swgt[:], in_=wgt_in[sl, :]), dma=True)
            sc.add("sync", lambda e, sl=sl: e.dma_start(out=smadd[:], in_=madd_in[sl, :]), dma=True)
            sc.add("sync", lambda e, sl=sl: e.dma_start(out=sguard[:], in_=guard_in[sl, :]), dma=True)
            sc.add("sync", lambda e, sl=sl: e.dma_start(out=shp[:], in_=hp_in[sl, :]), dma=True)
            sc.add("sync", lambda e, i=i: e.dma_start(
                out=smsk[:].rearrange("p (c x) -> p c x", c=5),
                in_=swmsk_in[i].rearrange("c p x -> p c x")), dma=True)
            for j in range(4):
                sc.add("tensor", lambda e, j=j: nc.tensor.transpose(
                    pT[0:64, j * 128:(j + 1) * 128], sqi[:, j * 64:(j + 1) * 64], identb[:]))
            sc.add("scalar", lambda e: nc.scalar.copy(qiT[:], pT[0:64, :]))
            for h in range(4):
                sc.add("tensor", lambda e, h=h: nc.tensor.matmul(
                    pE[:], qiT[:, h * 128:(h + 1) * 128], sciTb[:], start=True, stop=True))
                sc.add("vector", lambda e: nc.vector.tensor_relu(rel0[:], pE[:]))
                if h == 0:
                    sc.add("vector", lambda e, h=h: nc.vector.tensor_scalar(
                        iscacc[:], rel0[:], swgt[:, h:h + 1], None, op0=ALU.mult))
                else:
                    sc.add("vector", lambda e, h=h: nc.vector.scalar_tensor_tensor(
                        iscacc[:], rel0[:], swgt[:, h:h + 1], iscacc[:], op0=ALU.mult, op1=ALU.add))
            sc.add("vector", lambda e: nc.vector.tensor_add(iscacc[:], iscacc[:], smadd[:]))
            sc.add("vector", lambda e: nc.vector.tensor_copy(iscw[:], iscacc[:]))
            for it in range(4):
                sc.add("vector", lambda e: nc.vector.max(out=sm8[:], in_=iscw[:]))
                sc.add("vector", lambda e: nc.vector.match_replace(
                    out=iscw[:], in_to_replace=sm8[:], in_values=iscw[:], imm_value=NEG))
            sc.add("vector", lambda e: nc.vector.tensor_reduce(
                sthr[:], sm8[:], axis=AX.X, op=ALU.min))
            sc.add("vector", lambda e: nc.vector.tensor_scalar(
                vb[:], smadd[:], -1.0, None, op0=ALU.is_ge))
            sc.add("vector", lambda e: nc.vector.tensor_scalar(
                selbit[:], iscacc[:], sthr[:], None, op0=ALU.is_ge))
            sc.add("vector", lambda e: nc.vector.tensor_mul(selbit[:], selbit[:], vb[:]))
            sc.add("vector", lambda e: nc.vector.tensor_scalar(
                sladd[:], selbit[:], 1.0, 30000.0, op0=ALU.subtract, op1=ALU.mult))
            for h0 in range(0, NH, 4):
                for j in range(4):
                    h = h0 + j
                    sc.add("tensor", lambda e, h=h, j=j: nc.tensor.transpose(
                        pT[:, j * 128:(j + 1) * 128], sq[:, h * 128:(h + 1) * 128], identb[:]))
                sc.add("scalar", lambda e: nc.scalar.copy(qTb[:], pT[:]))
                for j in range(4):
                    h = h0 + j
                    lhs = qTb[:, j * 128:(j + 1) * 128]
                    sc.add("tensor", lambda e, lhs=lhs: nc.tensor.matmul(
                        pE[:], lhs, crmT[:], start=True, stop=True))
                    sc.add("vector", lambda e: nc.vector.tensor_add(scf[:], pE[:], sladd[:]))
                    sc.add("vector", lambda e: nc.vector.reduce_max(st1[:], scf[:], axis=AX.X, negate=True))
                    sc.add("vector", lambda e: nc.vector.tensor_scalar_mul(st1[:], st1[:], ISCALE))
                    sc.add("scalar", lambda e: nc.scalar.activation(
                        esc[:], scf[:], AF.Exp, bias=st1[:], scale=ISCALE, accum_out=st2[:]))
                    sc.add("vector", lambda e: nc.vector.tensor_add(st2[:], st2[:], sguard[:]))
                    sc.add("vector", lambda e: nc.vector.reciprocal(st2[:], st2[:]))
                    sc.add("vector", lambda e: nc.vector.tensor_mul(st2[:], st2[:], shp[:]))
                    for cchunk in range(2):
                        sc.add("tensor", lambda e, cchunk=cchunk: nc.tensor.transpose(
                            pT[:, cchunk * 128:(cchunk + 1) * 128],
                            esc[:, cchunk * 128:(cchunk + 1) * 128], identb[:]))
                    sc.add("scalar", lambda e: nc.scalar.copy(escT[:], pT[:, 0:256]))
                    sc.add("tensor", lambda e: nc.tensor.matmul(
                        pD[:, 0:128], escT[:, 0:128], crm0[:], start=True, stop=False, skip_group_check=True))
                    sc.add("tensor", lambda e: nc.tensor.matmul(
                        pD[:, 0:128], escT[:, 128:256], crm1[:], start=False, stop=True, skip_group_check=True))
                    sc.add("vector", lambda e, h=h: nc.vector.tensor_scalar(
                        attnb[:, h * 128:(h + 1) * 128], pD[:, 0:128], st2[:], None, op0=ALU.mult))
                    for jj in range(4):
                        sc.add("tensor", lambda e, lhs=lhs, i=i, jj=jj: nc.tensor.matmul(
                            pA[:, jj * 128:(jj + 1) * 128], lhs,
                            swkT[:, (i + jj) * 128:(i + jj + 1) * 128], start=True, stop=True,
                            skip_group_check=True))
                    sc.add("tensor", lambda e, lhs=lhs, i=i: nc.tensor.matmul(
                        pF[:], lhs, swkT[:, (i + 4) * 128:(i + 5) * 128], start=True, stop=True))
                    sc.add("vector", lambda e: nc.vector.reduce_max(st1[:], pA[:], axis=AX.X))
                    sc.add("vector", lambda e: nc.vector.reduce_max(st3[:], pF[:], axis=AX.X))
                    sc.add("vector", lambda e: nc.vector.tensor_max(st1[:], st1[:], st3[:]))
                    sc.add("vector", lambda e: nc.vector.tensor_scalar_mul(st1[:], st1[:], ISCALE))
                    sc.add("vector", lambda e, h=h: nc.vector.tensor_max(st1[:], st1[:], sinkb[:, h:h + 1]))
                    sc.add("vector", lambda e: nc.vector.tensor_scalar_mul(st1[:], st1[:], -1.0))
                    sc.add("scalar", lambda e: nc.scalar.activation(
                        esw[:], pA[:], AF.Exp, bias=st1[:], scale=ISCALE))
                    sc.add("scalar", lambda e: nc.scalar.activation(
                        eswd[:], pF[:], AF.Exp, bias=st1[:], scale=ISCALE))
                    sc.add("scalar", lambda e, h=h: nc.scalar.activation(
                        esk[:], sinkb[:, h:h + 1], AF.Exp, bias=st1[:], scale=1.0))
                    smv = smsk[:].rearrange("p (c x) -> p c x", c=5)
                    for jj in range(4):
                        sc.add("vector", lambda e, jj=jj: nc.vector.tensor_mul(
                            esw[:, jj * 128:(jj + 1) * 128], esw[:, jj * 128:(jj + 1) * 128],
                            smv[:, jj, :]))
                    sc.add("vector", lambda e: nc.vector.tensor_mul(eswd[:], eswd[:], smv[:, 4, :]))
                    for jj in range(4):
                        sc.add("tensor", lambda e, jj=jj: nc.tensor.transpose(
                            pT[:, jj * 128:(jj + 1) * 128], esw[:, jj * 128:(jj + 1) * 128], identb[:]))
                    sc.add("scalar", lambda e: nc.scalar.copy(eswT[:, 0:512], pT[:]))
                    sc.add("tensor", lambda e: nc.tensor.transpose(pT[:, 0:128], eswd[:], identb[:]))
                    sc.add("scalar", lambda e: nc.scalar.copy(eswT[:, 512:640], pT[:, 0:128]))
                    for jj in range(5):
                        sc.add("tensor", lambda e, jj=jj, i=i: nc.tensor.matmul(
                            pD[:], eswT[:, jj * 128:(jj + 1) * 128], swvext[:, i + jj, 0:129],
                            start=(jj == 0), stop=(jj == 4), skip_group_check=True))
                    sc.add("vector", lambda e: nc.vector.tensor_copy(st1[:], pD[:, 128:129]))
                    sc.add("vector", lambda e: nc.vector.tensor_add(st1[:], st1[:], esk[:]))
                    sc.add("vector", lambda e: nc.vector.reciprocal(st1[:], st1[:]))
                    sc.add("vector", lambda e, h=h: nc.vector.scalar_tensor_tensor(
                        attnb[:, h * 128:(h + 1) * 128], pD[:, 0:128], st1[:],
                        attnb[:, h * 128:(h + 1) * 128], op0=ALU.mult, op1=ALU.add))
            sc.add("vector", lambda e: nc.vector.tensor_copy(attnbb[:], attnb[:]))
            for c0 in range(0, 16, 4):
                for j in range(4):
                    dc = c0 + j
                    sc.add("tensor", lambda e, dc=dc, j=j: nc.tensor.transpose(
                        pT[:, j * 128:(j + 1) * 128], attnbb[:, dc * 128:(dc + 1) * 128], identb[:]))
                for j in range(4):
                    dc = c0 + j
                    sc.add("scalar", lambda e, dc=dc, j=j: nc.scalar.copy(
                        aT[:, dc, :], pT[:, j * 128:(j + 1) * 128]))
            for g in range(4):
                for cc in range(4):
                    sc.add("tensor", lambda e, g=g, cc=cc: nc.tensor.matmul(
                        pB[:], aT[:, g * 4 + cc, :], sgw[:, g * 4 + cc, :],
                        start=(cc == 0), stop=(cc == 3)))
                sc.add("scalar", lambda e, g=g: nc.scalar.copy(ogb[:, g * 512:(g + 1) * 512], pB[:]))
            for c0 in range(0, 16, 4):
                for j in range(4):
                    dc = c0 + j
                    sc.add("tensor", lambda e, dc=dc, j=j: nc.tensor.transpose(
                        pT[:, j * 128:(j + 1) * 128], ogb[:, dc * 128:(dc + 1) * 128], identb[:]))
                for j in range(4):
                    dc = c0 + j
                    sc.add("scalar", lambda e, dc=dc, j=j: nc.scalar.copy(
                        aT[:, dc, :], pT[:, j * 128:(j + 1) * 128]))
            sc.add("sync", lambda e, sl=sl: e.dma_start(out=sx[:], in_=x_in[sl, :]), dma=True)
            for b4 in range(4):
                for dc in range(16):
                    sc.add("tensor", lambda e, b4=b4, dc=dc: nc.tensor.matmul(
                        pB[:], aT[:, dc, :], sfw[:, dc, b4 * 512:(b4 + 1) * 512],
                        start=(dc == 0), stop=(dc == 15)))
                sc.add("vector", lambda e, b4=b4: nc.vector.tensor_add(
                    sx[:, b4 * 512:(b4 + 1) * 512], sx[:, b4 * 512:(b4 + 1) * 512], pB[:]))
            sc.add("gpsimd", lambda e, sl=sl: e.dma_start(out=o_x2[sl, :], in_=sx[:]), dma=True)

        for i in range(NT):
            sc.add("sync", lambda e, i=i: e.dma_start(out=attnbb[:], in_=xn_in[i * 128:(i + 1) * 128, :]), dma=True)
            for c0 in range(0, 16, 4):
                for j in range(4):
                    dc = c0 + j
                    sc.add("tensor", lambda e, dc=dc, j=j: nc.tensor.transpose(
                        pT[:, j * 128:(j + 1) * 128], attnbb[:, dc * 128:(dc + 1) * 128], identb[:]))
                for j in range(4):
                    dc = c0 + j
                    sc.add("scalar", lambda e, dc=dc, j=j, i=i: nc.scalar.copy(
                        xnT[:, dc, i * 128:(i + 1) * 128], pT[:, j * 128:(j + 1) * 128]))
        for os_ in range(16):
            for dc in range(16):
                sc.add("gpsimd", lambda e, os_=os_, dc=dc: e.dma_start(
                    out=sgate[:, dc, :], in_=wg_in[dc * 128:(dc + 1) * 128, os_ * 512:(os_ + 1) * 512]), dma=True)
                sc.add("gpsimd", lambda e, os_=os_, dc=dc: e.dma_start(
                    out=sup[:, dc, :], in_=wu_in[dc * 128:(dc + 1) * 128, os_ * 512:(os_ + 1) * 512]), dma=True)
            for cc in range(4):
                sc.add("gpsimd", lambda e, os_=os_, cc=cc: e.dma_start(
                    out=sdown[:, cc, :], in_=wd_in[os_ * 512 + cc * 128:os_ * 512 + (cc + 1) * 128, :]), dma=True)
            for i in range(NT):
                for dc in range(16):
                    sc.add("tensor", lambda e, i=i, dc=dc: nc.tensor.matmul(
                        pA[:], xnT[:, dc, i * 128:(i + 1) * 128], sgate[:, dc, :],
                        start=(dc == 0), stop=(dc == 15)))
                sc.add("scalar", lambda e: nc.scalar.activation(gb[:], pA[:], AF.Silu))
                for dc in range(16):
                    sc.add("tensor", lambda e, i=i, dc=dc: nc.tensor.matmul(
                        pB[:], xnT[:, dc, i * 128:(i + 1) * 128], sup[:, dc, :],
                        start=(dc == 0), stop=(dc == 15)))
                sc.add("vector", lambda e: nc.vector.tensor_mul(hb[:], pB[:], gb[:]))
                sc.add("scalar", lambda e: nc.scalar.activation(hab[:], hb[:], AF.Abs))
                hv = hab[:].rearrange("p (g s) -> p g s", s=4)
                sc.add("vector", lambda e: nc.vector.tensor_max(tscr[:, 0, :], hv[:, :, 0], hv[:, :, 1]))
                sc.add("vector", lambda e: nc.vector.tensor_max(tscr[:, 1, :], hv[:, :, 2], hv[:, :, 3]))
                sc.add("vector", lambda e: nc.vector.tensor_tensor(
                    tscr[:, 2, :], hv[:, :, 0], hv[:, :, 1], op=ALU.min))
                sc.add("vector", lambda e: nc.vector.tensor_tensor(
                    tscr[:, 3, :], hv[:, :, 2], hv[:, :, 3], op=ALU.min))
                sc.add("vector", lambda e: nc.vector.tensor_tensor(
                    tscr[:, 4, :], tscr[:, 0, :], tscr[:, 1, :], op=ALU.min))
                sc.add("vector", lambda e: nc.vector.tensor_max(tscr[:, 5, :], tscr[:, 2, :], tscr[:, 3, :]))
                sc.add("vector", lambda e: nc.vector.tensor_max(tscr[:, 4, :], tscr[:, 4, :], tscr[:, 5, :]))
                hbv = hb[:].rearrange("p (g s) -> p g s", s=4)
                hsv = hsp[:].rearrange("p (g s) -> p g s", s=4)
                for s4 in range(4):
                    sc.add("vector", lambda e, s4=s4: nc.vector.tensor_tensor(
                        tscr[:, 5, :], hv[:, :, s4], tscr[:, 4, :], op=ALU.is_ge))
                    sc.add("vector", lambda e, s4=s4: nc.vector.tensor_mul(
                        hsv[:, :, s4], hbv[:, :, s4], tscr[:, 5, :]))
                for j in range(4):
                    sc.add("tensor", lambda e, j=j: nc.tensor.transpose(
                        pT[:, j * 128:(j + 1) * 128], hsp[:, j * 128:(j + 1) * 128], identb[:]))
                sc.add("scalar", lambda e: nc.scalar.copy(hspT_[:], pT[:]))
                for b4 in range(4):
                    for cc in range(4):
                        sc.add("tensor", lambda e, b4=b4, cc=cc: nc.tensor.matmul(
                            pC[:], hspT[:, cc, :], sdown[:, cc, b4 * 512:(b4 + 1) * 512],
                            start=(cc == 0), stop=(cc == 3)))
                    sc.add("vector", lambda e: nc.vector.tensor_copy(obf[:], pC[:]))
                    sc.add("gpsimd", lambda e, i=i, b4=b4: e.dma_start(
                        out=o_x2[i * 128:(i + 1) * 128, b4 * 512:(b4 + 1) * 512], in_=obf[:],
                        accum_op=ALU.add), dma=True)
        sc.emit(nc, sem)
    return nc


def run_l1post(inp, x1, pre1):
    nc = build_l1post()
    x1f = np.asarray(x1).reshape(2, T, D)
    xs = _slices(x1f)
    xns = _slices(np.asarray(pre1["xn1"]).reshape(2, T, D))
    qs = _slices(np.asarray(pre1["q"]).reshape(2, T, D))
    qis = _slices(np.asarray(pre1["qi"]).reshape(2, T, 256))
    wgs = _slices(np.asarray(pre1["wgt"]).reshape(2, T, 4))
    swk = np.asarray(pre1["swk"]).reshape(2, T, 128)
    swv = np.asarray(pre1["swv"]).reshape(2, T, 128)
    in_maps = []
    for c in range(NCORE):
        b = c // 4
        r = c % 4
        pos0 = r * TPC
        kloc = np.zeros((TPC + 512, 128), dtype=BF)
        a0 = max(0, pos0 - 512)
        kloc[512 - (pos0 - a0):512 + TPC] = swk[b, a0:pos0 + TPC].astype(BF)
        vloc = np.zeros((TPC + 512, 128), dtype=BF)
        vloc[512 - (pos0 - a0):512 + TPC] = swv[b, a0:pos0 + TPC].astype(BF)
        msk = np.zeros((NT, 5, 128, 128), dtype=BF)
        qpos = pos0 + np.arange(128)
        for i in range(NT):
            qp = (pos0 + i * 128 + np.arange(128))[:, None]
            for j in range(5):
                kp = (pos0 - 512 + (i + j) * 128 + np.arange(128))[None, :]
                valid = (kp >= 0) & (kp <= qp) & (qp - kp < L1_SW)
                msk[i, j] = valid.astype(np.float32)
        in_maps.append({
            "x1": xs[c].astype(np.float32),
            "xn1": xns[c].astype(BF),
            "q": qs[c].astype(BF),
            "qi": qis[c].astype(BF),
            "wgt": wgs[c].astype(np.float32),
            "ckvT": np.ascontiguousarray(np.asarray(pre1["compkvT"][b], np.float32)),
            "ciT": np.ascontiguousarray(np.asarray(pre1["compiT"][b], np.float32)),
            "swkt": np.ascontiguousarray(kloc.T),
            "swv": np.ascontiguousarray(vloc.reshape(20, 128, 128)),
            "swmsk": msk,
            "sink": np.asarray(inp["L1_attn_sink_logit"], np.float32).reshape(1, NH),
            "knw": np.asarray(inp["L1_attn_k_norm_w"]),
            "groupw": np.asarray(inp["L1_attn_group_w"]),
            "finalw": np.asarray(inp["L1_attn_final_w"]),
            "wgate": np.asarray(inp["L1_ffn_w_gate"]),
            "wup": np.asarray(inp["L1_ffn_w_up"]),
            "wdown": np.asarray(inp["L1_ffn_w_down"]),
            "maskadd": _l1_valid_maskadd(pos0),
            "guard": _l1_guard(pos0),
            "haspick": _l1_haspick(pos0),
        })
    res = run_bass_kernel_spmd(nc, in_maps, core_ids=list(range(NCORE)))
    return _gather([res.results[c]["x2"] for c in range(NCORE)])



def _slices(full):
    out = []
    for c in range(NCORE):
        b = c // 4
        r = c % 4
        out.append(np.ascontiguousarray(full[b, r * TPC:(r + 1) * TPC]))
    return out


def _gather(slc):
    per_b = [np.concatenate(slc[b * 4:(b + 1) * 4], axis=0) for b in range(2)]
    return np.stack(per_b, axis=0)


def kernel(**inp):
    res = run_fused(inp, phase_limit="p1", dbg=False)
    x1 = bgather([res.results[c]["x1"] for c in range(NCORE)])
    pre1 = run_l1pre(inp, x1)
    x2 = run_l1post(inp, x1, pre1)
    return np.ascontiguousarray(x2.astype(np.float32))


# version 04
# speedup vs baseline: 1.3072x; 1.1419x over previous
import math
from contextlib import ExitStack

import numpy as np
import ml_dtypes
import concourse.bass as bass
import concourse.mybir as mybir
from concourse.bass_utils import run_bass_kernel_spmd
from concourse.aot_env import allow_ici_all_to_all

F32 = mybir.dt.float32
BF16 = mybir.dt.bfloat16
U8 = mybir.dt.uint8
AF = mybir.ActivationFunctionType
ALU = mybir.AluOpType
AX = mybir.AxisListType
BF = ml_dtypes.bfloat16

B = 2
T = 8192
D = 2048
NH = 16
HD = 128
NCORE = 8
TPC = 2048
NT = 16
HEAVY = 819
ISCALE = 1.0 / math.sqrt(128.0)
NEG = -1.0e38
_BAND_DELTAS = [768, 896, 1024, 1152, 1280]
WORLD = [[0, 1, 2, 3, 4, 5, 6, 7]]

ENGINES = ("sync", "scalar", "vector", "tensor", "gpsimd")
_ALL = "__ALL__"


class Sched2:

    def __init__(self):
        self.ops = []
        self.n_coll = 0

    def add(self, engine, fn, r=None, w=None, dma=False):
        if r is None and w is None:
            r = [_ALL]
            w = [_ALL]
        self.ops.append(dict(engine=engine, fn=fn, kind=("d" if dma else "c"),
                             r=tuple(r or ()), w=tuple(w or ()), coll=-1))

    def coll(self, fn, r, w):
        self.ops.append(dict(engine="gpsimd", fn=fn, kind="coll",
                             r=tuple(r), w=tuple(w), coll=self.n_coll))
        self.n_coll += 1

    def barrier(self):
        self.ops.append(dict(engine="gpsimd", fn=None, kind="barrier",
                             r=(_ALL,), w=(_ALL,), coll=-1))

    def emit(self, nc, es, barrier_tile):
        ops = self.ops
        ccount = {e: 0 for e in ENGINES}
        dcount = {e: 0 for e in ENGINES}
        for op in ops:
            e = op["engine"]
            if op["kind"] == "d":
                dcount[e] += 1
                op["cnt"] = dcount[e]
            elif op["kind"] == "coll":
                op["cnt"] = 1
            else:
                ccount[e] += 1
                op["cnt"] = ccount[e]

        def opkey(op):
            if op["kind"] == "coll":
                return ("coll", op["coll"])
            if op["kind"] == "d":
                return ("d", op["engine"])
            return ("c", op["engine"])

        last_write = {}
        readers = {}
        last_per_stream = {}
        last_barrier = None
        for op in ops:
            need = {}

            def dep(d):
                if d is None or d is op:
                    return
                k = opkey(d)
                if need.get(k, 0) < d["cnt"]:
                    need[k] = d["cnt"]

            if _ALL in op["r"]:
                for d in last_per_stream.values():
                    dep(d)
            else:
                dep(last_barrier)
                for t in op["r"]:
                    dep(last_write.get(t))
                for t in op["w"]:
                    dep(last_write.get(t))
                    for rd in readers.get(t, ()):
                        dep(rd)
            op["need"] = need
            if _ALL in op["w"]:
                last_write = {t: op for t in last_write}
                readers = {}
                last_barrier = op
            else:
                for t in op["r"]:
                    readers.setdefault(t, []).append(op)
                for t in op["w"]:
                    last_write[t] = op
                    readers[t] = []
            last_per_stream[opkey(op)] = op

        csem = {}
        dsem = {}
        for e in ENGINES:
            if ccount[e]:
                csem[e] = es.enter_context(nc.semaphore(f"c_{e}"))
            if dcount[e]:
                dsem[e] = es.enter_context(nc.semaphore(f"d_{e}"))
        collsems = [es.enter_context(nc.semaphore(f"cc{i}")) for i in range(self.n_coll)]

        streams = {e: [o for o in ops if o["engine"] == e] for e in ENGINES}
        with nc.Block() as block:
            for ename in ENGINES:
                if not streams[ename]:
                    continue

                def body(eng, ename=ename):
                    watermark = {}
                    for op in streams[ename]:
                        for k in sorted(op["need"], key=str):
                            cnt = op["need"][k]
                            if k[0] == "c":
                                sem, val = csem[k[1]], cnt
                            elif k[0] == "d":
                                sem, val = dsem[k[1]], 16 * cnt
                            else:
                                sem, val = collsems[k[1]], 1
                            if watermark.get(k, -1) >= val:
                                continue
                            eng.wait_ge(sem, val)
                            watermark[k] = val
                        if op["kind"] == "barrier":
                            nc.gpsimd.memset(barrier_tile[:], 0.0).then_inc(csem[ename], 1)
                        elif op["kind"] == "coll":
                            op["fn"](eng).then_inc(collsems[op["coll"]], 1)
                        elif op["kind"] == "d":
                            op["fn"](eng).then_inc(dsem[ename], 16)
                        else:
                            op["fn"](eng).then_inc(csem[ename], 1)
                    if ename == "gpsimd":
                        for e2 in ENGINES:
                            if e2 != "gpsimd" and ccount[e2]:
                                eng.wait_ge(csem[e2], ccount[e2])
                            if dcount[e2]:
                                eng.wait_ge(dsem[e2], 16 * dcount[e2])
                        for cs_ in collsems:
                            eng.wait_ge(cs_, 1)

                getattr(block, ename)(body)


def _rope_l0():
    inv = 1.0 / (10000.0 ** (np.arange(0, HD, 2, dtype=np.float64) / HD))
    pos = np.arange(T, dtype=np.float64)
    fr = pos[:, None] * inv[None, :]
    return np.cos(fr).astype(np.float32), np.sin(fr).astype(np.float32)


def _band_tiles():
    out = np.zeros((len(_BAND_DELTAS), 128, 512), dtype=np.float32)
    pi = np.arange(128)[:, None]
    j = np.arange(512)[None, :]
    for kk, dl in enumerate(_BAND_DELTAS):
        out[kk] = ((dl + pi - j) <= HEAVY).astype(np.float32)
    return out.astype(BF)


def bslices(full):
    out = []
    for c in range(NCORE):
        out.append(np.ascontiguousarray(np.concatenate(
            [full[0][1024 * c:1024 * (c + 1)], full[1][1024 * c:1024 * (c + 1)]], axis=0)))
    return out


def bgather(slc, dtype=np.float32):
    shp = slc[0].shape[1:]
    out = np.empty((B, T) + shp, dtype=dtype)
    for c in range(NCORE):
        out[0][1024 * c:1024 * (c + 1)] = slc[c][:1024]
        out[1][1024 * c:1024 * (c + 1)] = slc[c][1024:]
    return out


def _a2a(e, ins, outs):
    with allow_ici_all_to_all():
        return e.collective_compute("AllToAll", ALU.bypass, replica_groups=WORLD,
                                    ins=ins, outs=outs)


def _ag(e, ins, outs):
    return e.collective_compute("AllGather", ALU.bypass, replica_groups=WORLD,
                                ins=ins, outs=outs)


def build_fused(phase_limit="p1", dbg=False):
    nc = bass.Bass("TRN2", target_bir_lowering=False)

    x_in = nc.dram_tensor("x", [TPC, D], F32, kind="ExternalInput").ap()
    wq_in = nc.dram_tensor("wq", [D, D], F32, kind="ExternalInput").ap()
    wk_in = nc.dram_tensor("wk", [D, D], F32, kind="ExternalInput").ap()
    wv_in = nc.dram_tensor("wv", [D, D], F32, kind="ExternalInput").ap()
    wo_in = nc.dram_tensor("wo", [D, D], F32, kind="ExternalInput").ap()
    wfc_in = nc.dram_tensor("wfc", [4 * D, D], F32, kind="ExternalInput").ap()
    wpr_in = nc.dram_tensor("wproj", [D, 4 * D], F32, kind="ExternalInput").ap()
    cos_in = nc.dram_tensor("cos", [TPC, 64], F32, kind="ExternalInput").ap()
    sin_in = nc.dram_tensor("sin", [TPC, 64], F32, kind="ExternalInput").ap()
    band_in = nc.dram_tensor("bandbit", [len(_BAND_DELTAS), 128, 512], BF16, kind="ExternalInput").ap()
    o_x1 = nc.dram_tensor("x1", [TPC, D], F32, kind="ExternalOutput").ap()

    xn_scr = nc.dram_tensor("xn_scr", [TPC, D], BF16).ap()
    qa_in = nc.dram_tensor("qa_in", [8 * 512, 1024], BF16).ap()
    qa_out = nc.dram_tensor("qa_out", [8 * 512, 1024], BF16).ap()
    ka_in = nc.dram_tensor("ka_in", [8 * 512, 1024], BF16).ap()
    ka_out = nc.dram_tensor("ka_out", [8 * 512, 1024], BF16).ap()
    va_in = nc.dram_tensor("va_in", [8 * 1024, 512], BF16).ap()
    va_out = nc.dram_tensor("va_out", [8 * 1024, 512], BF16).ap()
    qm_in = nc.dram_tensor("qm_in", [32, 8], F32).ap()
    qm_out = nc.dram_tensor("qm_out", [32, 8], F32).ap()
    kn_in = nc.dram_tensor("kn_in", [32, 1], F32).ap()
    kn_out = nc.dram_tensor("kn_out", [32, 1], F32).ap()
    at_in = nc.dram_tensor("at_in", [T, 512], BF16).ap()
    at_out = nc.dram_tensor("at_out", [T, 512], BF16).ap()
    cs_loc = nc.dram_tensor("cs_loc", [4, 16, 512], F32).ap()
    hb_dram = nc.dram_tensor("hb_scr", [4, 32, 256], BF16).ap()
    h_scr = nc.dram_tensor("h_scr", [TPC, 4 * D], BF16).ap()
    t1t_scr = nc.dram_tensor("t1t", [16, 64, 128, 128], BF16).ap()
    t2t_scr = nc.dram_tensor("t2t", [64, 16, 128, 128], BF16).ap()
    x2t_scr = nc.dram_tensor("x2t", [64, 16, 128, 128], BF16).ap()

    dbg_outs = {}
    if dbg:
        for nm, shp, dt in [("d_xn", [TPC, D], BF16),
                            ("d_cs", [4, 16, 512], F32),
                            ("d_qm", [32, 8], F32),
                            ("d_kn", [32, 1], F32),
                            ("d_at", [T, 512], BF16)]:
            dbg_outs[nm] = nc.dram_tensor(nm, shp, dt, kind="ExternalOutput").ap()

    sc = Sched2()
    with ExitStack() as es:
        sb = lambda n, s, d: es.enter_context(nc.sbuf_tensor(n, s, d))
        ps = lambda n, s, d=F32: es.enter_context(nc.psum_tensor(n, s, d))

        AR0_ = sb("AR0", [128, 32768], BF16)
        AR1_ = sb("AR1", [128, 32768], BF16)
        ARF_ = sb("ARF", [128, 8192], F32)
        identb = sb("identb", [128, 128], BF16)
        identf = sb("identf", [128, 128], F32)
        c01 = sb("c01", [128, 128], BF16)
        btile = sb("btile", [1, 1], F32)
        epsb5 = sb("epsb5", [128, 1], F32)
        epsb6 = sb("epsb6", [128, 1], F32)
        sones = sb("sones", [1, 128], F32)
        onescol = sb("onescol", [128, 1], F32)
        ones16 = sb("ones16", [128, 16], BF16)
        st1 = sb("st1", [128, 1], F32)
        st2 = sb("st2", [128, 1], F32)
        st3 = sb("st3", [128, 1], F32)
        st4 = sb("st4", [128, 1], F32)
        nbuf = sb("nbuf", [128, NH], F32)
        kn2acc = sb("kn2acc", [128, 2 * NH], F32)
        qmaxT = sb("qmaxT", [16, 16], F32)
        ntrash = sb("ntrash", [128, 128], F32)
        scos = sb("scos", [128, 64], F32)
        ssin = sb("ssin", [128, 64], F32)
        xnb = sb("xnb", [128, D], BF16)
        wload = sb("wload", [128, D], BF16)
        outb = sb("outb", [128, D], BF16)
        tmp64 = ARF_[:, 2048:3072]
        tpose = sb("tpose", [128, 512], BF16)

        PS0 = ps("PS0", [128, 512])
        PS1 = ps("PS1", [128, 512])
        PS2 = ps("PS2", [128, 512])
        PS3 = ps("PS3", [128, 512])
        PSB0 = es.enter_context(nc.psum_tensor("PSB0", [128, 512], BF16))
        PSB1 = es.enter_context(nc.psum_tensor("PSB1", [128, 512], BF16))
        PSCS = ps("PSCS", [16, 512])

        xnT = AR0_[:].rearrange("p (c t) -> p c t", c=16)
        wt = AR1_[:].rearrange("p (c t) -> p c t", c=16)
        sx = ARF_[:, 0:2048]
        strash = ARF_[:, 2048:4096]
        qt = ARF_[:, 4096:6144]
        qrot = ARF_[:, 6144:8192]

        def PSB(k):
            return (PSB0, "PSB0") if k % 2 == 0 else (PSB1, "PSB1")

        sc.add("gpsimd", lambda e: nc.gpsimd.memset(identb[:], 0.0), r=[], w=["identb"])
        sc.add("gpsimd", lambda e: nc.gpsimd.affine_select(
            out=identb[:], in_=identb[:], compare_op=ALU.not_equal,
            fill=1.0, base=0, pattern=[[-1, 128]], channel_multiplier=1),
            r=["identb"], w=["identb"])
        sc.add("gpsimd", lambda e: nc.gpsimd.memset(identf[:], 0.0), r=[], w=["identf"])
        sc.add("gpsimd", lambda e: nc.gpsimd.affine_select(
            out=identf[:], in_=identf[:], compare_op=ALU.not_equal,
            fill=1.0, base=0, pattern=[[-1, 128]], channel_multiplier=1),
            r=["identf"], w=["identf"])
        sc.add("gpsimd", lambda e: nc.gpsimd.memset(c01[:], 1.0), r=[], w=["c01"])
        sc.add("gpsimd", lambda e: nc.gpsimd.affine_select(
            out=c01[:], in_=c01[:], compare_op=ALU.is_ge,
            fill=0.0, base=0, pattern=[[-1, 128]], channel_multiplier=1),
            r=["c01"], w=["c01"])
        sc.add("gpsimd", lambda e: nc.gpsimd.memset(epsb5[:], 1e-5), r=[], w=["epsb5"])
        sc.add("gpsimd", lambda e: nc.gpsimd.memset(epsb6[:], 1e-6), r=[], w=["epsb6"])
        sc.add("gpsimd", lambda e: nc.gpsimd.memset(sones[:], 1.0), r=[], w=["sones"])
        sc.add("gpsimd", lambda e: nc.gpsimd.memset(onescol[:], 1.0), r=[], w=["onescol"])
        sc.add("gpsimd", lambda e: nc.gpsimd.memset(ones16[:], 1.0), r=[], w=["ones16"])
        sc.add("vector", lambda e: nc.vector.memset(kn2acc[:], 0.0), r=[], w=["kn2acc"])

        for i in range(NT):
            sc.add("sync", lambda e, i=i: e.dma_start(out=sx[:], in_=x_in[i * 128:(i + 1) * 128, :]),
                   r=["sx_done"], w=["sx"], dma=True)
            sc.add("vector", lambda e: nc.vector.reduce_sum(st1[:], sx[:], axis=AX.X),
                   r=["sx"], w=["st1"])
            sc.add("scalar", lambda e: nc.scalar.activation(strash[:], sx[:], AF.Square, accum_out=st2[:]),
                   r=["sx"], w=["strash", "st2"])
            sc.add("vector", lambda e: nc.vector.tensor_scalar_mul(st1[:], st1[:], 1.0 / D),
                   r=["st1"], w=["st1"])
            sc.add("vector", lambda e: nc.vector.tensor_scalar_mul(st2[:], st2[:], 1.0 / D),
                   r=["st2"], w=["st2"])
            sc.add("vector", lambda e: nc.vector.tensor_mul(st3[:], st1[:], st1[:]),
                   r=["st1"], w=["st3"])
            sc.add("vector", lambda e: nc.vector.tensor_sub(st2[:], st2[:], st3[:]),
                   r=["st2", "st3"], w=["st2"])
            sc.add("scalar", lambda e: nc.scalar.activation(st3[:], st2[:], AF.Sqrt, bias=epsb5[:]),
                   r=["st2", "epsb5"], w=["st3"])
            sc.add("vector", lambda e: nc.vector.reciprocal(st4[:], st3[:]),
                   r=["st3"], w=["st4"])
            sc.add("vector", lambda e: nc.vector.scalar_tensor_tensor(
                st3[:], st1[:], -1.0, st4[:], op0=ALU.mult, op1=ALU.mult),
                r=["st1", "st4"], w=["st3"])
            sc.add("vector", lambda e: nc.vector.scalar_tensor_tensor(
                xnb[:], sx[:], st4[:], st3[:].broadcast_to([128, D]),
                op0=ALU.mult, op1=ALU.add),
                r=["sx", "st4", "st3"], w=["xnb", "sx_done"])
            sc.add("gpsimd", lambda e, i=i: e.dma_start(
                out=xn_scr[i * 128:(i + 1) * 128, :], in_=xnb[:]),
                r=["xnb"], w=["xn_scr"], dma=True)
            for c0 in range(0, 16, 4):
                pb, pbt = PSB(c0 // 4)
                for j in range(4):
                    c = c0 + j
                    sc.add("tensor", lambda e, c=c, j=j, pb=pb: nc.tensor.transpose(
                        pb[:, j * 128:(j + 1) * 128], xnb[:, c * 128:(c + 1) * 128], identb[:]),
                        r=["xnb", "identb"], w=[pbt])
                for j in range(4):
                    c = c0 + j
                    sc.add("scalar", lambda e, c=c, j=j, i=i, pb=pb: nc.scalar.copy(
                        xnT[:, c, i * 128:(i + 1) * 128], pb[:, j * 128:(j + 1) * 128]),
                        r=[pbt], w=[f"xnT:{i}"])

        def w_phase(w_ap, kind):
            wtok = f"wt_{kind}"
            for oc in range(16):
                sc.add("gpsimd", lambda e, oc=oc: e.dma_start(
                    out=wload[:], in_=w_ap[oc * 128:(oc + 1) * 128, :]),
                    r=["wload_done"], w=["wload"], dma=True)
                for c0 in range(0, 16, 4):
                    pb, pbt = PSB(c0 // 4)
                    for j in range(4):
                        hc = c0 + j
                        sc.add("tensor", lambda e, hc=hc, j=j, pb=pb: nc.tensor.transpose(
                            pb[:, j * 128:(j + 1) * 128], wload[:, hc * 128:(hc + 1) * 128], identb[:]),
                            r=["wload", "identb"],
                            w=[pbt] + (["wload_done"] if c0 == 12 else []))
                    for j in range(4):
                        hc = c0 + j
                        sc.add("scalar", lambda e, hc=hc, j=j, oc=oc, pb=pb: nc.scalar.copy(
                            wt[:, hc, oc * 128:(oc + 1) * 128], pb[:, j * 128:(j + 1) * 128]),
                            r=[pbt], w=[wtok])
            for i in range(NT):
                bt = i // 8
                tl = i % 8
                if kind in ("q", "k"):
                    sc.add("sync", lambda e, i=i: e.dma_start(out=scos[:], in_=cos_in[i * 128:(i + 1) * 128, :]),
                           r=["rope_done"], w=["scos"], dma=True)
                    sc.add("sync", lambda e, i=i: e.dma_start(out=ssin[:], in_=sin_in[i * 128:(i + 1) * 128, :]),
                           r=["rope_done"], w=["ssin"], dma=True)
                for b4 in range(4):
                    pm = PS0 if b4 % 2 == 0 else PS1
                    pmt = "PS0" if b4 % 2 == 0 else "PS1"
                    for hc in range(16):
                        sc.add("tensor", lambda e, i=i, b4=b4, hc=hc, pm=pm: nc.tensor.matmul(
                            pm[:], xnT[:, hc, i * 128:(i + 1) * 128],
                            wt[:, hc, b4 * 512:(b4 + 1) * 512],
                            start=(hc == 0), stop=(hc == 15)),
                            r=[f"xnT:{i}", wtok], w=[pmt])
                    sc.add("scalar", lambda e, b4=b4, pm=pm: nc.scalar.copy(
                        qt[:, b4 * 512:(b4 + 1) * 512], pm[:]),
                        r=[pmt], w=["qt"])
                if kind in ("q", "k"):
                    qv = qt.rearrange("p (h d2) -> p h d2", h=NH)
                    rv = qrot.rearrange("p (h d2) -> p h d2", h=NH)
                    tv = tmp64[:].rearrange("p (h d2) -> p h d2", h=NH)
                    cosb = scos[:].unsqueeze(1).broadcast_to([128, NH, 64])
                    sinb = ssin[:].unsqueeze(1).broadcast_to([128, NH, 64])
                    sc.add("vector", lambda e: nc.vector.tensor_mul(rv[:, :, 0:64], qv[:, :, 0:64], cosb),
                           r=["qt", "scos"], w=["qrot"])
                    sc.add("vector", lambda e: nc.vector.tensor_mul(tv, qv[:, :, 64:128], sinb),
                           r=["qt", "ssin"], w=["strash"])
                    sc.add("vector", lambda e: nc.vector.tensor_sub(rv[:, :, 0:64], rv[:, :, 0:64], tv),
                           r=["qrot", "strash"], w=["qrot"])
                    sc.add("vector", lambda e: nc.vector.tensor_mul(rv[:, :, 64:128], qv[:, :, 64:128], cosb),
                           r=["qt", "scos"], w=["qrot"])
                    sc.add("vector", lambda e: nc.vector.tensor_mul(tv, qv[:, :, 0:64], sinb),
                           r=["qt", "ssin"], w=["strash", "rope_done"])
                    sc.add("vector", lambda e: nc.vector.tensor_add(rv[:, :, 64:128], rv[:, :, 64:128], tv),
                           r=["qrot", "strash"], w=["qrot"])
                    for h in range(NH):
                        sc.add("scalar", lambda e, h=h: nc.scalar.activation(
                            ntrash[:], qrot[:, h * 128:(h + 1) * 128], AF.Square,
                            accum_out=nbuf[:, h:h + 1]),
                            r=["qrot"], w=["ntrash", "nbuf"])
                    if kind == "q":
                        sc.add("tensor", lambda e: nc.tensor.transpose(
                            PS2[0:16, 0:128], nbuf[:], identf[:]),
                            r=["nbuf", "identf"], w=["PS2"])
                        sc.add("scalar", lambda e: nc.scalar.copy(ntrash[0:16, 0:128], PS2[0:16, 0:128]),
                               r=["PS2"], w=["ntrash"])
                        sc.add("vector", lambda e, i=i: nc.vector.reduce_max(
                            qmaxT[:, i:i + 1], ntrash[0:16, 0:128], axis=AX.X),
                            r=["ntrash"], w=["qmaxT"])
                    else:
                        sc.add("vector", lambda e, bt=bt: nc.vector.tensor_max(
                            kn2acc[:, bt * 16:(bt + 1) * 16], kn2acc[:, bt * 16:(bt + 1) * 16], nbuf[:]),
                            r=["nbuf", "kn2acc"], w=["kn2acc"])
                    sc.add("vector", lambda e: nc.vector.tensor_copy(outb[:], qrot[:]),
                           r=["qrot"], w=["outb"])
                else:
                    sc.add("vector", lambda e: nc.vector.tensor_copy(outb[:], qt[:]),
                           r=["qt"], w=["outb"])
                if kind == "v":
                    for g in range(4):
                        sc.add("gpsimd", lambda e, i=i, bt=bt, tl=tl, g=g: e.dma_start(
                            out=va_in[(4 * bt + g) * 1024 + tl * 128:(4 * bt + g) * 1024 + (tl + 1) * 128,
                                      0:512],
                            in_=outb[:, g * 512:(g + 1) * 512]),
                            r=["outb"], w=["va_in"], dma=True)
                else:
                    dst = qa_in if kind == "q" else ka_in
                    dtok = "qa_in" if kind == "q" else "ka_in"
                    for c0 in range(0, 16, 4):
                        pb, pbt = PSB(c0 // 4)
                        for j in range(4):
                            hc = c0 + j
                            sc.add("tensor", lambda e, hc=hc, j=j, pb=pb: nc.tensor.transpose(
                                pb[:, j * 128:(j + 1) * 128], outb[:, hc * 128:(hc + 1) * 128], identb[:]),
                                r=["outb", "identb"], w=[pbt])
                        sc.add("scalar", lambda e, pb=pb: nc.scalar.copy(
                            tpose[:], pb[:]), r=[pbt], w=["tpose"])
                        for j in range(4):
                            hc = c0 + j
                            g, hj = hc // 4, hc % 4
                            row0 = (4 * bt + g) * 512 + hj * 128
                            sc.add("gpsimd", lambda e, row0=row0, j=j, tl=tl, dst=dst: e.dma_start(
                                out=dst[row0:row0 + 128, tl * 128:(tl + 1) * 128],
                                in_=tpose[:, j * 128:(j + 1) * 128]),
                                r=["tpose"], w=[dtok], dma=True)

        w_phase(wq_in, "q")
        w_phase(wk_in, "k")
        w_phase(wv_in, "v")

        for j8 in range(8):
            g, bj = j8 % 4, j8 // 4
            sc.add("gpsimd", lambda e, j8=j8, g=g, bj=bj: e.dma_start(
                out=qm_in[j8 * 4:(j8 + 1) * 4, :],
                in_=qmaxT[4 * g:4 * g + 4, bj * 8:(bj + 1) * 8]),
                r=["qmaxT"], w=["qm_in"], dma=True)
        sc.add("tensor", lambda e: nc.tensor.transpose(PS2[0:32, 0:128], kn2acc[:, 0:32], identf[:]),
               r=["kn2acc", "identf"], w=["PS2"])
        sc.add("scalar", lambda e: nc.scalar.copy(ntrash[0:32, 0:128], PS2[0:32, 0:128]),
               r=["PS2"], w=["ntrash"])
        sc.add("vector", lambda e: nc.vector.reduce_max(nbuf[0:32, 0:1], ntrash[0:32, 0:128], axis=AX.X),
               r=["ntrash"], w=["nbuf"])
        for j8 in range(8):
            g, bj = j8 % 4, j8 // 4
            sc.add("gpsimd", lambda e, j8=j8, g=g, bj=bj: e.dma_start(
                out=kn_in[j8 * 4:(j8 + 1) * 4, :],
                in_=nbuf[bj * 16 + 4 * g: bj * 16 + 4 * g + 4, 0:1]),
                r=["nbuf"], w=["kn_in"], dma=True)

        sc.coll(lambda e: _a2a(e, [qa_in.opt()], [qa_out.opt()]), r=["qa_in"], w=["qa_out"])
        sc.coll(lambda e: _a2a(e, [ka_in.opt()], [ka_out.opt()]), r=["ka_in"], w=["ka_out"])
        sc.coll(lambda e: _a2a(e, [va_in.opt()], [va_out.opt()]), r=["va_in"], w=["va_out"])
        sc.coll(lambda e: _a2a(e, [qm_in.opt()], [qm_out.opt()]), r=["qm_in"], w=["qm_out"])
        sc.coll(lambda e: _a2a(e, [kn_in.opt()], [kn_out.opt()]), r=["kn_in"], w=["kn_out"])

        if dbg:
            sc.add("gpsimd", lambda e: e.dma_start(out=dbg_outs["d_xn"], in_=xn_scr),
                   r=["xn_scr"], w=[], dma=True)
        if phase_limit == "p0":
            sc.add("gpsimd", lambda e: e.dma_start(out=o_x1[0:128, :], in_=sx[:]),
                   r=["sx"], w=["o_x1"], dma=True)
            sc.emit(nc, es, btile)
            return nc

        sc.barrier()

        ktb = AR1_[:, 0:8192]
        qtb = AR1_[:, 8192:16384]
        heavyb = AR1_[:, 16384:24576]
        estrip = AR0_[:, 0:8192]
        vext_ = AR0_[:, 8192:16640]
        vext = vext_.rearrange("p (c xx) -> p c xx", c=64)
        ediag = sb("ediag", [128, 128], BF16)
        sbig = sb("sbig", [128, 272], BF16)
        cmb = sb("cmb", [128, 512], BF16)
        ptsb = sb("ptsb", [128, 512], BF16)
        attnt = sb("attnt", [128, 128], BF16)
        dslots = sb("dslots", [128, 16], F32)
        scs = sb("scs", [16, 512], F32)
        qmrow = sb("qmrow", [1, 256], F32)
        knrow = sb("knrow", [1, 32], F32)
        kmax4 = sb("kmax4", [1, 4], F32)
        s11 = sb("s11", [1, 1], F32)
        negmb = st3
        dsum3 = sb("dsum3", [128, 1], F32)

        sc.add("sync", lambda e: e.dma_start(
            out=qmrow[:], in_=qm_out.rearrange("a b -> (a b)").unsqueeze(0)),
            r=["qm_out"], w=["qmrow"], dma=True)
        sc.add("sync", lambda e: e.dma_start(
            out=knrow[:], in_=kn_out.rearrange("a b -> (a b)").unsqueeze(0)),
            r=["kn_out"], w=["knrow"], dma=True)
        knv = knrow[:].rearrange("o (r j) -> o j r", r=8)
        sc.add("vector", lambda e: nc.vector.reduce_max(kmax4[:], knv, axis=AX.X),
               r=["knrow"], w=["kmax4"])
        sc.add("scalar", lambda e: nc.scalar.activation(kmax4[:], kmax4[:], AF.Sqrt),
               r=["kmax4"], w=["kmax4"])

        def mk_negm(jj, t, rtok="negmb"):
            idx = (4 * (t // 8) + jj) * 8 + (t % 8)
            sc.add("scalar", lambda e, idx=idx: nc.scalar.activation(
                s11[:], qmrow[0:1, idx:idx + 1], AF.Sqrt),
                r=["qmrow"], w=["s11"])
            sc.add("vector", lambda e, jj=jj: nc.vector.scalar_tensor_tensor(
                s11[:], s11[:], -ISCALE, kmax4[0:1, jj:jj + 1], op0=ALU.mult, op1=ALU.mult),
                r=["s11", "kmax4"], w=["s11"])
            sc.add("tensor", lambda e: nc.tensor.matmul(PS3[:, 0:1], sones[:], s11[:], start=True, stop=True),
                   r=["sones", "s11"], w=["PS3"])
            sc.add("scalar", lambda e: nc.scalar.copy(negmb[:], PS3[:, 0:1]),
                   r=["PS3"], w=[rtok])

        def load_pair(jj, dst, dtok, src, stok):
            for rho in range(8):
                sc.add("sync", lambda e, rho=rho, jj=jj, dst=dst, src=src: e.dma_start(
                    out=dst[:, rho * 1024:(rho + 1) * 1024],
                    in_=src[rho * 512 + jj * 128: rho * 512 + (jj + 1) * 128, :]),
                    r=[stok], w=[dtok], dma=True)

        for jj in range(4):
            load_pair(jj, ktb, "ktb", ka_out, "ka_out")
            load_pair(jj, qtb, "qtb", qa_out, "qa_out")
            first_cs = [True]
            for t in range(64):
                L = (t + 1) * 128
                nch = (L + 511) // 512
                dlim = L - 128
                mk_negm(jj, t)
                lhs = qtb[:, t * 128:(t + 1) * 128]
                for c in range(nch):
                    a = c * 512
                    w_ = min(512, L - a)
                    pq = PS0 if (c % 2 == 0) else PS1
                    pqt = "PS0" if (c % 2 == 0) else "PS1"
                    sc.add("tensor", lambda e, a=a, w_=w_, pq=pq, lhs=lhs: nc.tensor.matmul(
                        pq[:, 0:w_], lhs, ktb[:, a:a + w_], start=True, stop=True),
                        r=["qtb", "ktb"], w=[pqt])
                    if a + w_ <= dlim:
                        sc.add("scalar", lambda e, a=a, w_=w_, pq=pq, c=c: nc.scalar.activation(
                            estrip[:, a:a + w_], pq[:, 0:w_], AF.Exp,
                            bias=negmb[:], scale=ISCALE, accum_out=dslots[:, c:c + 1]),
                            r=[pqt, "negmb"], w=["estrip", "dslots"])
                    else:
                        wm = dlim - a
                        if wm > 0:
                            sc.add("scalar", lambda e, a=a, wm=wm, pq=pq, c=c: nc.scalar.activation(
                                estrip[:, a:a + wm], pq[:, 0:wm], AF.Exp,
                                bias=negmb[:], scale=ISCALE, accum_out=dslots[:, c:c + 1]),
                                r=[pqt, "negmb"], w=["estrip", "dslots"])
                        else:
                            sc.add("vector", lambda e, c=c: nc.vector.memset(dslots[:, c:c + 1], 0.0),
                                   r=[], w=["dslots"])
                        sc.add("scalar", lambda e, wm=wm, w_=w_, pq=pq: nc.scalar.activation(
                            ediag[:], pq[:, wm:w_], AF.Exp, bias=negmb[:], scale=ISCALE),
                            r=[pqt, "negmb"], w=["ediag"])
                        sc.add("vector", lambda e, a=a, wm=wm, w_=w_: nc.vector.tensor_mul(
                            estrip[:, a + wm:a + w_], ediag[:], c01[:]),
                            r=["ediag", "c01"], w=["estrip"])
                        sc.add("vector", lambda e, a=a, wm=wm, w_=w_: nc.vector.reduce_sum(
                            dsum3[:], estrip[:, a + wm:a + w_], axis=AX.X),
                            r=["estrip"], w=["dsum3"])
                sc.add("vector", lambda e, nch=nch: nc.vector.reduce_sum(
                    st1[:], dslots[:, 0:nch], axis=AX.X), r=["dslots"], w=["st1"])
                sc.add("vector", lambda e: nc.vector.tensor_add(st1[:], st1[:], dsum3[:]),
                       r=["st1", "dsum3"], w=["st1"])
                sc.add("vector", lambda e: nc.vector.reciprocal(st2[:], st1[:]),
                       r=["st1"], w=["st2"])
                sbv = sbig[:].rearrange("p (c xx) -> p c xx", c=16)
                sc.add("vector", lambda e: nc.vector.memset(sbig[:], 0.0), r=[], w=["sbig"])
                sc.add("vector", lambda e, sbv=sbv: nc.vector.tensor_scalar(
                    sbv[:, :, 0:1], ones16[:].unsqueeze(2), st2[:], None, op0=ALU.mult),
                    r=["ones16", "st2"], w=["sbig"])
                for c in range(nch):
                    a = c * 512
                    w_ = min(512, L - a)
                    sc.add("tensor", lambda e, c=c, a=a, w_=w_, f=first_cs[0],
                           s=(t == 63 and c == nch - 1): nc.tensor.matmul(
                        PSCS[:, 0:w_], sbig[:, 16 * c:16 * c + 16], estrip[:, a:a + w_],
                        start=f, stop=s, skip_group_check=True),
                        r=["sbig", "estrip"], w=["PSCS"])
                    first_cs[0] = False
            sc.add("scalar", lambda e: nc.scalar.copy(scs[:], PSCS[:]), r=["PSCS"], w=["scs"])
            sc.add("gpsimd", lambda e, jj=jj: e.dma_start(out=cs_loc[jj], in_=scs[:]),
                   r=["scs"], w=["cs_loc"], dma=True)

        selh = sb("selh", [128, 4], F32)
        selhT = sb("selhT", [4, 128], F32)
        scsb = sb("scsb", [128, 256], F32)
        slo = sb("slo", [128, 1], F32)
        shi = sb("shi", [128, 1], F32)
        smid = sb("smid", [128, 1], F32)
        cpart = sb("cpart", [128, 1], F32)
        scnt = sb("scnt", [4, 1], F32)
        srep = sb("srep", [128, 1], F32)
        condm = sb("condm", [128, 1], U8)
        getrash = sb("getrash", [128, 256], BF16)
        hbbis = sb("hbbis", [128, 256], BF16)
        sc.add("gpsimd", lambda e: nc.gpsimd.memset(selh[:], 1.0), r=[], w=["selh"])
        sc.add("gpsimd", lambda e: nc.gpsimd.affine_select(
            out=selh[:], in_=selh[:], compare_op=ALU.is_ge,
            fill=0.0, base=0, pattern=[[-32, 4]], channel_multiplier=1),
            r=["selh"], w=["selh"])
        sc.add("gpsimd", lambda e: nc.gpsimd.affine_select(
            out=selh[:], in_=selh[:], compare_op=ALU.is_ge,
            fill=0.0, base=31, pattern=[[32, 4]], channel_multiplier=-1),
            r=["selh"], w=["selh"])
        sc.add("gpsimd", lambda e: nc.gpsimd.memset(selhT[:], 1.0), r=[], w=["selhT"])
        sc.add("gpsimd", lambda e: nc.gpsimd.affine_select(
            out=selhT[:], in_=selhT[:], compare_op=ALU.is_ge,
            fill=0.0, base=0, pattern=[[1, 128]], channel_multiplier=-32),
            r=["selhT"], w=["selhT"])
        sc.add("gpsimd", lambda e: nc.gpsimd.affine_select(
            out=selhT[:], in_=selhT[:], compare_op=ALU.is_ge,
            fill=0.0, base=31, pattern=[[-1, 128]], channel_multiplier=32),
            r=["selhT"], w=["selhT"])
        for jj in range(4):
            sc.add("sync", lambda e, jj=jj: e.dma_start(
                out=scsb[jj * 32:(jj + 1) * 32, :],
                in_=cs_loc[jj].rearrange("p (a xx) -> (p a) xx", a=2)),
                r=["cs_loc"], w=["scsb"], dma=True)
        sc.add("vector", lambda e: nc.vector.memset(slo[:], 0.0), r=[], w=["slo"])
        sc.add("vector", lambda e: nc.vector.memset(shi[:], 8200.0), r=[], w=["shi"])
        for _ in range(45):
            sc.add("vector", lambda e: nc.vector.tensor_add(smid[:], slo[:], shi[:]),
                   r=["slo", "shi"], w=["smid"])
            sc.add("vector", lambda e: nc.vector.tensor_scalar_mul(smid[:], smid[:], 0.5),
                   r=["smid"], w=["smid"])
            sc.add("vector", lambda e: nc.vector.tensor_scalar(
                getrash[:], scsb[:], smid[:], 0.0, op0=ALU.is_ge, op1=ALU.add,
                accum_out=cpart[:]),
                r=["scsb", "smid"], w=["getrash", "cpart"])
            sc.add("tensor", lambda e: nc.tensor.matmul(PSCS[0:4, 0:1], selh[:], cpart[:], start=True, stop=True),
                   r=["selh", "cpart"], w=["PSCS"])
            sc.add("scalar", lambda e: nc.scalar.copy(scnt[:], PSCS[0:4, 0:1]), r=["PSCS"], w=["scnt"])
            sc.add("tensor", lambda e: nc.tensor.matmul(PS3[:, 0:1], selhT[:], scnt[:], start=True, stop=True),
                   r=["selhT", "scnt"], w=["PS3"])
            sc.add("scalar", lambda e: nc.scalar.copy(srep[:], PS3[:, 0:1]), r=["PS3"], w=["srep"])
            sc.add("vector", lambda e: nc.vector.tensor_scalar(
                condm[:], srep[:], 818.5, None, op0=ALU.is_ge), r=["srep"], w=["condm"])
            sc.add("vector", lambda e: nc.vector.select(slo[:], condm[:], smid[:], slo[:]),
                   r=["condm", "smid", "slo"], w=["slo"])
            sc.add("vector", lambda e: nc.vector.select(shi[:], condm[:], shi[:], smid[:]),
                   r=["condm", "smid", "shi"], w=["shi"])
        sc.add("vector", lambda e: nc.vector.tensor_scalar(
            hbbis[:], scsb[:], slo[:], None, op0=ALU.is_ge), r=["scsb", "slo"], w=["hbbis"])
        sc.add("gpsimd", lambda e: e.dma_start(
            out=hb_dram.rearrange("h s j -> (h s) j"), in_=hbbis[:]),
            r=["hbbis"], w=["hb_dram"], dma=True)

        if dbg:
            sc.add("gpsimd", lambda e: e.dma_start(out=dbg_outs["d_cs"], in_=cs_loc),
                   r=["cs_loc"], w=[], dma=True)
            sc.add("gpsimd", lambda e: e.dma_start(out=dbg_outs["d_qm"], in_=qm_out),
                   r=["qm_out"], w=[], dma=True)
            sc.add("gpsimd", lambda e: e.dma_start(out=dbg_outs["d_kn"], in_=kn_out),
                   r=["kn_out"], w=[], dma=True)
        if phase_limit == "pa":
            sc.add("gpsimd", lambda e: e.dma_start(out=o_x1[0:128, :], in_=sx[:]),
                   r=["sx"], w=["o_x1"], dma=True)
            sc.emit(nc, es, btile)
            return nc

        sband = sb("sband", [128, len(_BAND_DELTAS) * 512], BF16)
        for bi in range(len(_BAND_DELTAS)):
            sc.add("sync", lambda e, bi=bi: e.dma_start(
                out=sband[:, bi * 512:(bi + 1) * 512], in_=band_in[bi]),
                r=[], w=["sband"], dma=True)

        for jj in range(4):
            load_pair(jj, ktb, "ktb", ka_out, "ka_out")
            load_pair(jj, qtb, "qtb", qa_out, "qa_out")
            sc.add("sync", lambda e, jj=jj: e.dma_start(
                out=heavyb,
                in_=hb_dram[jj].rearrange("s j -> (s j)").unsqueeze(0).broadcast_to([128, T])),
                r=["hb_dram"], w=["heavyb"], dma=True)
            for rho in range(8):
                sc.add("sync", lambda e, rho=rho, jj=jj: e.dma_start(
                    out=vext[:, rho * 8:(rho + 1) * 8, 0:128],
                    in_=va_out[rho * 1024:(rho + 1) * 1024,
                               jj * 128:(jj + 1) * 128].rearrange("(c p) d2 -> p c d2", p=128)),
                    r=["va_out"], w=["vext"], dma=True)
            sc.add("vector", lambda e: nc.vector.memset(vext[:, :, 128:129], 1.0),
                   r=[], w=["vext"])

            for t in range(64):
                L = (t + 1) * 128
                nch = (L + 511) // 512
                nkc = L // 128
                dlim = L - 128
                mk_negm(jj, t)
                lhs = qtb[:, t * 128:(t + 1) * 128]
                for c in range(nch):
                    a = c * 512
                    w_ = min(512, L - a)
                    pq = PS0 if (c % 2 == 0) else PS1
                    pqt = "PS0" if (c % 2 == 0) else "PS1"
                    sc.add("tensor", lambda e, a=a, w_=w_, pq=pq, lhs=lhs: nc.tensor.matmul(
                        pq[:, 0:w_], lhs, ktb[:, a:a + w_], start=True, stop=True),
                        r=["qtb", "ktb"], w=[pqt])
                    if a + w_ <= dlim:
                        sc.add("scalar", lambda e, a=a, w_=w_, pq=pq: nc.scalar.activation(
                            estrip[:, a:a + w_], pq[:, 0:w_], AF.Exp,
                            bias=negmb[:], scale=ISCALE),
                            r=[pqt, "negmb"], w=["estrip"])
                    else:
                        wm = dlim - a
                        if wm > 0:
                            sc.add("scalar", lambda e, a=a, wm=wm, pq=pq: nc.scalar.activation(
                                estrip[:, a:a + wm], pq[:, 0:wm], AF.Exp,
                                bias=negmb[:], scale=ISCALE),
                                r=[pqt, "negmb"], w=["estrip"])
                        sc.add("scalar", lambda e, wm=wm, w_=w_, pq=pq: nc.scalar.activation(
                            ediag[:], pq[:, wm:w_], AF.Exp, bias=negmb[:], scale=ISCALE),
                            r=[pqt, "negmb"], w=["ediag"])
                        sc.add("vector", lambda e, a=a, wm=wm, w_=w_: nc.vector.tensor_mul(
                            estrip[:, a + wm:a + w_], ediag[:], c01[:]),
                            r=["ediag", "c01"], w=["estrip"])
                    dt_ = 128 * t - a
                    if dt_ >= 1408:
                        sc.add("vector", lambda e, a=a, w_=w_: nc.vector.tensor_mul(
                            estrip[:, a:a + w_], estrip[:, a:a + w_], heavyb[:, a:a + w_]),
                            r=["estrip", "heavyb"], w=["estrip"])
                    elif dt_ in _BAND_DELTAS:
                        bi = _BAND_DELTAS.index(dt_)
                        sc.add("vector", lambda e, a=a, w_=w_, bi=bi: nc.vector.tensor_max(
                            cmb[:, 0:w_], heavyb[:, a:a + w_], sband[:, bi * 512:bi * 512 + w_]),
                            r=["heavyb", "sband"], w=["cmb"])
                        sc.add("vector", lambda e, a=a, w_=w_: nc.vector.tensor_mul(
                            estrip[:, a:a + w_], estrip[:, a:a + w_], cmb[:, 0:w_]),
                            r=["estrip", "cmb"], w=["estrip"])
                for k0 in range(0, nkc, 4):
                    n4 = min(4, nkc - k0)
                    pb, pbt = PSB(k0 // 4)
                    for j in range(n4):
                        kc = k0 + j
                        sc.add("tensor", lambda e, kc=kc, j=j, pb=pb: nc.tensor.transpose(
                            pb[:, j * 128:(j + 1) * 128], estrip[:, kc * 128:(kc + 1) * 128], identb[:]),
                            r=["estrip", "identb"], w=[pbt])
                    sc.add("scalar", lambda e, n4=n4, pb=pb: nc.scalar.copy(
                        ptsb[:, 0:n4 * 128], pb[:, 0:n4 * 128]),
                        r=[pbt], w=["ptsb"])
                    for j in range(n4):
                        kc = k0 + j
                        sc.add("tensor", lambda e, kc=kc, j=j, s=(kc == nkc - 1), f=(kc == 0):
                               nc.tensor.matmul(
                            PS2[:, 0:129], ptsb[:, j * 128:(j + 1) * 128], vext[:, kc, 0:129],
                            start=f, stop=s, skip_group_check=True),
                            r=["ptsb", "vext"], w=["PS2"])
                sc.add("vector", lambda e: nc.vector.tensor_copy(st1[:], PS2[:, 128:129]),
                       r=["PS2"], w=["st1"])
                sc.add("vector", lambda e: nc.vector.reciprocal(st2[:], st1[:]),
                       r=["st1"], w=["st2"])
                sc.add("vector", lambda e: nc.vector.tensor_scalar(
                    attnt[:], PS2[:, 0:128], st2[:], None, op0=ALU.mult),
                    r=["PS2", "st2"], w=["attnt"])
                sc.add("gpsimd", lambda e, jj=jj, t=t: e.dma_start(
                    out=at_in[t * 128:(t + 1) * 128, jj * 128:(jj + 1) * 128], in_=attnt[:]),
                    r=["attnt"], w=["at_in"], dma=True)

        sc.coll(lambda e: _a2a(e, [at_in.opt()], [at_out.opt()]), r=["at_in"], w=["at_out"])

        if dbg:
            sc.add("gpsimd", lambda e: e.dma_start(out=dbg_outs["d_at"], in_=at_out),
                   r=["at_out"], w=[], dma=True)
        if phase_limit == "pb":
            sc.add("gpsimd", lambda e: e.dma_start(out=o_x1[0:128, :], in_=sx[:]),
                   r=["sx"], w=["o_x1"], dma=True)
            sc.emit(nc, es, btile)
            return nc

        sc.barrier()

        wtbuf = AR0_[:].rearrange("p (c t) -> p c t", c=16)
        rhs2 = AR0_[:].rearrange("p (c t) -> p c t", c=64)
        x2rawA = AR1_[:, 0:8192]
        x2rawB = AR1_[:, 8192:16384]
        rhsbA = AR1_[:, 0:8192].rearrange("p (c t) -> p c t", c=16)
        rhsbB = AR1_[:, 8192:16384].rearrange("p (c t) -> p c t", c=16)
        x2tbA = AR1_[:, 0:8192].rearrange("p (c t) -> p c t", c=64)
        x2tbB = AR1_[:, 8192:16384].rearrange("p (c t) -> p c t", c=64)
        hb2 = AR1_[:, 16384:24576]
        g1 = AR1_[:, 24576:32768]
        wcL = ARF_[:]
        ab = xnb
        abT = AR1_[:, 0:2048].rearrange("p (c t) -> p c t", c=16)
        sxf = ARF_[:, 0:2048]
        tb = tpose
        asum = sb("asum", [128, 1], F32)
        sg = sb("sg", [1, 1], F32)
        gb1 = sb("gb1", [128, 1], F32)
        gb2 = sb("gb2", [128, 1], F32)
        thrb = sb("thrb", [128, 1], F32)
        negthr = sb("negthr", [128, 1], F32)
        obf = ARF_[:, 2048:2560]
        g3 = hb2

        for oc in range(16):
            sc.add("gpsimd", lambda e, oc=oc: e.dma_start(
                out=ab[:], in_=wo_in[oc * 128:(oc + 1) * 128, :]),
                r=["ab_done"], w=["ab"], dma=True)
            for c0 in range(0, 16, 4):
                pb, pbt = PSB(c0 // 4)
                for j in range(4):
                    hc = c0 + j
                    sc.add("tensor", lambda e, hc=hc, j=j, pb=pb: nc.tensor.transpose(
                        pb[:, j * 128:(j + 1) * 128], ab[:, hc * 128:(hc + 1) * 128], identb[:]),
                        r=["ab", "identb"],
                        w=[pbt] + (["ab_done"] if c0 == 12 else []))
                for j in range(4):
                    hc = c0 + j
                    sc.add("scalar", lambda e, hc=hc, j=j, oc=oc, pb=pb: nc.scalar.copy(
                        wtbuf[:, hc, oc * 128:(oc + 1) * 128], pb[:, j * 128:(j + 1) * 128]),
                        r=[pbt], w=["wtbuf"])
        for i in range(NT):
            bt, tl = i // 8, i % 8
            for g in range(4):
                row0 = (4 * bt + g) * 1024 + tl * 128
                sc.add("sync", lambda e, row0=row0, g=g: e.dma_start(
                    out=ab[:, g * 512:(g + 1) * 512], in_=at_out[row0:row0 + 128, 0:512]),
                    r=["at_out", "ab_done"], w=["ab"], dma=True)
            sc.add("sync", lambda e, i=i: e.dma_start(out=sxf[:], in_=x_in[i * 128:(i + 1) * 128, :]),
                   r=["sxf_done"], w=["sxf"], dma=True)
            for c0 in range(0, 16, 4):
                pb, pbt = PSB(c0 // 4)
                for j in range(4):
                    hc = c0 + j
                    sc.add("tensor", lambda e, hc=hc, j=j, pb=pb: nc.tensor.transpose(
                        pb[:, j * 128:(j + 1) * 128], ab[:, hc * 128:(hc + 1) * 128], identb[:]),
                        r=["ab", "identb"],
                        w=[pbt] + (["ab_done"] if c0 == 12 else []))
                for j in range(4):
                    hc = c0 + j
                    sc.add("scalar", lambda e, hc=hc, j=j, pb=pb: nc.scalar.copy(
                        abT[:, hc, :], pb[:, j * 128:(j + 1) * 128]),
                        r=[pbt], w=["abT"])
            for b4 in range(4):
                pm = PS0 if b4 % 2 == 0 else PS1
                pmt = "PS0" if b4 % 2 == 0 else "PS1"
                for hc in range(16):
                    sc.add("tensor", lambda e, b4=b4, hc=hc, pm=pm: nc.tensor.matmul(
                        pm[:], abT[:, hc, :], wtbuf[:, hc, b4 * 512:(b4 + 1) * 512],
                        start=(hc == 0), stop=(hc == 15)),
                        r=["abT", "wtbuf"], w=[pmt])
                sc.add("vector", lambda e, b4=b4, pm=pm: nc.vector.tensor_add(
                    sxf[:, b4 * 512:(b4 + 1) * 512], sxf[:, b4 * 512:(b4 + 1) * 512], pm[:]),
                    r=[pmt, "sxf"], w=["sxf"])
            sc.add("gpsimd", lambda e, i=i: e.dma_start(out=o_x1[i * 128:(i + 1) * 128, :], in_=sxf[:]),
                   r=["sxf"], w=["o_x1", "sxf_done"], dma=True)

        sc.barrier()

        sc.add("vector", lambda e: nc.vector.memset(asum[:], 0.0), r=[], w=["asum"])
        for c in range(64):
            sc.add("sync", lambda e, c=c: e.dma_start(out=wcL[:, 0:D], in_=wfc_in[c * 128:(c + 1) * 128, :]),
                   r=["wcL_done"], w=["wcL"], dma=True)
            sc.add("vector", lambda e: nc.vector.reduce_sum(st1[:], wcL[:, 0:D], axis=AX.X, apply_absolute_value=True),
                   r=["wcL"], w=["st1", "wcL_done"])
            sc.add("vector", lambda e: nc.vector.tensor_add(asum[:], asum[:], st1[:]),
                   r=["asum", "st1"], w=["asum"])
        sc.add("tensor", lambda e: nc.tensor.matmul(PS3[0:1, 0:1], asum[:], onescol[:], start=True, stop=True),
               r=["asum", "onescol"], w=["PS3"])
        sc.add("scalar", lambda e: nc.scalar.copy(sg[:], PS3[0:1, 0:1]), r=["PS3"], w=["sg"])
        sc.add("vector", lambda e: nc.vector.tensor_scalar_mul(sg[:], sg[:], 1.0 / (4 * D * D)),
               r=["sg"], w=["sg"])
        sc.add("tensor", lambda e: nc.tensor.matmul(PS3[:, 0:1], sones[:], sg[:], start=True, stop=True),
               r=["sones", "sg"], w=["PS3"])
        sc.add("scalar", lambda e: nc.scalar.copy(gb1[:], PS3[:, 0:1]), r=["PS3"], w=["gb1"])
        sc.add("vector", lambda e: nc.vector.tensor_scalar(
            thrb[:], gb1[:], 0.5, 0.5e-6, op0=ALU.mult, op1=ALU.add), r=["gb1"], w=["thrb"])
        sc.add("vector", lambda e: nc.vector.tensor_scalar_mul(negthr[:], thrb[:], -1.0),
               r=["thrb"], w=["negthr"])
        for c in range(64):
            sc.add("sync", lambda e, c=c: e.dma_start(out=wcL[:, 0:D], in_=wfc_in[c * 128:(c + 1) * 128, :]),
                   r=["wcL_done"], w=["wcL"], dma=True)
            sc.add("vector", lambda e: nc.vector.tensor_scalar(
                g1[:, 0:D], wcL[:, 0:D], thrb[:], None, op0=ALU.is_ge),
                r=["wcL", "thrb"], w=["g1"])
            sc.add("vector", lambda e: nc.vector.tensor_scalar(
                g3[:, 0:D], wcL[:, 0:D], negthr[:], None, op0=ALU.is_le),
                r=["wcL", "negthr"], w=["hb2", "wcL_done"])
            sc.add("vector", lambda e: nc.vector.tensor_sub(g1[:, 0:D], g1[:, 0:D], g3[:, 0:D]),
                   r=["g1", "hb2"], w=["g1"])
            for c0 in range(0, 16, 4):
                pb, pbt = PSB(c0 // 4)
                for j in range(4):
                    dc = c0 + j
                    sc.add("tensor", lambda e, dc=dc, j=j, pb=pb: nc.tensor.transpose(
                        pb[:, j * 128:(j + 1) * 128], g1[:, dc * 128:(dc + 1) * 128], identb[:]),
                        r=["g1", "identb"], w=[pbt])
                sc.add("scalar", lambda e, pb=pb: nc.scalar.copy(tb[:], pb[:]), r=[pbt], w=["tb"])
                for j in range(4):
                    dc = c0 + j
                    sc.add("gpsimd", lambda e, dc=dc, j=j, c=c: e.dma_start(
                        out=t1t_scr[dc, c], in_=tb[:, j * 128:(j + 1) * 128]),
                        r=["tb"], w=["t1t"], dma=True)

        sc.barrier()

        for i in range(NT):
            sc.add("sync", lambda e, i=i: e.dma_start(out=ab[:], in_=xn_scr[i * 128:(i + 1) * 128, :]),
                   r=["ab_done"], w=["ab"], dma=True)
            sc.add("vector", lambda e: nc.vector.reduce_sum(st1[:], ab[:], axis=AX.X),
                   r=["ab"], w=["st1"])
            sc.add("scalar", lambda e: nc.scalar.activation(wcL[:, 0:D], ab[:], AF.Square, accum_out=st2[:]),
                   r=["ab"], w=["wcL", "st2"])
            sc.add("vector", lambda e: nc.vector.tensor_scalar_mul(st1[:], st1[:], 1.0 / D),
                   r=["st1"], w=["st1"])
            sc.add("vector", lambda e: nc.vector.tensor_scalar_mul(st2[:], st2[:], 1.0 / D),
                   r=["st2"], w=["st2"])
            sc.add("vector", lambda e: nc.vector.tensor_mul(st3[:], st1[:], st1[:]),
                   r=["st1"], w=["st3"])
            sc.add("vector", lambda e: nc.vector.tensor_sub(st2[:], st2[:], st3[:]),
                   r=["st2", "st3"], w=["st2"])
            sc.add("scalar", lambda e: nc.scalar.activation(st3[:], st2[:], AF.Sqrt, bias=epsb6[:]),
                   r=["st2", "epsb6"], w=["st3"])
            sc.add("vector", lambda e: nc.vector.reciprocal(st4[:], st3[:]),
                   r=["st3"], w=["st4"])
            sc.add("vector", lambda e: nc.vector.scalar_tensor_tensor(
                st3[:], st1[:], -1.0, st4[:], op0=ALU.mult, op1=ALU.mult),
                r=["st1", "st4"], w=["st3"])
            sc.add("vector", lambda e: nc.vector.scalar_tensor_tensor(
                wcL[:, 0:D], ab[:], st4[:], st3[:].broadcast_to([128, D]), op0=ALU.mult, op1=ALU.add),
                r=["ab", "st4", "st3"], w=["wcL"])
            sc.add("vector", lambda e: nc.vector.reduce_max(st1[:], wcL[:, 0:D], axis=AX.X, apply_absolute_value=True),
                   r=["wcL"], w=["st1"])
            sc.add("vector", lambda e: nc.vector.tensor_scalar_mul(st1[:], st1[:], (128.0 - 1e-6) / 128.0),
                   r=["st1"], w=["st1"])
            sc.add("vector", lambda e: nc.vector.tensor_scalar_mul(st2[:], st1[:], -1.0),
                   r=["st1"], w=["st2"])
            sc.add("vector", lambda e: nc.vector.tensor_scalar(
                ab[:], wcL[:, 0:D], st1[:], st2[:], op0=ALU.min, op1=ALU.max),
                r=["wcL", "st1", "st2"], w=["ab"])
            for c0 in range(0, 16, 4):
                pb, pbt = PSB(c0 // 4)
                for j in range(4):
                    dc = c0 + j
                    sc.add("tensor", lambda e, dc=dc, j=j, pb=pb: nc.tensor.transpose(
                        pb[:, j * 128:(j + 1) * 128], ab[:, dc * 128:(dc + 1) * 128], identb[:]),
                        r=["ab", "identb"],
                        w=[pbt] + (["ab_done"] if c0 == 12 else []))
                for j in range(4):
                    dc = c0 + j
                    sc.add("scalar", lambda e, dc=dc, j=j, i=i, pb=pb: nc.scalar.copy(
                        wtbuf[:, dc, i * 128:(i + 1) * 128], pb[:, j * 128:(j + 1) * 128]),
                        r=[pbt], w=["wtbuf"])

        sc.barrier()

        for os_ in range(16):
            rb = rhsbA if os_ % 2 == 0 else rhsbB
            rt = f"rhsb{os_ % 2}"
            for dc in range(16):
                sc.add("sync", lambda e, os_=os_, dc=dc, rb=rb: e.dma_start(
                    out=rb[:, dc, :].rearrange("p (a o) -> p a o", a=4),
                    in_=t1t_scr[dc, os_ * 4:os_ * 4 + 4].rearrange("a p o -> p a o")),
                    r=["t1t", rt + "_done"], w=[rt], dma=True)
            for i in range(NT):
                pm = PS0 if i % 2 == 0 else PS1
                pmt = "PS0" if i % 2 == 0 else "PS1"
                for dc in range(16):
                    sc.add("tensor", lambda e, i=i, dc=dc, rb=rb, pm=pm: nc.tensor.matmul(
                        pm[:], wtbuf[:, dc, i * 128:(i + 1) * 128], rb[:, dc, :],
                        start=(dc == 0), stop=(dc == 15)),
                        r=["wtbuf", rt], w=[pmt] + ([rt + "_done"] if i == NT - 1 and dc == 15 else []))
                sc.add("scalar", lambda e, pm=pm: nc.scalar.activation(
                    tb[:], pm[:], AF.Gelu, scale=gb1[:]),
                    r=[pmt, "gb1"], w=["tb"])
                sc.add("gpsimd", lambda e, i=i, os_=os_: e.dma_start(
                    out=h_scr[i * 128:(i + 1) * 128, os_ * 512:(os_ + 1) * 512], in_=tb[:]),
                    r=["tb"], w=["h_scr"], dma=True)

        sc.barrier()

        sc.add("vector", lambda e: nc.vector.memset(asum[:], 0.0), r=[], w=["asum"])
        for c in range(16):
            sc.add("sync", lambda e, c=c: e.dma_start(out=wcL[:], in_=wpr_in[c * 128:(c + 1) * 128, :]),
                   r=["wcL_done"], w=["wcL"], dma=True)
            sc.add("vector", lambda e: nc.vector.reduce_sum(st1[:], wcL[:], axis=AX.X, apply_absolute_value=True),
                   r=["wcL"], w=["st1", "wcL_done"])
            sc.add("vector", lambda e: nc.vector.tensor_add(asum[:], asum[:], st1[:]),
                   r=["asum", "st1"], w=["asum"])
        sc.add("tensor", lambda e: nc.tensor.matmul(PS3[0:1, 0:1], asum[:], onescol[:], start=True, stop=True),
               r=["asum", "onescol"], w=["PS3"])
        sc.add("scalar", lambda e: nc.scalar.copy(sg[:], PS3[0:1, 0:1]), r=["PS3"], w=["sg"])
        sc.add("vector", lambda e: nc.vector.tensor_scalar_mul(sg[:], sg[:], 1.0 / (4 * D * D)),
               r=["sg"], w=["sg"])
        sc.add("tensor", lambda e: nc.tensor.matmul(PS3[:, 0:1], sones[:], sg[:], start=True, stop=True),
               r=["sones", "sg"], w=["PS3"])
        sc.add("scalar", lambda e: nc.scalar.copy(gb2[:], PS3[:, 0:1]), r=["PS3"], w=["gb2"])
        sc.add("vector", lambda e: nc.vector.tensor_scalar(
            thrb[:], gb2[:], 0.5, 0.5e-6, op0=ALU.mult, op1=ALU.add), r=["gb2"], w=["thrb"])
        sc.add("vector", lambda e: nc.vector.tensor_scalar_mul(negthr[:], thrb[:], -1.0),
               r=["thrb"], w=["negthr"])
        for c in range(16):
            sc.add("sync", lambda e, c=c: e.dma_start(out=wcL[:], in_=wpr_in[c * 128:(c + 1) * 128, :]),
                   r=["wcL_done"], w=["wcL"], dma=True)
            sc.add("vector", lambda e: nc.vector.tensor_scalar(
                g1[:], wcL[:], thrb[:], None, op0=ALU.is_ge),
                r=["wcL", "thrb"], w=["g1"])
            sc.add("vector", lambda e: nc.vector.tensor_scalar(
                g3[:], wcL[:], negthr[:], None, op0=ALU.is_le),
                r=["wcL", "negthr"], w=["hb2", "wcL_done"])
            sc.add("vector", lambda e: nc.vector.tensor_sub(g1[:], g1[:], g3[:]),
                   r=["g1", "hb2"], w=["g1"])
            for c0 in range(0, 64, 4):
                pb, pbt = PSB(c0 // 4)
                for j in range(4):
                    dc = c0 + j
                    sc.add("tensor", lambda e, dc=dc, j=j, pb=pb: nc.tensor.transpose(
                        pb[:, j * 128:(j + 1) * 128], g1[:, dc * 128:(dc + 1) * 128], identb[:]),
                        r=["g1", "identb"], w=[pbt])
                sc.add("scalar", lambda e, pb=pb: nc.scalar.copy(tb[:], pb[:]), r=[pbt], w=["tb"])
                for j in range(4):
                    dc = c0 + j
                    sc.add("gpsimd", lambda e, dc=dc, j=j, c=c: e.dma_start(
                        out=t2t_scr[dc, c], in_=tb[:, j * 128:(j + 1) * 128]),
                        r=["tb"], w=["t2t"], dma=True)

        sc.barrier()

        for i in range(NT):
            sc.add("sync", lambda e, i=i: e.dma_start(out=hb2[:], in_=h_scr[i * 128:(i + 1) * 128, :]),
                   r=["hb2_done"], w=["hb2"], dma=True)
            sc.add("vector", lambda e: nc.vector.reduce_sum(st1[:], hb2[:], axis=AX.X),
                   r=["hb2"], w=["st1"])
            sc.add("scalar", lambda e: nc.scalar.activation(g1[:], hb2[:], AF.Square, accum_out=st2[:]),
                   r=["hb2"], w=["g1", "st2"])
            sc.add("vector", lambda e: nc.vector.tensor_scalar_mul(st1[:], st1[:], 1.0 / (4 * D)),
                   r=["st1"], w=["st1"])
            sc.add("vector", lambda e: nc.vector.tensor_scalar_mul(st2[:], st2[:], 1.0 / (4 * D)),
                   r=["st2"], w=["st2"])
            sc.add("vector", lambda e: nc.vector.tensor_mul(st3[:], st1[:], st1[:]),
                   r=["st1"], w=["st3"])
            sc.add("vector", lambda e: nc.vector.tensor_sub(st2[:], st2[:], st3[:]),
                   r=["st2", "st3"], w=["st2"])
            sc.add("scalar", lambda e: nc.scalar.activation(st3[:], st2[:], AF.Sqrt, bias=epsb6[:]),
                   r=["st2", "epsb6"], w=["st3"])
            sc.add("vector", lambda e: nc.vector.reciprocal(st4[:], st3[:]),
                   r=["st3"], w=["st4"])
            sc.add("vector", lambda e: nc.vector.scalar_tensor_tensor(
                st3[:], st1[:], -1.0, st4[:], op0=ALU.mult, op1=ALU.mult),
                r=["st1", "st4"], w=["st3"])
            sc.add("vector", lambda e: nc.vector.scalar_tensor_tensor(
                wcL[:], hb2[:], st4[:], st3[:].broadcast_to([128, 4 * D]), op0=ALU.mult, op1=ALU.add),
                r=["hb2", "st4", "st3"], w=["wcL"])
            sc.add("vector", lambda e: nc.vector.reduce_max(st1[:], wcL[:], axis=AX.X, apply_absolute_value=True),
                   r=["wcL"], w=["st1"])
            sc.add("vector", lambda e: nc.vector.tensor_scalar_mul(st1[:], st1[:], (128.0 - 1e-6) / 128.0),
                   r=["st1"], w=["st1"])
            sc.add("vector", lambda e: nc.vector.tensor_scalar_mul(st2[:], st1[:], -1.0),
                   r=["st1"], w=["st2"])
            sc.add("vector", lambda e: nc.vector.tensor_scalar(
                hb2[:], wcL[:], st1[:], st2[:], op0=ALU.min, op1=ALU.max),
                r=["wcL", "st1", "st2"], w=["hb2"])
            for c0 in range(0, 64, 4):
                pb, pbt = PSB(c0 // 4)
                for j in range(4):
                    dc = c0 + j
                    sc.add("tensor", lambda e, dc=dc, j=j, pb=pb: nc.tensor.transpose(
                        pb[:, j * 128:(j + 1) * 128], hb2[:, dc * 128:(dc + 1) * 128], identb[:]),
                        r=["hb2", "identb"],
                        w=[pbt] + (["hb2_done"] if c0 == 60 else []))
                sc.add("scalar", lambda e, pb=pb: nc.scalar.copy(tb[:], pb[:]), r=[pbt], w=["tb"])
                for j in range(4):
                    dc = c0 + j
                    sc.add("gpsimd", lambda e, dc=dc, j=j, i=i: e.dma_start(
                        out=x2t_scr[dc, i], in_=tb[:, j * 128:(j + 1) * 128]),
                        r=["tb"], w=["x2t"], dma=True)

        sc.barrier()

        for os_ in range(4):
            for dc in range(64):
                sc.add("sync", lambda e, os_=os_, dc=dc: e.dma_start(
                    out=rhs2[:, dc, :].rearrange("p (a o) -> p a o", a=4),
                    in_=t2t_scr[dc, os_ * 4:os_ * 4 + 4].rearrange("a p o -> p a o")),
                    r=["t2t", "rhs2_done"], w=["rhs2"], dma=True)
            for i in range(NT):
                xb = x2tbA if i % 2 == 0 else x2tbB
                xraw = x2rawA if i % 2 == 0 else x2rawB
                xt = f"x2tb{i % 2}"
                sc.add("sync", lambda e, i=i, xraw=xraw: e.dma_start(
                    out=xraw.rearrange("p (a o) -> p a o", a=64),
                    in_=x2t_scr[:, i].rearrange("a p o -> p a o")),
                    r=["x2t", xt + "_done"], w=[xt], dma=True)
                pm = PS0 if i % 2 == 0 else PS1
                pmt = "PS0" if i % 2 == 0 else "PS1"
                for dc in range(64):
                    sc.add("tensor", lambda e, i=i, dc=dc, xb=xb, pm=pm: nc.tensor.matmul(
                        pm[:], xb[:, dc, :], rhs2[:, dc, :],
                        start=(dc == 0), stop=(dc == 63)),
                        r=[xt, "rhs2"],
                        w=[pmt] + ([xt + "_done"] if dc == 63 else [])
                          + (["rhs2_done"] if i == NT - 1 and dc == 63 else []))
                sc.add("vector", lambda e, pm=pm: nc.vector.tensor_scalar(
                    obf[:], pm[:], gb2[:], None, op0=ALU.mult),
                    r=[pmt, "gb2"], w=["obf"])
                sc.add("gpsimd", lambda e, i=i, os_=os_: e.dma_start(
                    out=o_x1[i * 128:(i + 1) * 128, os_ * 512:(os_ + 1) * 512], in_=obf[:],
                    accum_op=ALU.add),
                    r=["obf", "o_x1"], w=["o_x1"], dma=True)

        if phase_limit == "p1":
            sc.emit(nc, es, btile)
            return nc

        sc.barrier()

        swqc = AR1_[:, 0:16384].rearrange("p (c t) -> p c t", c=16)
        swqup = AR1_[:, 16384:32768].rearrange("p (c t) -> p c t", c=8)
        swkvb = AR0_[:, 0:2048].rearrange("p (c t) -> p c t", c=16)
        swzb = AR0_[:, 2048:4096].rearrange("p (c t) -> p c t", c=16)
        swkw = AR0_[:, 4096:6144].rearrange("p (c t) -> p c t", c=16)
        swvw = AR0_[:, 6144:8192].rearrange("p (c t) -> p c t", c=16)
        swikv = AR0_[:, 8192:9216].rearrange("p (c t) -> p c t", c=16)
        swiz = AR0_[:, 9216:10240].rearrange("p (c t) -> p c t", c=16)
        swdq = AR0_[:, 10240:11264].rearrange("p (c t) -> p c t", c=16)
        sww = AR0_[:, 11264:11328].rearrange("p (c t) -> p c t", c=16)
        xnTt = AR0_[:, 16384:18432].rearrange("p (c t) -> p c t", c=16)
        hqb = AR0_[:, 18432:19456]
        hqT = AR0_[:, 19456:20480].rearrange("p (c t) -> p c t", c=8)
        swiuq = sb("swiuq", [64, 256], BF16)
        PERS_ = sb("PERS", [128, 7264], BF16)
        swkT_sb = PERS_[:, 0:3072]
        vextsw_ = PERS_[:, 3072:6240]
        vextsw = vextsw_.rearrange("p (c xx) -> p c xx", c=24)
        crm_sb = PERS_[:, 6240:6752]
        crmT2 = PERS_[:, 6752:7264]
        qf = ARF_[:, 4096:6144]
        qrot1 = ARF_[:, 2048:4096]
        tmp32 = ARF_[:, 6144:6656]
        k0 = ARF_[:, 6656:6784]
        k1 = ARF_[:, 6784:6912]
        zf = ARF_[:, 6912:7040]
        cf = ARF_[:, 7040:7168]
        zT = ARF_[:, 7168:7296]
        cT = ARF_[:, 7296:7424]
        biasb = ARF_[:, 7424:7552]
        biasi = ARF_[:, 7552:7616]
        qnwb = ARF_[:, 7616:7744]
        knwb = ARF_[:, 7744:7872]
        ckvT = ARF_[:, 7872:7936]
        sckvT2 = ARF_[:, 5376:5888]
        zm = sb("zm", [128, 4], F32)
        gs = sb("gs", [128, 4], F32)
        ciT = sb("ciT", [64, 64], F32)
        rstd16 = sb("rstd16", [128, NH], F32)
        dqb = sb("dqb", [128, 64], BF16)
        dqT = sb("dqT", [64, 128], BF16)
        scos1 = sb("scos1", [128, 32], F32)
        ssin1 = sb("ssin1", [128, 32], F32)
        kb16 = sb("kb16", [128, 128], BF16)

        wqc_in = nc.dram_tensor("wqc", [D, 1024], F32, kind="ExternalInput").ap()
        wqup_in = nc.dram_tensor("wqup", [1024, D], F32, kind="ExternalInput").ap()
        wkvb_in = nc.dram_tensor("wkvb", [D, 128], F32, kind="ExternalInput").ap()
        wzb_in = nc.dram_tensor("wzb", [D, 128], F32, kind="ExternalInput").ap()
        biasb_in = nc.dram_tensor("biasb", [32, 128], F32, kind="ExternalInput").ap()
        wk1_in = nc.dram_tensor("wk1", [D, 128], F32, kind="ExternalInput").ap()
        wv1_in = nc.dram_tensor("wv1", [D, 128], F32, kind="ExternalInput").ap()
        wikv_in = nc.dram_tensor("wikv", [D, 64], F32, kind="ExternalInput").ap()
        wiz_in = nc.dram_tensor("wiz", [D, 64], F32, kind="ExternalInput").ap()
        biasi_in = nc.dram_tensor("biasi", [32, 64], F32, kind="ExternalInput").ap()
        wdq_in = nc.dram_tensor("wdq", [D, 64], F32, kind="ExternalInput").ap()
        wiuq_in = nc.dram_tensor("wiuq", [64, 256], F32, kind="ExternalInput").ap()
        ww_in = nc.dram_tensor("ww", [D, 4], F32, kind="ExternalInput").ap()
        qnw_in = nc.dram_tensor("qnw", [128], F32, kind="ExternalInput").ap()
        knw_in = nc.dram_tensor("knw", [128], F32, kind="ExternalInput").ap()
        cos1_in = nc.dram_tensor("cos1", [TPC, 32], F32, kind="ExternalInput").ap()
        sin1_in = nc.dram_tensor("sin1", [TPC, 32], F32, kind="ExternalInput").ap()
        sink_in = nc.dram_tensor("sink", [1, NH], F32, kind="ExternalInput").ap()
        gw_in = nc.dram_tensor("groupw", [4, 512, 512], F32, kind="ExternalInput").ap()
        fw_in = nc.dram_tensor("finalw", [D, D], F32, kind="ExternalInput").ap()
        wg_in = nc.dram_tensor("wgate", [D, 4 * D], F32, kind="ExternalInput").ap()
        wu_in = nc.dram_tensor("wup", [D, 4 * D], F32, kind="ExternalInput").ap()
        wd_in = nc.dram_tensor("wdown", [4 * D, D], F32, kind="ExternalInput").ap()
        madd_in = nc.dram_tensor("maskadd", [TPC, 256], F32, kind="ExternalInput").ap()
        guard_in = nc.dram_tensor("guard", [TPC, 1], F32, kind="ExternalInput").ap()
        hp_in = nc.dram_tensor("haspick", [TPC, 1], F32, kind="ExternalInput").ap()
        swmsk_in = nc.dram_tensor("swmsk", [NT, 5, 128, 128], BF16, kind="ExternalInput").ap()
        wsel_in = nc.dram_tensor("wsel", [128, 8], F32, kind="ExternalInput").ap()
        o_x2 = nc.dram_tensor("x2", [TPC, D], F32, kind="ExternalOutput").ap()
        xn1_scr = nc.dram_tensor("xn1_scr", [TPC, D], BF16).ap()
        q1_scr = nc.dram_tensor("q1_scr", [TPC, D], BF16).ap()
        qi_scr = nc.dram_tensor("qi_scr", [TPC, 256], BF16).ap()
        wgt_scr = nc.dram_tensor("wgt_scr", [TPC, 4], F32).ap()
        tails_in = nc.dram_tensor("tails_in", [2048, 128], BF16).ap()
        tails_out = nc.dram_tensor("tails_out", [8 * 2048, 128], BF16).ap()
        ckv_ctr = nc.dram_tensor("ckv_ctr", [64, 128], F32).ap()
        ckv_g = nc.dram_tensor("ckv_g", [512, 128], F32).ap()
        ci_ctr = nc.dram_tensor("ci_ctr", [64, 64], F32).ap()
        ci_g = nc.dram_tensor("ci_g", [512, 64], F32).ap()

        if dbg:
            for nm, shp, dt in [("d_xn1", [TPC, D], BF16), ("d_q1", [TPC, D], BF16),
                                ("d_ckv", [512, 128], F32), ("d_swkT", [128, 3072], BF16)]:
                dbg_outs[nm] = nc.dram_tensor(nm, shp, dt, kind="ExternalOutput").ap()

        for dc in range(16):
            sc.add("gpsimd", lambda e, dc=dc: e.dma_start(out=swqc[:, dc, :], in_=wqc_in[dc * 128:(dc + 1) * 128, :]),
                   r=[], w=["swqc"], dma=True)
            sc.add("gpsimd", lambda e, dc=dc: e.dma_start(out=swkvb[:, dc, :], in_=wkvb_in[dc * 128:(dc + 1) * 128, :]),
                   r=[], w=["swkvb"], dma=True)
            sc.add("gpsimd", lambda e, dc=dc: e.dma_start(out=swzb[:, dc, :], in_=wzb_in[dc * 128:(dc + 1) * 128, :]),
                   r=[], w=["swzb"], dma=True)
            sc.add("gpsimd", lambda e, dc=dc: e.dma_start(out=swkw[:, dc, :], in_=wk1_in[dc * 128:(dc + 1) * 128, :]),
                   r=[], w=["swkw"], dma=True)
            sc.add("gpsimd", lambda e, dc=dc: e.dma_start(out=swvw[:, dc, :], in_=wv1_in[dc * 128:(dc + 1) * 128, :]),
                   r=[], w=["swvw"], dma=True)
            sc.add("gpsimd", lambda e, dc=dc: e.dma_start(out=swikv[:, dc, :], in_=wikv_in[dc * 128:(dc + 1) * 128, :]),
                   r=[], w=["swikv"], dma=True)
            sc.add("gpsimd", lambda e, dc=dc: e.dma_start(out=swiz[:, dc, :], in_=wiz_in[dc * 128:(dc + 1) * 128, :]),
                   r=[], w=["swiz"], dma=True)
            sc.add("gpsimd", lambda e, dc=dc: e.dma_start(out=swdq[:, dc, :], in_=wdq_in[dc * 128:(dc + 1) * 128, :]),
                   r=[], w=["swdq"], dma=True)
            sc.add("gpsimd", lambda e, dc=dc: e.dma_start(out=sww[:, dc, :], in_=ww_in[dc * 128:(dc + 1) * 128, :]),
                   r=[], w=["sww"], dma=True)
        for dc in range(8):
            sc.add("gpsimd", lambda e, dc=dc: e.dma_start(out=swqup[:, dc, :], in_=wqup_in[dc * 128:(dc + 1) * 128, :]),
                   r=[], w=["swqup"], dma=True)
        sc.add("gpsimd", lambda e: e.dma_start(out=swiuq[:], in_=wiuq_in),
               r=[], w=["swiuq"], dma=True)
        for rr in range(4):
            sc.add("sync", lambda e, rr=rr: e.dma_start(out=biasb[rr * 32:(rr + 1) * 32, :], in_=biasb_in),
                   r=[], w=["biasb"], dma=True)
            sc.add("sync", lambda e, rr=rr: e.dma_start(out=biasi[rr * 32:(rr + 1) * 32, :], in_=biasi_in),
                   r=[], w=["biasi"], dma=True)
        sc.add("sync", lambda e: e.dma_start(out=qnwb[:], in_=qnw_in.unsqueeze(0).broadcast_to([128, 128])),
               r=[], w=["qnwb"], dma=True)
        sc.add("sync", lambda e: e.dma_start(out=knwb[:], in_=knw_in.unsqueeze(0).broadcast_to([128, 128])),
               r=[], w=["knwb"], dma=True)

        for i in range(NT):
            sl = slice(i * 128, (i + 1) * 128)
            hf = i // 8
            tl = i % 8
            sc.add("sync", lambda e, sl=sl: e.dma_start(out=sx[:], in_=o_x1[sl, :]),
                   r=["o_x1", "sx_done"], w=["sx"], dma=True)
            sc.add("sync", lambda e, sl=sl: e.dma_start(out=scos1[:], in_=cos1_in[sl, :]),
                   r=["rope1_done"], w=["scos1"], dma=True)
            sc.add("sync", lambda e, sl=sl: e.dma_start(out=ssin1[:], in_=sin1_in[sl, :]),
                   r=["rope1_done"], w=["ssin1"], dma=True)
            sc.add("vector", lambda e: nc.vector.reduce_sum(st1[:], sx[:], axis=AX.X),
                   r=["sx"], w=["st1"])
            sc.add("scalar", lambda e: nc.scalar.activation(ntrash[:], sx[:, 0:128], AF.Square, accum_out=st2[:]),
                   r=["sx"], w=["ntrash", "st2"])
            for cq in range(1, 16):
                sc.add("scalar", lambda e, cq=cq: nc.scalar.activation(
                    ntrash[:], sx[:, cq * 128:(cq + 1) * 128], AF.Square, accum_out=nbuf[:, cq:cq + 1]),
                    r=["sx"], w=["ntrash", "nbuf"])
            sc.add("vector", lambda e: nc.vector.reduce_sum(st3[:], nbuf[:, 1:16], axis=AX.X),
                   r=["nbuf"], w=["st3"])
            sc.add("vector", lambda e: nc.vector.tensor_add(st2[:], st2[:], st3[:]),
                   r=["st2", "st3"], w=["st2"])
            sc.add("vector", lambda e: nc.vector.tensor_scalar_mul(st1[:], st1[:], 1.0 / D),
                   r=["st1"], w=["st1"])
            sc.add("vector", lambda e: nc.vector.tensor_scalar_mul(st2[:], st2[:], 1.0 / D),
                   r=["st2"], w=["st2"])
            sc.add("vector", lambda e: nc.vector.tensor_mul(st3[:], st1[:], st1[:]),
                   r=["st1"], w=["st3"])
            sc.add("vector", lambda e: nc.vector.tensor_sub(st2[:], st2[:], st3[:]),
                   r=["st2", "st3"], w=["st2"])
            sc.add("scalar", lambda e: nc.scalar.activation(st3[:], st2[:], AF.Sqrt, bias=epsb5[:]),
                   r=["st2", "epsb5"], w=["st3"])
            sc.add("vector", lambda e: nc.vector.reciprocal(st4[:], st3[:]),
                   r=["st3"], w=["st4"])
            sc.add("vector", lambda e: nc.vector.scalar_tensor_tensor(
                st3[:], st1[:], -1.0, st4[:], op0=ALU.mult, op1=ALU.mult),
                r=["st1", "st4"], w=["st3"])
            sc.add("vector", lambda e: nc.vector.scalar_tensor_tensor(
                xnb[:], sx[:], st4[:], st3[:].broadcast_to([128, D]), op0=ALU.mult, op1=ALU.add),
                r=["sx", "st4", "st3"], w=["xnb", "sx_done"])
            sc.add("gpsimd", lambda e, sl=sl: e.dma_start(out=xn1_scr[sl, :], in_=xnb[:]),
                   r=["xnb"], w=["xn1_scr"], dma=True)
            for c0 in range(0, 16, 4):
                pb, pbt = PSB(c0 // 4)
                for j in range(4):
                    dc = c0 + j
                    sc.add("tensor", lambda e, dc=dc, j=j, pb=pb: nc.tensor.transpose(
                        pb[:, j * 128:(j + 1) * 128], xnb[:, dc * 128:(dc + 1) * 128], identb[:]),
                        r=["xnb", "identb"], w=[pbt])
                for j in range(4):
                    dc = c0 + j
                    sc.add("scalar", lambda e, dc=dc, j=j, pb=pb: nc.scalar.copy(
                        xnTt[:, dc, :], pb[:, j * 128:(j + 1) * 128]),
                        r=[pbt], w=["xnTt"])
            for b2 in range(2):
                pm = PS0 if b2 % 2 == 0 else PS1
                pmt = "PS0" if b2 % 2 == 0 else "PS1"
                for dc in range(16):
                    sc.add("tensor", lambda e, b2=b2, dc=dc, pm=pm: nc.tensor.matmul(
                        pm[:], xnTt[:, dc, :], swqc[:, dc, b2 * 512:(b2 + 1) * 512],
                        start=(dc == 0), stop=(dc == 15)),
                        r=["xnTt", "swqc"], w=[pmt])
                sc.add("scalar", lambda e, b2=b2, pm=pm: nc.scalar.copy(hqb[:, b2 * 512:(b2 + 1) * 512], pm[:]),
                       r=[pmt], w=["hqb"])
            for c0 in range(0, 8, 4):
                pb, pbt = PSB(c0 // 4)
                for j in range(4):
                    dc = c0 + j
                    sc.add("tensor", lambda e, dc=dc, j=j, pb=pb: nc.tensor.transpose(
                        pb[:, j * 128:(j + 1) * 128], hqb[:, dc * 128:(dc + 1) * 128], identb[:]),
                        r=["hqb", "identb"], w=[pbt])
                for j in range(4):
                    dc = c0 + j
                    sc.add("scalar", lambda e, dc=dc, j=j, pb=pb: nc.scalar.copy(
                        hqT[:, dc, :], pb[:, j * 128:(j + 1) * 128]),
                        r=[pbt], w=["hqT"])
            for b4 in range(4):
                pm = PS0 if b4 % 2 == 0 else PS1
                pmt = "PS0" if b4 % 2 == 0 else "PS1"
                for dc in range(8):
                    sc.add("tensor", lambda e, b4=b4, dc=dc, pm=pm: nc.tensor.matmul(
                        pm[:], hqT[:, dc, :], swqup[:, dc, b4 * 512:(b4 + 1) * 512],
                        start=(dc == 0), stop=(dc == 7)),
                        r=["hqT", "swqup"], w=[pmt])
                sc.add("scalar", lambda e, b4=b4, pm=pm: nc.scalar.copy(qf[:, b4 * 512:(b4 + 1) * 512], pm[:]),
                       r=[pmt], w=["qf"])
            qv = qf.rearrange("p (h d2) -> p h d2", h=NH)
            rv = qrot1.rearrange("p (h d2) -> p h d2", h=NH)
            tv = tmp32.rearrange("p (h d2) -> p h d2", h=NH)
            cosb = scos1[:].unsqueeze(1).broadcast_to([128, NH, 32])
            sinb = ssin1[:].unsqueeze(1).broadcast_to([128, NH, 32])
            sc.add("vector", lambda e: nc.vector.tensor_mul(rv[:, :, 0:32], qv[:, :, 0:32], cosb),
                   r=["qf", "scos1"], w=["qrot1"])
            sc.add("vector", lambda e: nc.vector.tensor_mul(tv, qv[:, :, 32:64], sinb),
                   r=["qf", "ssin1"], w=["tmp32"])
            sc.add("vector", lambda e: nc.vector.tensor_sub(rv[:, :, 0:32], rv[:, :, 0:32], tv),
                   r=["qrot1", "tmp32"], w=["qrot1"])
            sc.add("vector", lambda e: nc.vector.tensor_mul(rv[:, :, 32:64], qv[:, :, 32:64], cosb),
                   r=["qf", "scos1"], w=["qrot1"])
            sc.add("vector", lambda e: nc.vector.tensor_mul(tv, qv[:, :, 0:32], sinb),
                   r=["qf", "ssin1"], w=["tmp32"])
            sc.add("vector", lambda e: nc.vector.tensor_add(rv[:, :, 32:64], rv[:, :, 32:64], tv),
                   r=["qrot1", "tmp32"], w=["qrot1"])
            sc.add("vector", lambda e: nc.vector.tensor_copy(rv[:, :, 64:128], qv[:, :, 64:128]),
                   r=["qf"], w=["qrot1"])
            for h in range(NH):
                sc.add("scalar", lambda e, h=h: nc.scalar.activation(
                    ntrash[:], qrot1[:, h * 128:(h + 1) * 128], AF.Square, accum_out=nbuf[:, h:h + 1]),
                    r=["qrot1"], w=["ntrash", "nbuf"])
            sc.add("vector", lambda e: nc.vector.tensor_scalar(
                rstd16[:], nbuf[:], 1.0 / 128.0, None, op0=ALU.mult),
                r=["nbuf"], w=["rstd16"])
            sc.add("scalar", lambda e: nc.scalar.activation(rstd16[:], rstd16[:], AF.Sqrt, bias=epsb6[:]),
                   r=["rstd16", "epsb6"], w=["rstd16"])
            sc.add("vector", lambda e: nc.vector.reciprocal(rstd16[:], rstd16[:]),
                   r=["rstd16"], w=["rstd16"])
            for h in range(NH):
                sc.add("vector", lambda e, h=h: nc.vector.scalar_tensor_tensor(
                    outb[:, h * 128:(h + 1) * 128], qrot1[:, h * 128:(h + 1) * 128],
                    rstd16[:, h:h + 1], qnwb[:], op0=ALU.mult, op1=ALU.mult),
                    r=["qrot1", "rstd16", "qnwb"], w=["outb"])
            sc.add("gpsimd", lambda e, sl=sl: e.dma_start(out=q1_scr[sl, :], in_=outb[:]),
                   r=["outb"], w=["q1_scr"], dma=True)
            sc.add("tensor", lambda e: nc.tensor.matmul(
                PS2[:, 0:128], xnTt[:, 0, :], swkw[:, 0, :], start=True, stop=False),
                r=["xnTt", "swkw"], w=["PS2"])
            for dc in range(1, 16):
                sc.add("tensor", lambda e, dc=dc: nc.tensor.matmul(
                    PS2[:, 0:128], xnTt[:, dc, :], swkw[:, dc, :],
                    start=False, stop=(dc == 15)),
                    r=["xnTt", "swkw"], w=["PS2"])
            sc.add("scalar", lambda e: nc.scalar.copy(k0[:], PS2[:, 0:128]), r=["PS2"], w=["k0"])
            sc.add("scalar", lambda e: nc.scalar.activation(ntrash[:], k0[:], AF.Square, accum_out=st1[:]),
                   r=["k0"], w=["ntrash", "st1"])
            sc.add("vector", lambda e: nc.vector.tensor_scalar_mul(st1[:], st1[:], 1.0 / 128.0),
                   r=["st1"], w=["st1"])
            sc.add("scalar", lambda e: nc.scalar.activation(st1[:], st1[:], AF.Sqrt, bias=epsb6[:]),
                   r=["st1", "epsb6"], w=["st1"])
            sc.add("vector", lambda e: nc.vector.reciprocal(st1[:], st1[:]),
                   r=["st1"], w=["st1"])
            sc.add("vector", lambda e: nc.vector.scalar_tensor_tensor(
                k1[:], k0[:], st1[:], knwb[:], op0=ALU.mult, op1=ALU.mult),
                r=["k0", "st1", "knwb"], w=["k1"])
            sc.add("vector", lambda e: nc.vector.tensor_mul(kb16[:, 0:32], k1[:, 0:32], scos1[:, 0:32]),
                   r=["k1", "scos1"], w=["kb16"])
            sc.add("vector", lambda e: nc.vector.tensor_mul(tmp32[:, 0:32], k1[:, 32:64], ssin1[:, 0:32]),
                   r=["k1", "ssin1"], w=["tmp32"])
            sc.add("vector", lambda e: nc.vector.tensor_sub(kb16[:, 0:32], kb16[:, 0:32], tmp32[:, 0:32]),
                   r=["kb16", "tmp32"], w=["kb16"])
            sc.add("vector", lambda e: nc.vector.tensor_mul(kb16[:, 32:64], k1[:, 32:64], scos1[:, 0:32]),
                   r=["k1", "scos1"], w=["kb16"])
            sc.add("vector", lambda e: nc.vector.tensor_mul(tmp32[:, 0:32], k1[:, 0:32], ssin1[:, 0:32]),
                   r=["k1", "ssin1"], w=["tmp32", "rope1_done"])
            sc.add("vector", lambda e: nc.vector.tensor_add(kb16[:, 32:64], kb16[:, 32:64], tmp32[:, 0:32]),
                   r=["kb16", "tmp32"], w=["kb16"])
            sc.add("vector", lambda e: nc.vector.tensor_copy(kb16[:, 64:128], k1[:, 64:128]),
                   r=["k1"], w=["kb16"])
            sc.add("tensor", lambda e: nc.tensor.transpose(PSB0[:, 0:128], kb16[:], identb[:]),
                   r=["kb16", "identb"], w=["PSB0"])
            col0 = hf * 1536 + 512 + tl * 128
            sc.add("scalar", lambda e, col0=col0: nc.scalar.copy(
                swkT_sb[:, col0:col0 + 128], PSB0[:, 0:128]),
                r=["PSB0"], w=["swkT_sb"])
            if tl >= 4:
                trow = hf * 512 + (tl - 4) * 128
                sc.add("gpsimd", lambda e, trow=trow: e.dma_start(
                    out=tails_in[trow:trow + 128, :], in_=kb16[:]),
                    r=["kb16"], w=["tails_in"], dma=True)
            sc.add("tensor", lambda e: nc.tensor.matmul(
                PS2[:, 0:128], xnTt[:, 0, :], swvw[:, 0, :], start=True, stop=False),
                r=["xnTt", "swvw"], w=["PS2"])
            for dc in range(1, 16):
                sc.add("tensor", lambda e, dc=dc: nc.tensor.matmul(
                    PS2[:, 0:128], xnTt[:, dc, :], swvw[:, dc, :],
                    start=False, stop=(dc == 15)),
                    r=["xnTt", "swvw"], w=["PS2"])
            sc.add("scalar", lambda e: nc.scalar.copy(kb16[:], PS2[:, 0:128]), r=["PS2"], w=["kb16"])
            sc.add("vector", lambda e, hf=hf, tl=tl: nc.vector.tensor_copy(
                vextsw[:, 12 * hf + 4 + tl, 0:128], kb16[:]),
                r=["kb16"], w=["vextsw"])
            if tl >= 4:
                trow = 1024 + hf * 512 + (tl - 4) * 128
                sc.add("gpsimd", lambda e, trow=trow: e.dma_start(
                    out=tails_in[trow:trow + 128, :], in_=kb16[:]),
                    r=["kb16"], w=["tails_in"], dma=True)
            for dc in range(16):
                sc.add("tensor", lambda e, dc=dc: nc.tensor.matmul(
                    PS2[:, 0:128], xnTt[:, dc, :], swkvb[:, dc, :], start=(dc == 0), stop=(dc == 15)),
                    r=["xnTt", "swkvb"], w=["PS2"])
            for dc in range(16):
                sc.add("tensor", lambda e, dc=dc: nc.tensor.matmul(
                    PS3[:, 0:128], xnTt[:, dc, :], swzb[:, dc, :], start=(dc == 0), stop=(dc == 15)),
                    r=["xnTt", "swzb"], w=["PS3"])
            sc.add("vector", lambda e: nc.vector.tensor_add(zf[:], PS3[:, 0:128], biasb[:]),
                   r=["PS3", "biasb"], w=["zf"])
            sc.add("scalar", lambda e: nc.scalar.copy(cf[:], PS2[:, 0:128]), r=["PS2"], w=["cf"])
            sc.add("tensor", lambda e: nc.tensor.transpose(PS2[:, 0:128], zf[:], identf[:]),
                   r=["zf", "identf"], w=["PS2"])
            sc.add("tensor", lambda e: nc.tensor.transpose(PS3[:, 0:128], cf[:], identf[:]),
                   r=["cf", "identf"], w=["PS3"])
            sc.add("scalar", lambda e: nc.scalar.copy(zT[:], PS2[:, 0:128]), r=["PS2"], w=["zT"])
            sc.add("scalar", lambda e: nc.scalar.copy(cT[:], PS3[:, 0:128]), r=["PS3"], w=["cT"])
            zvw = zT.rearrange("p (g m) -> p g m", g=4)
            cvw = cT.rearrange("p (g m) -> p g m", g=4)
            sc.add("vector", lambda e: nc.vector.reduce_max(zm[:], zvw, axis=AX.X),
                   r=["zT"], w=["zm"])
            sc.add("vector", lambda e: nc.vector.tensor_sub(zvw, zvw, zm[:].unsqueeze(2).broadcast_to([128, 4, 32])),
                   r=["zT", "zm"], w=["zT"])
            sc.add("scalar", lambda e: nc.scalar.activation(zT[:], zT[:], AF.Exp),
                   r=["zT"], w=["zT"])
            sc.add("vector", lambda e: nc.vector.reduce_sum(gs[:], zvw, axis=AX.X),
                   r=["zT"], w=["gs"])
            sc.add("vector", lambda e: nc.vector.reciprocal(gs[:], gs[:]),
                   r=["gs"], w=["gs"])
            sc.add("vector", lambda e: nc.vector.tensor_mul(zvw, zvw, gs[:].unsqueeze(2).broadcast_to([128, 4, 32])),
                   r=["zT", "gs"], w=["zT"])
            sc.add("vector", lambda e: nc.vector.tensor_mul(cvw, cvw, zvw),
                   r=["cT", "zT"], w=["cT"])
            sc.add("vector", lambda e, i=i: nc.vector.reduce_sum(ckvT[:, i * 4:(i + 1) * 4], cvw, axis=AX.X),
                   r=["cT"], w=["ckvT"])
            for dc in range(16):
                sc.add("tensor", lambda e, dc=dc: nc.tensor.matmul(
                    PS2[:, 0:64], xnTt[:, dc, :], swikv[:, dc, :], start=(dc == 0), stop=(dc == 15)),
                    r=["xnTt", "swikv"], w=["PS2"])
            for dc in range(16):
                sc.add("tensor", lambda e, dc=dc: nc.tensor.matmul(
                    PS3[:, 0:64], xnTt[:, dc, :], swiz[:, dc, :], start=(dc == 0), stop=(dc == 15)),
                    r=["xnTt", "swiz"], w=["PS3"])
            sc.add("vector", lambda e: nc.vector.tensor_add(zf[:, 0:64], PS3[:, 0:64], biasi[:]),
                   r=["PS3", "biasi"], w=["zf"])
            sc.add("scalar", lambda e: nc.scalar.copy(cf[:, 0:64], PS2[:, 0:64]), r=["PS2"], w=["cf"])
            sc.add("tensor", lambda e: nc.tensor.transpose(PS2[0:64, 0:128], zf[:, 0:64], identf[:]),
                   r=["zf", "identf"], w=["PS2"])
            sc.add("tensor", lambda e: nc.tensor.transpose(PS3[0:64, 0:128], cf[:, 0:64], identf[:]),
                   r=["cf", "identf"], w=["PS3"])
            sc.add("scalar", lambda e: nc.scalar.copy(zT[0:64, :], PS2[0:64, 0:128]), r=["PS2"], w=["zT"])
            sc.add("scalar", lambda e: nc.scalar.copy(cT[0:64, :], PS3[0:64, 0:128]), r=["PS3"], w=["cT"])
            zv2 = zT[0:64, :].rearrange("p (g m) -> p g m", g=4)
            cv2 = cT[0:64, :].rearrange("p (g m) -> p g m", g=4)
            sc.add("vector", lambda e: nc.vector.reduce_max(zm[0:64, :], zv2, axis=AX.X),
                   r=["zT"], w=["zm"])
            sc.add("vector", lambda e: nc.vector.tensor_sub(zv2, zv2, zm[0:64, :].unsqueeze(2).broadcast_to([64, 4, 32])),
                   r=["zT", "zm"], w=["zT"])
            sc.add("scalar", lambda e: nc.scalar.activation(zT[0:64, :], zT[0:64, :], AF.Exp),
                   r=["zT"], w=["zT"])
            sc.add("vector", lambda e: nc.vector.reduce_sum(gs[0:64, :], zv2, axis=AX.X),
                   r=["zT"], w=["gs"])
            sc.add("vector", lambda e: nc.vector.reciprocal(gs[0:64, :], gs[0:64, :]),
                   r=["gs"], w=["gs"])
            sc.add("vector", lambda e: nc.vector.tensor_mul(zv2, zv2, gs[0:64, :].unsqueeze(2).broadcast_to([64, 4, 32])),
                   r=["zT", "gs"], w=["zT"])
            sc.add("vector", lambda e: nc.vector.tensor_mul(cv2, cv2, zv2),
                   r=["cT", "zT"], w=["cT"])
            sc.add("vector", lambda e, i=i: nc.vector.reduce_sum(ciT[:, i * 4:(i + 1) * 4], cv2, axis=AX.X),
                   r=["cT"], w=["ciT"])
            for dc in range(16):
                sc.add("tensor", lambda e, dc=dc: nc.tensor.matmul(
                    PS2[:, 0:64], xnTt[:, dc, :], swdq[:, dc, :], start=(dc == 0), stop=(dc == 15)),
                    r=["xnTt", "swdq"], w=["PS2"])
            sc.add("scalar", lambda e: nc.scalar.copy(dqb[:], PS2[:, 0:64]), r=["PS2"], w=["dqb"])
            sc.add("tensor", lambda e: nc.tensor.transpose(PSB0[0:64, 0:128], dqb[:], identb[:]),
                   r=["dqb", "identb"], w=["PSB0"])
            sc.add("scalar", lambda e: nc.scalar.copy(dqT[:], PSB0[0:64, 0:128]), r=["PSB0"], w=["dqT"])
            sc.add("tensor", lambda e: nc.tensor.matmul(PS2[:, 0:256], dqT[:], swiuq[:], start=True, stop=True),
                   r=["dqT", "swiuq"], w=["PS2"])
            sc.add("vector", lambda e: nc.vector.tensor_copy(wload[:, 0:256], PS2[:, 0:256]),
                   r=["PS2"], w=["wload"])
            sc.add("gpsimd", lambda e, sl=sl: e.dma_start(out=qi_scr[sl, :], in_=wload[:, 0:256]),
                   r=["wload"], w=["qi_scr"], dma=True)
            for dc in range(16):
                sc.add("tensor", lambda e, dc=dc: nc.tensor.matmul(
                    PS3[:, 0:4], xnTt[:, dc, :], sww[:, dc, :], start=(dc == 0), stop=(dc == 15)),
                    r=["xnTt", "sww"], w=["PS3"])
            sc.add("vector", lambda e: nc.vector.tensor_copy(zm[:], PS3[:, 0:4]),
                   r=["PS3"], w=["zm"])
            sc.add("gpsimd", lambda e, sl=sl: e.dma_start(out=wgt_scr[sl, :], in_=zm[:]),
                   r=["zm"], w=["wgt_scr"], dma=True)

        sc.add("tensor", lambda e: nc.tensor.transpose(PS2[0:64, 0:128], ckvT[:, 0:64], identf[:]),
               r=["ckvT", "identf"], w=["PS2"])
        sc.add("scalar", lambda e: nc.scalar.copy(ntrash[0:64, 0:128], PS2[0:64, 0:128]),
               r=["PS2"], w=["ntrash"])
        sc.add("gpsimd", lambda e: e.dma_start(out=ckv_ctr, in_=ntrash[0:64, 0:128]),
               r=["ntrash"], w=["ckv_ctr"], dma=True)
        sc.add("tensor", lambda e: nc.tensor.transpose(PS3[0:64, 0:64], ciT[:, 0:64], identf[0:64, 0:64]),
               r=["ciT", "identf"], w=["PS3"])
        sc.add("scalar", lambda e: nc.scalar.copy(ntrash[0:64, 0:64], PS3[0:64, 0:64]),
               r=["PS3"], w=["ntrash"])
        sc.add("gpsimd", lambda e: e.dma_start(out=ci_ctr, in_=ntrash[0:64, 0:64]),
               r=["ntrash"], w=["ci_ctr"], dma=True)
        sc.coll(lambda e: _ag(e, [tails_in.opt()], [tails_out.opt()]), r=["tails_in"], w=["tails_out"])
        sc.coll(lambda e: _ag(e, [ckv_ctr.opt()], [ckv_g.opt()]), r=["ckv_ctr"], w=["ckv_g"])
        sc.coll(lambda e: _ag(e, [ci_ctr.opt()], [ci_g.opt()]), r=["ci_ctr"], w=["ci_g"])

        if dbg:
            sc.add("gpsimd", lambda e: e.dma_start(out=dbg_outs["d_xn1"], in_=xn1_scr),
                   r=["xn1_scr"], w=[], dma=True)
            sc.add("gpsimd", lambda e: e.dma_start(out=dbg_outs["d_q1"], in_=q1_scr),
                   r=["q1_scr"], w=[], dma=True)
            sc.add("gpsimd", lambda e: e.dma_start(out=dbg_outs["d_ckv"], in_=ckv_g),
                   r=["ckv_g"], w=[], dma=True)

        sc.barrier()

        sfw = AR0_[:].rearrange("p (c t) -> p c t", c=16)
        arena_ = AR1_[:, 0:24832]
        swvE = arena_[:, 0:0]
        sgw = AR1_[:, 0:8192].rearrange("p (c xx) -> p c xx", c=16)
        sgate = AR1_[:, 0:8192].rearrange("p (c xx) -> p c xx", c=16)
        sup = AR1_[:, 8192:16384].rearrange("p (c xx) -> p c xx", c=16)
        sdown = AR1_[:, 16384:24576].rearrange("p (c xx) -> p c xx", c=4)
        xn1T_ = AR1_[:, 8192:10240].rearrange("p (c t) -> p c t", c=16)
        sq = outb
        sqi = wload[:, 0:256]
        qiT = wload[:, 256:768]
        SP2_ = AR1_[:, 24832:30720]
        ogb = SP2_[:, 0:2048]
        esw = SP2_[:, 2048:2560]
        eswT = SP2_[:, 2560:3200]
        esc = SP2_[:, 3200:3456]
        escT = SP2_[:, 3456:3712]
        eswd = SP2_[:, 3712:3840]
        qTb = tpose
        sciTb2 = AR1_[0:64, 30720:31232]
        tailsel = sb("tailsel", [128, 128], BF16)
        wselb = sb("wselb", [128, 8], F32)
        sinkb = sb("sinkb", [128, NH], F32)
        attnb = ARF_[:, 0:2048]
        sxr = ARF_[:, 2048:4096]
        rel0 = ARF_[:, 4096:4352]
        iscacc = ARF_[:, 4352:4608]
        iscw = ARF_[:, 4608:4864]
        scf = ARF_[:, 4864:5120]
        smadd = ARF_[:, 5120:5376]
        sm8 = sb("sm8", [128, 8], F32)
        sthr = sb("sthr", [128, 1], F32)
        vb2 = SP2_[:, 3840:4096]
        selbit = SP2_[:, 4096:4352]
        smsk = SP2_[:, 4352:4992]
        sladd = ARF_[:, 6656:6912]
        esk = sb("esk", [128, 1], F32)
        sguard = sb("sguard", [128, 1], F32)
        shp = sb("shp", [128, 1], F32)
        swgt = zm

        sc.add("sync", lambda e: e.dma_start(out=wselb[:], in_=wsel_in), r=[], w=["wselb"], dma=True)
        sc.add("sync", lambda e: e.dma_start(out=sinkb[:], in_=sink_in.broadcast_to([128, NH])),
               r=[], w=["sinkb"], dma=True)
        for hf in range(2):
            for kk in range(4):
                sc.add("vector", lambda e: nc.vector.memset(tailsel[:], 0.0), r=["tailsel_c"], w=["tailsel"])
                for rho in range(8):
                    sc.add("sync", lambda e, rho=rho, hf=hf, kk=kk: e.dma_start(
                        out=kb16[:], in_=tails_out[rho * 2048 + hf * 512 + kk * 128:
                                                   rho * 2048 + hf * 512 + (kk + 1) * 128, :]),
                        r=["tails_out", "kb16_done"], w=["kb16"], dma=True)
                    sc.add("vector", lambda e, rho=rho: nc.vector.scalar_tensor_tensor(
                        tailsel[:], kb16[:], wselb[:, rho:rho + 1], tailsel[:],
                        op0=ALU.mult, op1=ALU.add),
                        r=["kb16", "wselb", "tailsel"], w=["tailsel", "kb16_done"])
                sc.add("tensor", lambda e: nc.tensor.transpose(PSB0[:, 0:128], tailsel[:], identb[:]),
                       r=["tailsel", "identb"], w=["PSB0"])
                sc.add("scalar", lambda e, hf=hf, kk=kk: nc.scalar.copy(
                    swkT_sb[:, hf * 1536 + kk * 128: hf * 1536 + (kk + 1) * 128], PSB0[:, 0:128]),
                    r=["PSB0"], w=["swkT_sb", "tailsel_c"])
                sc.add("vector", lambda e: nc.vector.memset(tailsel[:], 0.0), r=["tailsel_c"], w=["tailsel"])
                for rho in range(8):
                    sc.add("sync", lambda e, rho=rho, hf=hf, kk=kk: e.dma_start(
                        out=kb16[:], in_=tails_out[rho * 2048 + 1024 + hf * 512 + kk * 128:
                                                   rho * 2048 + 1024 + hf * 512 + (kk + 1) * 128, :]),
                        r=["tails_out", "kb16_done"], w=["kb16"], dma=True)
                    sc.add("vector", lambda e, rho=rho: nc.vector.scalar_tensor_tensor(
                        tailsel[:], kb16[:], wselb[:, rho:rho + 1], tailsel[:],
                        op0=ALU.mult, op1=ALU.add),
                        r=["kb16", "wselb", "tailsel"], w=["tailsel", "kb16_done"])
                sc.add("vector", lambda e, hf=hf, kk=kk: nc.vector.tensor_copy(
                    vextsw[:, 12 * hf + kk, 0:128], tailsel[:]),
                    r=["tailsel"], w=["vextsw", "tailsel_c"])
        sc.add("vector", lambda e: nc.vector.memset(vextsw[:, :, 128:129], 1.0),
               r=[], w=["vextsw"])

        for bslot in range(2):
            for rho in range(4):
                rho2 = rho * 2 + 0
                pass
        for bslot in range(2):
            for rho in range(8):
                sc.add("sync", lambda e, bslot=bslot, rho=rho: e.dma_start(
                    out=ntrash[0:32, 0:128],
                    in_=ckv_g[rho * 64 + bslot * 32: rho * 64 + (bslot + 1) * 32, :]),
                    r=["ckv_g", "ntr_done"], w=["ntrash"], dma=True)
                sc.add("tensor", lambda e: nc.tensor.transpose(PS2[:, 0:32], ntrash[0:32, 0:128], identf[0:32, 0:32]),
                       r=["ntrash", "identf"], w=["PS2", "ntr_done"])
                sc.add("scalar", lambda e, bslot=bslot, rho=rho: nc.scalar.copy(
                    sckvT2[:, bslot * 256 + rho * 32: bslot * 256 + (rho + 1) * 32], PS2[:, 0:32]),
                    r=["PS2"], w=["sckvT2"])
                sc.add("sync", lambda e, bslot=bslot, rho=rho: e.dma_start(
                    out=ntrash[0:32, 0:64],
                    in_=ci_g[rho * 64 + bslot * 32: rho * 64 + (bslot + 1) * 32, :]),
                    r=["ci_g", "ntr_done"], w=["ntrash"], dma=True)
                sc.add("tensor", lambda e: nc.tensor.transpose(PS3[0:64, 0:32], ntrash[0:32, 0:64], identf[0:32, 0:32]),
                       r=["ntrash", "identf"], w=["PS3", "ntr_done"])
                sc.add("vector", lambda e, bslot=bslot, rho=rho: nc.vector.tensor_copy(
                    sciTb2[:, bslot * 256 + rho * 32: bslot * 256 + (rho + 1) * 32], PS3[0:64, 0:32]),
                    r=["PS3"], w=["sciTb2"])
        k0crm = ARF_[:, 5952:6080]
        for cchunk in range(4):
            sc.add("tensor", lambda e, cchunk=cchunk: nc.tensor.transpose(
                PS2[:, 0:128], sckvT2[:, cchunk * 128:(cchunk + 1) * 128], identf[:]),
                r=["sckvT2", "identf"], w=["PS2"])
            sc.add("scalar", lambda e: nc.scalar.copy(k0crm[:], PS2[:, 0:128]),
                   r=["PS2"], w=["k0crm"])
            sc.add("scalar", lambda e: nc.scalar.activation(ntrash[:], k0crm[:], AF.Square, accum_out=st1[:]),
                   r=["k0crm"], w=["ntrash", "st1"])
            sc.add("vector", lambda e: nc.vector.tensor_scalar_mul(st1[:], st1[:], 1.0 / 128.0),
                   r=["st1"], w=["st1"])
            sc.add("scalar", lambda e: nc.scalar.activation(st1[:], st1[:], AF.Sqrt, bias=epsb6[:]),
                   r=["st1", "epsb6"], w=["st1"])
            sc.add("vector", lambda e: nc.vector.reciprocal(st1[:], st1[:]),
                   r=["st1"], w=["st1"])
            sc.add("vector", lambda e, cchunk=cchunk: nc.vector.scalar_tensor_tensor(
                crm_sb[:, cchunk * 128:(cchunk + 1) * 128], k0crm[:], st1[:], knwb[:],
                op0=ALU.mult, op1=ALU.mult),
                r=["k0crm", "st1", "knwb"], w=["crm_sb"])
            sc.add("tensor", lambda e, cchunk=cchunk: nc.tensor.transpose(
                PSB0[:, 0:128], crm_sb[:, cchunk * 128:(cchunk + 1) * 128], identb[:]),
                r=["crm_sb", "identb"], w=["PSB0"])
            sc.add("scalar", lambda e, cchunk=cchunk: nc.scalar.copy(
                crmT2[:, cchunk * 128:(cchunk + 1) * 128], PSB0[:, 0:128]),
                r=["PSB0"], w=["crmT2"])
        for g in range(4):
            for cc in range(4):
                sc.add("gpsimd", lambda e, g=g, cc=cc: e.dma_start(
                    out=sgw[:, g * 4 + cc, :], in_=gw_in[g, cc * 128:(cc + 1) * 128, :]),
                    r=[], w=["sgw"], dma=True)
        for dc in range(16):
            sc.add("gpsimd", lambda e, dc=dc: e.dma_start(
                out=sfw[:, dc, :], in_=fw_in[dc * 128:(dc + 1) * 128, :]),
                r=[], w=["sfw"], dma=True)

        crm_v = crm_sb.rearrange("p (c xx) -> p c xx", c=4)
        for i in range(NT):
            sl = slice(i * 128, (i + 1) * 128)
            hf = i // 8
            tl = i % 8
            sc.add("sync", lambda e, sl=sl: e.dma_start(out=sq[:], in_=q1_scr[sl, :]),
                   r=["q1_scr", "sq_done"], w=["sq"], dma=True)
            sc.add("sync", lambda e, sl=sl: e.dma_start(out=sqi[:], in_=qi_scr[sl, :]),
                   r=["qi_scr", "sqi_done"], w=["sqi"], dma=True)
            sc.add("sync", lambda e, sl=sl: e.dma_start(out=swgt[:], in_=wgt_scr[sl, :]),
                   r=["wgt_scr", "zm_done"], w=["zm"], dma=True)
            sc.add("sync", lambda e, sl=sl: e.dma_start(out=smadd[:], in_=madd_in[sl, :]),
                   r=["smadd_done"], w=["smadd"], dma=True)
            sc.add("sync", lambda e, sl=sl: e.dma_start(out=sguard[:], in_=guard_in[sl, :]),
                   r=["sg_done"], w=["sguard"], dma=True)
            sc.add("sync", lambda e, sl=sl: e.dma_start(out=shp[:], in_=hp_in[sl, :]),
                   r=["sg_done"], w=["shp"], dma=True)
            sc.add("sync", lambda e, i=i: e.dma_start(
                out=smsk[:].rearrange("p (c xx) -> p c xx", c=5),
                in_=swmsk_in[i].rearrange("c p xx -> p c xx")),
                r=["smsk_done"], w=["smsk"], dma=True)
            for j in range(4):
                sc.add("tensor", lambda e, j=j: nc.tensor.transpose(
                    PSB0[0:64, j * 128:(j + 1) * 128], sqi[:, j * 64:(j + 1) * 64], identb[:]),
                    r=["sqi", "identb"], w=["PSB0"])
            sc.add("scalar", lambda e: nc.scalar.copy(qiT[0:64, :], PSB0[0:64, 0:512]),
                   r=["PSB0"], w=["qiT", "sqi_done"])
            for h4 in range(4):
                sc.add("tensor", lambda e, h4=h4, hf=hf: nc.tensor.matmul(
                    PS2[:, 0:256], qiT[0:64, h4 * 128:(h4 + 1) * 128], sciTb2[:, hf * 256:(hf + 1) * 256],
                    start=True, stop=True),
                    r=["qiT", "sciTb2"], w=["PS2"])
                sc.add("vector", lambda e: nc.vector.tensor_relu(rel0[:], PS2[:, 0:256]),
                       r=["PS2"], w=["rel0"])
                if h4 == 0:
                    sc.add("vector", lambda e, h4=h4: nc.vector.tensor_scalar(
                        iscacc[:], rel0[:], zm[:, h4:h4 + 1], None, op0=ALU.mult),
                        r=["rel0", "zm"], w=["iscacc"])
                else:
                    sc.add("vector", lambda e, h4=h4: nc.vector.scalar_tensor_tensor(
                        iscacc[:], rel0[:], zm[:, h4:h4 + 1], iscacc[:], op0=ALU.mult, op1=ALU.add),
                        r=["rel0", "zm", "iscacc"],
                        w=["iscacc"] + (["zm_done"] if h4 == 3 else []))
            sc.add("vector", lambda e: nc.vector.tensor_add(iscacc[:], iscacc[:], smadd[:]),
                   r=["iscacc", "smadd"], w=["iscacc"])
            sc.add("vector", lambda e: nc.vector.tensor_copy(iscw[:], iscacc[:]),
                   r=["iscacc"], w=["iscw"])
            for it in range(4):
                sc.add("vector", lambda e: nc.vector.max(out=sm8[:], in_=iscw[:]),
                       r=["iscw"], w=["sm8"])
                sc.add("vector", lambda e: nc.vector.match_replace(
                    out=iscw[:], in_to_replace=sm8[:], in_values=iscw[:], imm_value=NEG),
                    r=["iscw", "sm8"], w=["iscw"])
            sc.add("vector", lambda e: nc.vector.tensor_reduce(
                sthr[:], sm8[:], axis=AX.X, op=ALU.min), r=["sm8"], w=["sthr"])
            sc.add("vector", lambda e: nc.vector.tensor_scalar(
                vb2[:], smadd[:], -1.0, None, op0=ALU.is_ge),
                r=["smadd"], w=["vb2", "smadd_done"])
            sc.add("vector", lambda e: nc.vector.tensor_scalar(
                selbit[:], iscacc[:], sthr[:], None, op0=ALU.is_ge),
                r=["iscacc", "sthr"], w=["selbit"])
            sc.add("vector", lambda e: nc.vector.tensor_mul(selbit[:], selbit[:], vb2[:]),
                   r=["selbit", "vb2"], w=["selbit"])
            sc.add("vector", lambda e: nc.vector.tensor_scalar(
                sladd[:], selbit[:], 1.0, 30000.0, op0=ALU.subtract, op1=ALU.mult),
                r=["selbit"], w=["sladd"])
            for h0 in range(0, NH, 4):
                pbq, pbqt = PSB(h0 // 4)
                for j in range(4):
                    h = h0 + j
                    sc.add("tensor", lambda e, h=h, j=j, pbq=pbq: nc.tensor.transpose(
                        pbq[:, j * 128:(j + 1) * 128], sq[:, h * 128:(h + 1) * 128], identb[:]),
                        r=["sq", "identb"], w=[pbqt])
                sc.add("scalar", lambda e, pbq=pbq: nc.scalar.copy(qTb[:], pbq[:]),
                       r=[pbqt], w=["qTb"] + (["sq_done"] if h0 == 12 else []))
                for j in range(4):
                    h = h0 + j
                    lhs = qTb[:, j * 128:(j + 1) * 128]
                    sc.add("tensor", lambda e, lhs=lhs, hf=hf: nc.tensor.matmul(
                        PS2[:, 0:256], lhs, crmT2[:, hf * 256:(hf + 1) * 256], start=True, stop=True),
                        r=["qTb", "crmT2"], w=["PS2"])
                    sc.add("vector", lambda e: nc.vector.tensor_add(scf[:], PS2[:, 0:256], sladd[:]),
                           r=["PS2", "sladd"], w=["scf"])
                    sc.add("vector", lambda e: nc.vector.reduce_max(st1[:], scf[:], axis=AX.X, negate=True),
                           r=["scf"], w=["st1"])
                    sc.add("vector", lambda e: nc.vector.tensor_scalar_mul(st1[:], st1[:], ISCALE),
                           r=["st1"], w=["st1"])
                    sc.add("scalar", lambda e: nc.scalar.activation(
                        esc[:], scf[:], AF.Exp, bias=st1[:], scale=ISCALE, accum_out=st2[:]),
                        r=["scf", "st1"], w=["esc", "st2"])
                    sc.add("vector", lambda e: nc.vector.tensor_add(st2[:], st2[:], sguard[:]),
                           r=["st2", "sguard"], w=["st2"])
                    sc.add("vector", lambda e: nc.vector.reciprocal(st2[:], st2[:]),
                           r=["st2"], w=["st2"])
                    sc.add("vector", lambda e: nc.vector.tensor_mul(st2[:], st2[:], shp[:]),
                           r=["st2", "shp"], w=["st2"])
                    for cchunk in range(2):
                        sc.add("tensor", lambda e, cchunk=cchunk: nc.tensor.transpose(
                            PSB1[:, cchunk * 128:(cchunk + 1) * 128],
                            esc[:, cchunk * 128:(cchunk + 1) * 128], identb[:]),
                            r=["esc", "identb"], w=["PSB1"])
                    sc.add("scalar", lambda e: nc.scalar.copy(escT[:], PSB1[:, 0:256]),
                           r=["PSB1"], w=["escT"])
                    sc.add("tensor", lambda e, hf=hf: nc.tensor.matmul(
                        PS3[:, 0:128], escT[:, 0:128], crm_v[:, 2 * hf, :], start=True, stop=False,
                        skip_group_check=True),
                        r=["escT", "crm_sb"], w=["PS3"])
                    sc.add("tensor", lambda e, hf=hf: nc.tensor.matmul(
                        PS3[:, 0:128], escT[:, 128:256], crm_v[:, 2 * hf + 1, :], start=False, stop=True,
                        skip_group_check=True),
                        r=["escT", "crm_sb"], w=["PS3"])
                    sc.add("vector", lambda e, h=h: nc.vector.tensor_scalar(
                        attnb[:, h * 128:(h + 1) * 128], PS3[:, 0:128], st2[:], None, op0=ALU.mult),
                        r=["PS3", "st2"], w=["attnb"])
                    for jjw in range(4):
                        sc.add("tensor", lambda e, lhs=lhs, hf=hf, tl=tl, jjw=jjw: nc.tensor.matmul(
                            PS0[:, jjw * 128:(jjw + 1) * 128], lhs,
                            swkT_sb[:, hf * 1536 + (tl + jjw) * 128: hf * 1536 + (tl + jjw + 1) * 128],
                            start=True, stop=True, skip_group_check=True),
                            r=["qTb", "swkT_sb"], w=["PS0"])
                    sc.add("tensor", lambda e, lhs=lhs, hf=hf, tl=tl: nc.tensor.matmul(
                        PS1[:, 0:128], lhs,
                        swkT_sb[:, hf * 1536 + (tl + 4) * 128: hf * 1536 + (tl + 5) * 128],
                        start=True, stop=True),
                        r=["qTb", "swkT_sb"], w=["PS1"])
                    sc.add("vector", lambda e: nc.vector.reduce_max(st1[:], PS0[:], axis=AX.X),
                           r=["PS0"], w=["st1"])
                    sc.add("vector", lambda e: nc.vector.reduce_max(st3[:], PS1[:, 0:128], axis=AX.X),
                           r=["PS1"], w=["st3"])
                    sc.add("vector", lambda e: nc.vector.tensor_max(st1[:], st1[:], st3[:]),
                           r=["st1", "st3"], w=["st1"])
                    sc.add("vector", lambda e: nc.vector.tensor_scalar_mul(st1[:], st1[:], ISCALE),
                           r=["st1"], w=["st1"])
                    sc.add("vector", lambda e, h=h: nc.vector.tensor_max(st1[:], st1[:], sinkb[:, h:h + 1]),
                           r=["st1", "sinkb"], w=["st1"])
                    sc.add("vector", lambda e: nc.vector.tensor_scalar_mul(st1[:], st1[:], -1.0),
                           r=["st1"], w=["st1"])
                    sc.add("scalar", lambda e: nc.scalar.activation(
                        esw[:], PS0[:], AF.Exp, bias=st1[:], scale=ISCALE),
                        r=["PS0", "st1"], w=["esw"])
                    sc.add("scalar", lambda e: nc.scalar.activation(
                        eswd[:], PS1[:, 0:128], AF.Exp, bias=st1[:], scale=ISCALE),
                        r=["PS1", "st1"], w=["eswd"])
                    sc.add("scalar", lambda e, h=h: nc.scalar.activation(
                        esk[:], sinkb[:, h:h + 1], AF.Exp, bias=st1[:], scale=1.0),
                        r=["sinkb", "st1"], w=["esk"])
                    smv = smsk[:].rearrange("p (c xx) -> p c xx", c=5)
                    for jjw in range(4):
                        sc.add("vector", lambda e, jjw=jjw: nc.vector.tensor_mul(
                            esw[:, jjw * 128:(jjw + 1) * 128], esw[:, jjw * 128:(jjw + 1) * 128],
                            smv[:, jjw, :]),
                            r=["esw", "smsk"], w=["esw"])
                    sc.add("vector", lambda e: nc.vector.tensor_mul(eswd[:], eswd[:], smv[:, 4, :]),
                           r=["eswd", "smsk"], w=["eswd"])
                    for jjw in range(4):
                        sc.add("tensor", lambda e, jjw=jjw: nc.tensor.transpose(
                            PSB1[:, jjw * 128:(jjw + 1) * 128], esw[:, jjw * 128:(jjw + 1) * 128], identb[:]),
                            r=["esw", "identb"], w=["PSB1"])
                    sc.add("scalar", lambda e: nc.scalar.copy(eswT[:, 0:512], PSB1[:]),
                           r=["PSB1"], w=["eswT"])
                    sc.add("tensor", lambda e: nc.tensor.transpose(PSB1[:, 0:128], eswd[:], identb[:]),
                           r=["eswd", "identb"], w=["PSB1"])
                    sc.add("scalar", lambda e: nc.scalar.copy(eswT[:, 512:640], PSB1[:, 0:128]),
                           r=["PSB1"], w=["eswT"])
                    for jjw in range(5):
                        sc.add("tensor", lambda e, jjw=jjw, hf=hf, tl=tl: nc.tensor.matmul(
                            PS3[:, 0:129], eswT[:, jjw * 128:(jjw + 1) * 128],
                            vextsw[:, 12 * hf + tl + jjw, 0:129],
                            start=(jjw == 0), stop=(jjw == 4), skip_group_check=True),
                            r=["eswT", "vextsw"], w=["PS3"])
                    sc.add("vector", lambda e: nc.vector.tensor_copy(st1[:], PS3[:, 128:129]),
                           r=["PS3"], w=["st1"])
                    sc.add("vector", lambda e: nc.vector.tensor_add(st1[:], st1[:], esk[:]),
                           r=["st1", "esk"], w=["st1"])
                    sc.add("vector", lambda e: nc.vector.reciprocal(st1[:], st1[:]),
                           r=["st1"], w=["st1"])
                    sc.add("vector", lambda e, h=h: nc.vector.scalar_tensor_tensor(
                        attnb[:, h * 128:(h + 1) * 128], PS3[:, 0:128], st1[:],
                        attnb[:, h * 128:(h + 1) * 128], op0=ALU.mult, op1=ALU.add),
                        r=["PS3", "st1", "attnb"], w=["attnb"])
            sc.add("vector", lambda e: nc.vector.tensor_copy(sq[:], attnb[:]),
                   r=["attnb"], w=["sq"])
            for c0 in range(0, 16, 4):
                pb, pbt = PSB(c0 // 4)
                for j in range(4):
                    dc = c0 + j
                    sc.add("tensor", lambda e, dc=dc, j=j, pb=pb: nc.tensor.transpose(
                        pb[:, j * 128:(j + 1) * 128], sq[:, dc * 128:(dc + 1) * 128], identb[:]),
                        r=["sq", "identb"], w=[pbt])
                for j in range(4):
                    dc = c0 + j
                    sc.add("scalar", lambda e, dc=dc, j=j, pb=pb: nc.scalar.copy(
                        xn1T_[:, dc, :], pb[:, j * 128:(j + 1) * 128]),
                        r=[pbt], w=["aT"])
            for g in range(4):
                pm = PS0 if g % 2 == 0 else PS1
                pmt = "PS0" if g % 2 == 0 else "PS1"
                for cc in range(4):
                    sc.add("tensor", lambda e, g=g, cc=cc, pm=pm: nc.tensor.matmul(
                        pm[:], xn1T_[:, g * 4 + cc, :], sgw[:, g * 4 + cc, :],
                        start=(cc == 0), stop=(cc == 3)),
                        r=["aT", "sgw"], w=[pmt])
                sc.add("scalar", lambda e, g=g, pm=pm: nc.scalar.copy(ogb[:, g * 512:(g + 1) * 512], pm[:]),
                       r=[pmt], w=["ogb"])
            for c0 in range(0, 16, 4):
                pb, pbt = PSB(c0 // 4)
                for j in range(4):
                    dc = c0 + j
                    sc.add("tensor", lambda e, dc=dc, j=j, pb=pb: nc.tensor.transpose(
                        pb[:, j * 128:(j + 1) * 128], ogb[:, dc * 128:(dc + 1) * 128], identb[:]),
                        r=["ogb", "identb"], w=[pbt])
                for j in range(4):
                    dc = c0 + j
                    sc.add("scalar", lambda e, dc=dc, j=j, pb=pb: nc.scalar.copy(
                        xn1T_[:, dc, :], pb[:, j * 128:(j + 1) * 128]),
                        r=[pbt], w=["aT"])
            sc.add("sync", lambda e, sl=sl: e.dma_start(out=sxr[:], in_=o_x1[sl, :]),
                   r=["o_x1", "sxr_done"], w=["sxr"], dma=True)
            for b4 in range(4):
                pm = PS0 if b4 % 2 == 0 else PS1
                pmt = "PS0" if b4 % 2 == 0 else "PS1"
                for dc in range(16):
                    sc.add("tensor", lambda e, b4=b4, dc=dc, pm=pm: nc.tensor.matmul(
                        pm[:], xn1T_[:, dc, :], sfw[:, dc, b4 * 512:(b4 + 1) * 512],
                        start=(dc == 0), stop=(dc == 15)),
                        r=["aT", "sfw"], w=[pmt])
                sc.add("vector", lambda e, b4=b4, pm=pm: nc.vector.tensor_add(
                    sxr[:, b4 * 512:(b4 + 1) * 512], sxr[:, b4 * 512:(b4 + 1) * 512], pm[:]),
                    r=[pmt, "sxr"], w=["sxr"])
            sc.add("gpsimd", lambda e, sl=sl: e.dma_start(out=o_x2[sl, :], in_=sxr[:]),
                   r=["sxr"], w=["o_x2", "sxr_done"], dma=True)

        sc.barrier()

        xnT3 = AR0_[:].rearrange("p (c t) -> p c t", c=16)
        hbF = SP2_[:, 0:512]
        gbF = SP2_[:, 512:1024]
        habF = SP2_[:, 1024:1536]
        hspF = SP2_[:, 1536:2048]
        hspT_ = SP2_[:, 2048:2560]
        hspT = hspT_.rearrange("p (c t) -> p c t", c=4)
        tscr_ = SP2_[:, 2560:3328]
        tscr = tscr_.rearrange("p (c t) -> p c t", c=6)
        obF = ARF_[:, 0:512]
        for i in range(NT):
            sc.add("sync", lambda e, i=i: e.dma_start(out=xnb[:], in_=xn1_scr[i * 128:(i + 1) * 128, :]),
                   r=["xn1_scr", "xnb_done"], w=["xnb"], dma=True)
            for c0 in range(0, 16, 4):
                pb, pbt = PSB(c0 // 4)
                for j in range(4):
                    dc = c0 + j
                    sc.add("tensor", lambda e, dc=dc, j=j, pb=pb: nc.tensor.transpose(
                        pb[:, j * 128:(j + 1) * 128], xnb[:, dc * 128:(dc + 1) * 128], identb[:]),
                        r=["xnb", "identb"],
                        w=[pbt] + (["xnb_done"] if c0 == 12 else []))
                for j in range(4):
                    dc = c0 + j
                    sc.add("scalar", lambda e, dc=dc, j=j, i=i, pb=pb: nc.scalar.copy(
                        xnT3[:, dc, i * 128:(i + 1) * 128], pb[:, j * 128:(j + 1) * 128]),
                        r=[pbt], w=[f"xnT3:{i}"])
        for os_ in range(16):
            for dc in range(16):
                sc.add("gpsimd", lambda e, os_=os_, dc=dc: e.dma_start(
                    out=sgate[:, dc, :], in_=wg_in[dc * 128:(dc + 1) * 128, os_ * 512:(os_ + 1) * 512]),
                    r=["sgate_done"], w=["sgate"], dma=True)
                sc.add("gpsimd", lambda e, os_=os_, dc=dc: e.dma_start(
                    out=sup[:, dc, :], in_=wu_in[dc * 128:(dc + 1) * 128, os_ * 512:(os_ + 1) * 512]),
                    r=["sup_done"], w=["sup"], dma=True)
            for cc in range(4):
                sc.add("gpsimd", lambda e, os_=os_, cc=cc: e.dma_start(
                    out=sdown[:, cc, :], in_=wd_in[os_ * 512 + cc * 128:os_ * 512 + (cc + 1) * 128, :]),
                    r=["sdown_done"], w=["sdown"], dma=True)
            for i in range(NT):
                for dc in range(16):
                    sc.add("tensor", lambda e, i=i, dc=dc: nc.tensor.matmul(
                        PS0[:], xnT3[:, dc, i * 128:(i + 1) * 128], sgate[:, dc, :],
                        start=(dc == 0), stop=(dc == 15)),
                        r=[f"xnT3:{i}", "sgate"],
                        w=["PS0"] + (["sgate_done"] if i == NT - 1 and dc == 15 else []))
                sc.add("scalar", lambda e: nc.scalar.activation(gbF[:], PS0[:], AF.Silu),
                       r=["PS0"], w=["gbF"])
                for dc in range(16):
                    sc.add("tensor", lambda e, i=i, dc=dc: nc.tensor.matmul(
                        PS1[:], xnT3[:, dc, i * 128:(i + 1) * 128], sup[:, dc, :],
                        start=(dc == 0), stop=(dc == 15)),
                        r=[f"xnT3:{i}", "sup"],
                        w=["PS1"] + (["sup_done"] if i == NT - 1 and dc == 15 else []))
                sc.add("vector", lambda e: nc.vector.tensor_mul(hbF[:], PS1[:], gbF[:]),
                       r=["PS1", "gbF"], w=["hbF"])
                sc.add("scalar", lambda e: nc.scalar.activation(habF[:], hbF[:], AF.Abs),
                       r=["hbF"], w=["habF"])
                hv = habF.rearrange("p (g s) -> p g s", s=4)
                sc.add("vector", lambda e: nc.vector.tensor_max(tscr[:, 0, :], hv[:, :, 0], hv[:, :, 1]),
                       r=["habF"], w=["tscr"])
                sc.add("vector", lambda e: nc.vector.tensor_max(tscr[:, 1, :], hv[:, :, 2], hv[:, :, 3]),
                       r=["habF"], w=["tscr"])
                sc.add("vector", lambda e: nc.vector.tensor_tensor(
                    tscr[:, 2, :], hv[:, :, 0], hv[:, :, 1], op=ALU.min),
                    r=["habF"], w=["tscr"])
                sc.add("vector", lambda e: nc.vector.tensor_tensor(
                    tscr[:, 3, :], hv[:, :, 2], hv[:, :, 3], op=ALU.min),
                    r=["habF"], w=["tscr"])
                sc.add("vector", lambda e: nc.vector.tensor_tensor(
                    tscr[:, 4, :], tscr[:, 0, :], tscr[:, 1, :], op=ALU.min),
                    r=["tscr"], w=["tscr"])
                sc.add("vector", lambda e: nc.vector.tensor_max(tscr[:, 5, :], tscr[:, 2, :], tscr[:, 3, :]),
                       r=["tscr"], w=["tscr"])
                sc.add("vector", lambda e: nc.vector.tensor_max(tscr[:, 4, :], tscr[:, 4, :], tscr[:, 5, :]),
                       r=["tscr"], w=["tscr"])
                hbv = hbF.rearrange("p (g s) -> p g s", s=4)
                hsv = hspF.rearrange("p (g s) -> p g s", s=4)
                for s4 in range(4):
                    sc.add("vector", lambda e, s4=s4: nc.vector.tensor_tensor(
                        tscr[:, 5, :], hv[:, :, s4], tscr[:, 4, :], op=ALU.is_ge),
                        r=["habF", "tscr"], w=["tscr"])
                    sc.add("vector", lambda e, s4=s4: nc.vector.tensor_mul(
                        hsv[:, :, s4], hbv[:, :, s4], tscr[:, 5, :]),
                        r=["hbF", "tscr"], w=["hspF"])
                pbh, pbht = PSB(i % 2)
                for j in range(4):
                    sc.add("tensor", lambda e, j=j, pbh=pbh: nc.tensor.transpose(
                        pbh[:, j * 128:(j + 1) * 128], hspF[:, j * 128:(j + 1) * 128], identb[:]),
                        r=["hspF", "identb"], w=[pbht])
                sc.add("scalar", lambda e, pbh=pbh: nc.scalar.copy(hspT_[:], pbh[:]),
                       r=[pbht], w=["hspT"])
                for b4 in range(4):
                    for cc in range(4):
                        sc.add("tensor", lambda e, b4=b4, cc=cc: nc.tensor.matmul(
                            PS2[:], hspT[:, cc, :], sdown[:, cc, b4 * 512:(b4 + 1) * 512],
                            start=(cc == 0), stop=(cc == 3)),
                            r=["hspT", "sdown"],
                            w=["PS2"] + (["sdown_done"] if i == NT - 1 and b4 == 3 and cc == 3 else []))
                    sc.add("vector", lambda e: nc.vector.tensor_copy(obF[:], PS2[:]),
                           r=["PS2"], w=["obF"])
                    sc.add("gpsimd", lambda e, i=i, b4=b4: e.dma_start(
                        out=o_x2[i * 128:(i + 1) * 128, b4 * 512:(b4 + 1) * 512], in_=obF[:],
                        accum_op=ALU.add),
                        r=["obF", "o_x2"], w=["o_x2"], dma=True)

        sc.emit(nc, es, btile)
    return nc


def _rope_l1():
    inv = 1.0 / (10000.0 ** (np.arange(0, 64, 2, dtype=np.float64) / 64))
    pos = np.arange(T, dtype=np.float64)
    fr = pos[:, None] * inv[None, :]
    return np.cos(fr).astype(np.float32), np.sin(fr).astype(np.float32)


def _l1_tables(c):
    pos = 1024 * c + np.arange(1024)
    t2 = np.concatenate([pos, pos])
    n = np.arange(256)[None, :]
    valid = (n * 32 + 31) < t2[:, None]
    maskadd = np.where(valid, 0.0, NEG).astype(np.float32)
    guard = (t2 < 32.5).astype(np.float32)[:, None]
    haspick = (t2 >= 32).astype(np.float32)[:, None]
    swmsk = np.zeros((NT, 5, 128, 128), dtype=BF)
    for i in range(NT):
        tl = i % 8
        qp = (1024 * c + 128 * tl + np.arange(128))[:, None]
        for jj in range(5):
            kp = (1024 * c - 512 + 128 * (tl + jj) + np.arange(128))[None, :]
            v = (kp >= 0) & (kp <= qp) & (qp - kp < 512)
            swmsk[i, jj] = v.astype(np.float32)
    wsel = np.zeros((128, 8), np.float32)
    if c > 0:
        wsel[:, c - 1] = 1.0
    return maskadd, guard, haspick, swmsk, wsel


def run_fused(inp, phase_limit="p1", dbg=False):
    cos, sin = _rope_l0()
    cos1, sin1 = _rope_l1()
    band = np.ascontiguousarray(_band_tiles())
    nc = build_fused(phase_limit=phase_limit, dbg=dbg)
    xs = bslices(np.asarray(inp["L0_attn_hidden_states"]))
    in_maps = []
    for c in range(NCORE):
        rows = np.arange(1024 * c, 1024 * (c + 1))
        cosc = np.ascontiguousarray(np.concatenate([cos[rows], cos[rows]], axis=0))
        sinc = np.ascontiguousarray(np.concatenate([sin[rows], sin[rows]], axis=0))
        m = {
            "x": xs[c],
            "wq": np.asarray(inp["L0_attn_Wq"]),
            "wk": np.asarray(inp["L0_attn_Wk"]),
            "wv": np.asarray(inp["L0_attn_Wv"]),
            "wo": np.asarray(inp["L0_attn_Wo"]),
            "wfc": np.asarray(inp["L0_ffn_w_fc"]),
            "wproj": np.asarray(inp["L0_ffn_w_proj"]),
            "cos": cosc,
            "sin": sinc,
            "bandbit": band,
        }
        if phase_limit == "full":
            maskadd, guard, haspick, swmsk, wsel = _l1_tables(c)
            m.update({
                "wqc": np.asarray(inp["L1_attn_w_qc"]),
                "wqup": np.asarray(inp["L1_attn_w_qup"]),
                "wkvb": np.asarray(inp["L1_attn_w_kv_b"]),
                "wzb": np.asarray(inp["L1_attn_w_z_b"]),
                "biasb": np.asarray(inp["L1_attn_bias_b"]),
                "wk1": np.asarray(inp["L1_attn_w_k"]),
                "wv1": np.asarray(inp["L1_attn_w_v"]),
                "wikv": np.asarray(inp["L1_attn_wi_kv"]),
                "wiz": np.asarray(inp["L1_attn_wi_z"]),
                "biasi": np.asarray(inp["L1_attn_bias_i"]),
                "wdq": np.asarray(inp["L1_attn_w_dq"]),
                "wiuq": np.asarray(inp["L1_attn_w_iuq"]),
                "ww": np.asarray(inp["L1_attn_w_w"]),
                "qnw": np.asarray(inp["L1_attn_q_norm_w"]),
                "knw": np.asarray(inp["L1_attn_k_norm_w"]),
                "cos1": np.ascontiguousarray(np.concatenate([cos1[rows], cos1[rows]], axis=0)[:, 0:32]),
                "sin1": np.ascontiguousarray(np.concatenate([sin1[rows], sin1[rows]], axis=0)[:, 0:32]),
                "sink": np.asarray(inp["L1_attn_sink_logit"], np.float32).reshape(1, NH),
                "groupw": np.asarray(inp["L1_attn_group_w"]),
                "finalw": np.asarray(inp["L1_attn_final_w"]),
                "wgate": np.asarray(inp["L1_ffn_w_gate"]),
                "wup": np.asarray(inp["L1_ffn_w_up"]),
                "wdown": np.asarray(inp["L1_ffn_w_down"]),
                "maskadd": maskadd,
                "guard": guard,
                "haspick": haspick,
                "swmsk": swmsk,
                "wsel": wsel,
            })
        in_maps.append(m)
    res = run_bass_kernel_spmd(nc, in_maps, core_ids=list(range(NCORE)))
    return res


def kernel(**inp):
    res = run_fused(inp, phase_limit="full", dbg=False)
    x2 = bgather([res.results[c]["x2"] for c in range(NCORE)])
    return np.ascontiguousarray(x2.astype(np.float32))
